# version 07
# baseline (speedup 1.0000x reference)
import os
os.environ.setdefault("TRN_ALLOW_ICI_ALL_TO_ALL", "1")
import numpy as np
import ml_dtypes
import concourse.bass as bass
import concourse.mybir as mybir
from concourse.bass_utils import run_bass_kernel_spmd
from contextlib import ExitStack

F32 = mybir.dt.float32
BF16 = mybir.dt.bfloat16
U8 = mybir.dt.uint8
I32 = mybir.dt.int32
AF = mybir.ActivationFunctionType
ALU = mybir.AluOpType

C = 1024
T = 8192
TO = 4096
HC = 512
NCH = 8
HCH = 4
L = 64
SG = 4096
NSEG = 2
TT = 512
NTT = T // TT
NTO = TO // TT
NEG = -1e38
EPS = 1e-8
LN_EPS = 1e-5
AX = None

TAPS = []


class Prog:
    def __init__(self, nc):
        self.nc = nc
        self.ops = {e: [] for e in ("sync", "gpsimd", "vector", "scalar", "tensor")}
        self.cnt = {}
        self.hw = {e: {} for e in self.ops}
        self.sems = {}

    def sem(self, stack, name):
        h = stack.enter_context(self.nc.semaphore(name))
        self.sems[name] = h
        self.cnt[name] = 0
        return h

    def bump(self, sem_name, k):
        self.cnt[sem_name] += k
        return (sem_name, self.cnt[sem_name])

    def tok(self, sem_name):
        return (sem_name, self.cnt[sem_name])

    def wait(self, eng, tokv):
        if tokv is None:
            return
        sem_name, val = tokv
        if val <= 0:
            return
        if self.hw[eng].get(sem_name, -1) >= val:
            return
        self.hw[eng][sem_name] = val
        h = self.sems[sem_name]
        self.ops[eng].append(lambda e, h=h, v=val: e.wait_ge(h, v))

    def op(self, eng, fn, inc=None):
        if inc is None:
            self.ops[eng].append(fn)
            return None
        sem_name, k = inc
        h = self.sems[sem_name]
        self.ops[eng].append(lambda e, fn=fn, h=h, k=k: fn(e).then_inc(h, k))
        return self.bump(sem_name, k)

    def retro_inc(self, eng, sem_name, k=1):
        old = self.ops[eng].pop()
        h = self.sems[sem_name]
        self.ops[eng].append(lambda e, f=old, h=h, k=k: f(e).then_inc(h, k))
        return self.bump(sem_name, k)

    def barrier(self):
        toks = [(s, self.cnt[s]) for s in self.cnt]
        for eng in self.ops:
            for t in toks:
                self.wait(eng, t)

    def run(self, block):
        for eng in ("sync", "gpsimd", "vector", "scalar", "tensor"):
            ops = self.ops[eng]
            dec = getattr(block, eng)

            def make(ops):
                def body(e):
                    for fn in ops:
                        fn(e)
                return body
            dec(make(ops))


def build(taps=()):
    global AX
    AX = mybir.AxisListType.X
    nc = bass.Bass("TRN2", target_bir_lowering=False)

    def din(name, shape, dtype=F32):
        return nc.dram_tensor(name, shape, dtype, kind="ExternalInput").ap()

    def dout(name, shape, dtype=F32):
        return nc.dram_tensor(name, shape, dtype, kind="ExternalOutput").ap()

    def dtmp(name, shape, dtype=F32):
        return nc.dram_tensor(name, shape, dtype).ap()

    X0F = din("x0f", [C, T])
    X0O = din("x0own", [C, TO])
    WTMC = din("wtmc", [C, 3 * HC])
    VECW = din("vecw", [HC, 5])
    VECP = din("vecp", [C, 2])
    VECN = din("vecn", [C, 8])
    SCL = din("scl", [1, 4])
    WCM = din("wcm", [C, 4 * C])
    WCP = din("wcp", [4 * C, C])
    L0WT = din("l0wt", [C, C])
    WQT = din("wqt", [C, C])
    WKT = din("wkt", [C, C])
    WVT = din("wvt", [C, C])
    K0A = din("k0a", [C, 4 * C], U8)
    K0B = din("k0b", [C, 4 * C], U8)
    K1U = din("k1u", [4 * C, 4 * C], U8)
    K2U = din("k2u", [4 * C, C], U8)
    IOTAG = din("iotag", [128, L])
    IDN = din("idn", [128, 128])
    IDNB = din("idnb", [128, 128], BF16)
    MSKE = din("mske", [128, 128])
    MSKO = din("msko", [128, 128])
    SELI = din("sel", [128, 1])

    OUT = dout("out", [TO, C])

    ZOWN = dtmp("zown", [HC, T + 1])
    Z2H = dtmp("z2h", [C, T], BF16)
    Z2L = dtmp("z2l", [C, T], BF16)
    X1IN = dtmp("x1in", [HC, T])
    X1OUT = dtmp("x1out", [C, T])
    RRES = dtmp("rres", [C, TO])
    X4S = dtmp("x4s", [C, TO])
    ARI = dtmp("ari", [1, 8])
    ARO = dtmp("aro", [1, 8])
    WCMHB = dtmp("wcmhb", [32, 128, NCH * 128], BF16)
    WCMLB = dtmp("wcmlb", [32, 128, NCH * 128], BF16)
    WCPHB = dtmp("wcphb", [NCH, 128, 32 * 128], BF16)
    WCPLB = dtmp("wcplb", [NCH, 128, 32 * 128], BF16)
    QWTB = dtmp("qwtb", [NCH, 128, NCH * 128], BF16)
    PRJ = {}
    for nm in ("wq", "wk", "wv"):
        PRJ[nm + "h"] = dtmp(nm + "hb", [NCH, 128, NCH * 128], BF16)
        PRJ[nm + "l"] = dtmp(nm + "lb", [NCH, 128, NCH * 128], BF16)
    D0B = dtmp("d0b", [32, 128, NCH * 128], BF16)
    W1BB = dtmp("w1bb", [32, 128, 32 * 128], BF16)
    W2BB = dtmp("w2bb", [NCH, 128, 32 * 128], BF16)
    XCH = dtmp("xch", [2 * C, TO], BF16)
    XCG = dtmp("xcg", [4 * C, TO], BF16)
    QSH = dtmp("qsh", [C, TO], BF16)
    QSL = dtmp("qsl", [C, TO], BF16)
    KSH = dtmp("ksh", [C, T], BF16)
    KSL = dtmp("ksl", [C, T], BF16)
    VTH = dtmp("vth", [T, C], BF16)
    VTL = dtmp("vtl", [T, C], BF16)

    DBG = {}

    st = ExitStack()
    E = st.enter_context

    SLOT = 4096
    NSLOT = 10
    WK = E(nc.sbuf_tensor("wk", [128, NSLOT * SLOT], F32))
    ONES0 = E(nc.sbuf_tensor("ones0", [128, SG], BF16))
    VSB = E(nc.sbuf_tensor("vsb", [128, 192], F32))
    IOT = E(nc.sbuf_tensor("iot", [128, L], F32))
    IDNS = E(nc.sbuf_tensor("idns", [128, 128], F32))
    IDNBS = E(nc.sbuf_tensor("idnbs", [128, 128], BF16))
    MSKES = E(nc.sbuf_tensor("mskes", [128, 128], F32))
    MSKOS = E(nc.sbuf_tensor("mskos", [128, 128], F32))
    SELS = E(nc.sbuf_tensor("sels", [128, 1], F32))
    ONEB = E(nc.sbuf_tensor("oneb", [128, 1], BF16))
    ONER = E(nc.sbuf_tensor("oner", [1, 128], F32))
    SCLS = E(nc.sbuf_tensor("scls", [1, 4], F32))
    SM = E(nc.sbuf_tensor("smx", [128, 1024], F32))
    SR = E(nc.sbuf_tensor("srx", [1, 1024], F32))
    WTB = E(nc.sbuf_tensor("wtb", [128, 6 * 1024], BF16))
    WTS = None
    STR = E(nc.sbuf_tensor("strm", [128, 4096], F32))
    WTS = STR
    PS = [E(nc.psum_tensor("ps%d" % i, [128, 512], F32)) for i in range(4)]
    PB0 = E(nc.psum_tensor("pb0", [128, 512], F32))
    PB1 = E(nc.psum_tensor("pb1", [128, 512], F32))
    PROW = E(nc.psum_tensor("prow", [1, 512], F32))
    PT = E(nc.psum_tensor("ptx", [128, 128], F32))
    PTB = PT[:].bitcast(BF16)[:, 0:128]
    PTBS = [PTB,
            PB0[:].bitcast(BF16)[:, 0:128],
            PB1[:].bitcast(BF16)[:, 0:128]]

    P = Prog(nc)
    for s_ in ("dS", "dG", "dA", "tS", "vS", "sS", "cS"):
        P.sem(st, s_)

    def slot(i, lo=0, hi=SLOT):
        return WK[:, i * SLOT + lo: i * SLOT + hi]

    def slotb(i, lo, hi):
        return WK[:, i * SLOT: (i + 1) * SLOT].bitcast(BF16)[:, lo:hi]

    def sloti(i, lo, hi):
        return WK[:, i * SLOT + lo: i * SLOT + hi].bitcast(I32)

    def strb(lo, hi):
        return STR.bitcast(BF16)[:, lo:hi]

    def stru(lo, hi):
        return STR.bitcast(U8)[:, lo:hi]

    def dma(q, out, in_, pre=None, ncok=False):
        sem = {"sync": "dS", "gpsimd": "dG", "scalar": "dA"}[q]
        if pre is not None:
            for t_ in (pre if isinstance(pre, list) else [pre]):
                P.wait(q, t_)
        if ncok:
            return P.op(q, lambda e, o=out, i=in_: e.dma_start(
                out=o, in_=i, allow_slow_non_contiguous=True), inc=(sem, 16))
        return P.op(q, lambda e, o=out, i=in_: e.dma_start(out=o, in_=i), inc=(sem, 16))

    def snap(name, src, shape, dtype=F32):
        if name not in taps:
            return
        dd = dout("dbg_" + name, shape, dtype)
        P.op("vector", lambda e: e.engine_nop(), inc=("vS", 1))
        P.op("scalar", lambda e: e.activation(SM[:, 815:816], SM[:, 815:816], AF.Square),
             inc=("sS", 1))
        P.barrier()
        dma("gpsimd", dd, src)
        P.barrier()

    bank_ev = {}
    LG = {}

    def bank_wait(b):
        P.wait("tensor", bank_ev.get(b))

    def bank_done(b, tok):
        bank_ev[b] = tok

    def VF(fn):
        t_ = P.op("vector", fn, inc=("vS", 1))
        P.wait("vector", t_)
        return t_

    VS = {}
    _vn = [0]

    def vreg(name, n):
        VS[name] = (_vn[0], n)
        _vn[0] += n

    for nm in ("w", "u", "tmk", "tmv", "tmr", "om_tmk", "om_tmv", "om_tmr", "ew", "w64"):
        vreg(nm, HCH)
    for nm in ("g1", "b1"):
        vreg(nm, NCH)
    for nm in ("g2", "b2", "cmk", "cmr", "om_cmk", "om_cmr", "l0b", "bq", "bk", "bv"):
        vreg(nm, NCH)
    for nm in ("one", "qmaxc", "kmaxc", "mbc", "wsc", "iscale", "wsis", "csb",
               "negmb", "invb", "wsisb", "csbb", "thb", "thnb", "epsc"):
        vreg(nm, 1)
    assert _vn[0] <= 192

    def vg(name, ch=0):
        b, n = VS[name]
        assert ch < n, name
        return VSB[:, b + ch: b + ch + 1]

    def vc(name):
        b, n = VS[name]
        return VSB[0:1, b: b + 1]

    for ap_, src in ((IOT[:], IOTAG), (IDNS[:], IDN), (IDNBS[:], IDNB),
                     (MSKES[:], MSKE), (MSKOS[:], MSKO), (SELS[:], SELI), (SCLS[:], SCL)):
        dma("sync", ap_, src)
    for ci in range(HCH):
        for vi, nm in enumerate(("w", "u", "tmk", "tmv", "tmr")):
            dma("sync", vg(nm, ci), VECW[ci * 128:(ci + 1) * 128, vi:vi + 1], ncok=True)
    for ci in range(NCH):
        for vi, nm in enumerate(("g1", "b1")):
            dma("sync", vg(nm, ci), VECP[ci * 128:(ci + 1) * 128, vi:vi + 1], ncok=True)
        for vi, nm in enumerate(("g2", "b2", "cmk", "cmr", "l0b", "bq", "bk", "bv")):
            dma("sync", vg(nm, ci), VECN[ci * 128:(ci + 1) * 128, vi:vi + 1], ncok=True)
    t_all = P.tok("dS")
    P.wait("vector", t_all)
    P.wait("scalar", t_all)
    P.op("vector", lambda e: e.memset(vg("one"), 1.0))
    P.op("vector", lambda e: e.memset(vg("epsc"), LN_EPS))
    P.op("vector", lambda e: e.memset(SM[:, 815:816], 0.0))
    P.op("vector", lambda e: e.memset(ONEB[:], 1.0))
    P.op("vector", lambda e: e.memset(ONER[:], 1.0))
    P.op("vector", lambda e: e.memset(ONES0[:], 1.0))
    P.op("vector", lambda e: e.memset(ONES0[:, 0:SG:L], 0.0))
    for nm in ("tmk", "tmv", "tmr"):
        for ci in range(HCH):
            P.op("vector", lambda e, o=vg("om_" + nm, ci), i=vg(nm, ci):
                 e.tensor_scalar(o, i, -1.0, 1.0, ALU.mult, ALU.add))
    for nm in ("cmk", "cmr"):
        for ci in range(NCH):
            P.op("vector", lambda e, o=vg("om_" + nm, ci), i=vg(nm, ci):
                 e.tensor_scalar(o, i, -1.0, 1.0, ALU.mult, ALU.add))
    for ci in range(HCH):
        P.op("scalar", lambda e, o=vg("ew", ci), i=vg("w", ci): e.activation(o, i, AF.Exp),
             inc=("sS", 1))
        P.op("vector", lambda e, o=vg("w64", ci), i=vg("w", ci):
             e.tensor_scalar(o, i, float(L), None, ALU.mult))
    VF(lambda e: e.tensor_tensor(vc("csb"), SCLS[0:1, 0:1], SCLS[0:1, 1:2], ALU.mult))
    VF(lambda e: e.tensor_tensor(vc("csb"), vc("csb"), SCLS[0:1, 2:3], ALU.mult))
    tv0 = VF(lambda e: e.tensor_scalar(vc("csb"), vc("csb"), 8.0, None, ALU.mult))

    def bcast_cell(src_cell, dst_col, pre_tok):
        P.wait("tensor", pre_tok)
        bank_wait("PT")
        tt_ = P.op("tensor", lambda e, s=src_cell: e.matmul(PT[:, 0:1], ONER[:], s,
                                                            start=True, stop=True),
                   inc=("tS", 1))
        P.wait("vector", tt_)
        tok = P.op("vector", lambda e, d=dst_col: e.tensor_copy(d, PT[:, 0:1]), inc=("vS", 1))
        bank_done("PT", tok)
        return tok

    t_csbb = bcast_cell(vc("csb"), vg("csbb"), tv0)
    P.barrier()

    def stats_pass(src_fn, n_sl, sq_fn, ready_tok):
        for ci in range(n_sl):
            P.wait("tensor", ready_tok)
            if ci == 0:
                bank_wait("PROW")
            P.op("tensor", lambda e, r=src_fn(ci), st_=(ci == 0), sp=(ci == n_sl - 1):
                 e.matmul(PROW[0:1, :], vg("one"), r, start=st_, stop=sp),
                 inc=("tS", 1) if ci == n_sl - 1 else None)
        tsum = P.tok("tS")
        P.wait("vector", tsum)
        tv = P.op("vector", lambda e: e.tensor_scalar(SR[0:1, 0:512], PROW[0:1, :],
                                                      1.0 / C, None, ALU.mult), inc=("vS", 1))
        fin = None
        for ci in range(n_sl):
            P.wait("scalar", ready_tok)
            tsq = P.op("scalar", lambda e, o=sq_fn(ci), i=src_fn(ci):
                       e.activation(o, i, AF.Square), inc=("sS", 1))
            P.wait("tensor", tsq)
            if ci == 0:
                P.wait("tensor", tv)
            fin = P.op("tensor", lambda e, r=sq_fn(ci), st_=(ci == 0), sp=(ci == n_sl - 1):
                       e.matmul(PROW[0:1, :], vg("one"), r, start=st_, stop=sp),
                       inc=("tS", 1) if ci == n_sl - 1 else None)
        return fin

    def stats_finalize(tok):
        P.wait("vector", tok)
        P.op("vector", lambda e: e.tensor_scalar(SM[0:1, 0:512], PROW[0:1, :],
                                                 1.0 / C, None, ALU.mult))
        P.op("vector", lambda e: e.tensor_tensor(SR[0:1, 512:1024], SR[0:1, 0:512],
                                                 SR[0:1, 0:512], ALU.mult))
        tv = P.op("vector", lambda e: e.tensor_tensor(SM[0:1, 0:512], SM[0:1, 0:512],
                                                      SR[0:1, 512:1024], ALU.subtract),
                  inc=("vS", 1))
        tv = P.op("vector", lambda e: e.tensor_scalar(SM[0:1, 0:512], SM[0:1, 0:512],
                                                      LN_EPS, None, ALU.add), inc=("vS", 1))
        bank_done("PROW", tv)
        P.wait("scalar", tv)
        ts_ = P.op("scalar", lambda e: e.activation(SM[0:1, 0:512], SM[0:1, 0:512], AF.Sqrt),
                   inc=("sS", 1))
        P.wait("vector", ts_)
        return P.op("vector", lambda e: e.reciprocal(SR[0:1, 512:1024], SM[0:1, 0:512]),
                    inc=("vS", 1))

    def stats_bcast(tok):
        P.wait("tensor", tok)
        bank_wait("PB0")
        bank_wait("PB1")
        bank_wait(("PTB", 1))
        bank_wait(("PTB", 2))
        P.op("tensor", lambda e: e.matmul(PB0[:], ONER[:], SR[0:1, 0:512], start=True, stop=True))
        return P.op("tensor", lambda e: e.matmul(PB1[:], ONER[:], SR[0:1, 512:1024],
                                                 start=True, stop=True), inc=("tS", 1))

    for tt in range(NTT):
        c0 = tt * TT
        P.wait("vector", P.tok("dG"))
        tl = None
        for ci in range(NCH):
            tl = dma("sync", slot(0, ci * TT, (ci + 1) * TT),
                     X0F[ci * 128:(ci + 1) * 128, c0:c0 + TT], pre=P.tok("vS"))
        tok = stats_pass(lambda ci: slot(0, ci * TT, (ci + 1) * TT), NCH,
                         lambda ci: slot(2, ci * TT, (ci + 1) * TT), tl)
        tok = stats_finalize(tok)
        tbc = stats_bcast(tok)
        P.wait("vector", tbc)
        for ci in range(NCH):
            P.op("vector", lambda e, ci=ci: e.tensor_tensor(
                slot(1, ci * TT, (ci + 1) * TT), slot(0, ci * TT, (ci + 1) * TT),
                PB0[:], ALU.subtract))
            P.op("vector", lambda e, ci=ci: e.tensor_tensor(
                slot(1, ci * TT, (ci + 1) * TT), slot(1, ci * TT, (ci + 1) * TT),
                PB1[:], ALU.mult))
        tz = P.op("vector", lambda e: e.engine_nop(), inc=("vS", 1))
        bank_done("PB0", tz)
        bank_done("PB1", tz)
        for ci in range(HCH):
            dma("gpsimd", ZOWN[ci * 128:(ci + 1) * 128, c0 + 1:c0 + TT + 1],
                slot(1, ci * TT, (ci + 1) * TT), pre=tz)
            if tt == 0:
                dma("gpsimd", ZOWN[ci * 128:(ci + 1) * 128, 0:1],
                    slot(1, ci * TT, ci * TT + 1), pre=tz, ncok=True)
        tok = stats_pass(lambda ci: slot(1, ci * TT, (ci + 1) * TT), NCH,
                         lambda ci: slot(2, ci * TT, (ci + 1) * TT), tz)
        tok = stats_finalize(tok)
        tbc = stats_bcast(tok)
        P.wait("vector", tbc)
        tvlast = None
        for ci in range(NCH):
            sz = slot(1, ci * TT, (ci + 1) * TT)
            s2_ = slot(2, ci * TT, (ci + 1) * TT)
            P.op("vector", lambda e, a=s2_, b=sz: e.tensor_tensor(a, b, PB0[:], ALU.subtract))
            P.op("vector", lambda e, a=s2_: e.tensor_tensor(a, a, PB1[:], ALU.mult))
            P.op("vector", lambda e, a=s2_, g=vg("g1", ci), b=vg("b1", ci):
                 e.tensor_scalar(a, a, g, b, ALU.mult, ALU.add))
            bh = slotb(3, ci * TT, (ci + 1) * TT)
            bl = slotb(3, SLOT + ci * TT, SLOT + (ci + 1) * TT)
            P.op("vector", lambda e, o=bh, i=s2_: e.tensor_copy(o, i))
            P.op("vector", lambda e, o=s2_, i=bh: e.tensor_tensor(o, o, i, ALU.subtract))
            tvlast = P.op("vector", lambda e, o=bl, i=s2_: e.tensor_copy(o, i), inc=("vS", 1))
            dma("gpsimd", Z2H[ci * 128:(ci + 1) * 128, c0:c0 + TT], bh, pre=tvlast)
            dma("gpsimd", Z2L[ci * 128:(ci + 1) * 128, c0:c0 + TT], bl, pre=tvlast)
        bank_done("PB0", tvlast)
        bank_done("PB1", tvlast)
    P.barrier()
    snap("z2h", Z2H, [C, T], BF16)
    snap("zown", ZOWN, [HC, T + 1])

    def wkv_chain(ci, s):
        dbg0 = (ci == 0 and s == 0)

        def csnap(name, src, shape, dtype=F32):
            if dbg0:
                snap(name, src, shape, dtype)

        c0_ = ci * 4
        CA = SM[:, c0_ + 0:c0_ + 1]
        CB = SM[:, c0_ + 1:c0_ + 2]
        CAo = SM[:, c0_ + 2:c0_ + 3]
        CBo = SM[:, c0_ + 3:c0_ + 4]
        smb = 16
        CM = SM[:, smb:smb + L]
        D0C = SM[:, smb + L:smb + 2 * L]
        D1C = SM[:, smb + 2 * L:smb + 3 * L]
        DH = SM[:, smb + 3 * L:smb + 4 * L]
        DC = SM[:, smb + 4 * L:smb + 5 * L]
        MC = SM[:, smb + 5 * L:smb + 6 * L]
        PG = SM[:, smb + 6 * L:smb + 7 * L]
        RE = SM[:, smb + 7 * L:smb + 8 * L]
        RC = SM[:, smb + 8 * L:smb + 9 * L]
        AC = SM[:, smb + 9 * L:smb + 10 * L]
        DT = SM[:, smb + 10 * L:smb + 11 * L]
        T1 = SM[:, smb + 11 * L:smb + 12 * L]
        U5 = SM[:, smb + 12 * L:smb + 12 * L + 1]
        k = slot(0, 0, SG); v = slot(1, 0, SG); r = slot(2, 0, SG)
        zx = slot(3, 0, SG)
        w_, u_, w64 = vg("w", ci), vg("u", ci), vg("w64", ci)

        def V(fn, inc=None):
            return P.op("vector", fn, inc=inc)

        def Sc(fn, inc=None):
            return P.op("scalar", fn, inc=inc)

        def vsync():
            tv_ = P.retro_inc("vector", "vS")
            P.wait("scalar", tv_)
            return tv_

        def ssync():
            ts_ = P.retro_inc("scalar", "sS")
            P.wait("vector", ts_)
            return ts_

        def SCN(fn):
            t_ = P.op("vector", fn, inc=("vS", 1))
            P.wait("vector", t_)
            return t_

        if s == 0:
            VF(lambda e: e.memset(CA, 0.0))
            VF(lambda e: e.memset(CB, NEG))
        else:
            VF(lambda e: e.tensor_copy(CA, CAo))
            VF(lambda e: e.tensor_copy(CB, CBo))
        csnap("carry0", SM[:, 0:16], [128, 16])
        vsync()
        Sc(lambda e: e.activation(slot(4, 0, SG), k, AF.Exp))
        ssync()
        V(lambda e: e.tensor_copy(slot(5, 0, SG), vg("ew", ci).broadcast_to((128, SG))))
        V(lambda e: e.memset(slot(5, 0, SG)[:, 0:SG:L], 0.0))
        SCN(lambda e: e.tensor_tensor_scan(slot(6, 0, SG), slot(5, 0, SG), slot(4, 0, SG),
                                         0.0, ALU.mult, ALU.add))
        Dloc = slot(6, 0, SG)
        V(lambda e: e.tensor_copy(DT, Dloc[:, L - 1:SG:L]))
        csnap("dloc0", Dloc, [128, SG])
        V(lambda e: e.tensor_scalar(T1, IOT[:], w64, None, ALU.mult))
        V(lambda e: e.tensor_scalar(T1, T1, 0.0, None, ALU.max))
        V(lambda e: e.tensor_scalar(CM, IOT[:], w64, w64, ALU.mult, ALU.add))
        V(lambda e: e.tensor_scalar(CM, CM, CB, None, ALU.add))
        V(lambda e: e.tensor_tensor(CM, CM, T1, ALU.max))
        V(lambda e: e.tensor_tensor(D0C[:, 1:L], CM[:, 0:L - 1], CM[:, 1:L], ALU.subtract))
        V(lambda e: e.tensor_scalar(D0C[:, 1:L], D0C[:, 1:L], w64, None, ALU.add))
        VF(lambda e: e.tensor_scalar(U5, CB, w64, None, ALU.add))
        VF(lambda e: e.tensor_tensor(U5, U5, CM[:, 0:1], ALU.subtract))
        V(lambda e: e.tensor_scalar(D1C, CM, -1.0, None, ALU.mult))
        vsync()
        Sc(lambda e: e.activation(D0C[:, 1:L], D0C[:, 1:L], AF.Exp))
        Sc(lambda e: e.activation(D1C, D1C, AF.Exp))
        Sc(lambda e: e.activation(U5, U5, AF.Exp))
        ssync()
        VF(lambda e: e.memset(D0C[:, 0:1], 0.0))
        V(lambda e: e.tensor_tensor(D1C, D1C, DT, ALU.mult))
        VF(lambda e: e.tensor_tensor(D1C[:, 0:1], D1C[:, 0:1], U5, ALU.add))
        SCN(lambda e: e.tensor_tensor_scan(DH, D0C, D1C, 0.0, ALU.mult, ALU.add))
        V(lambda e: e.tensor_copy(DC[:, 1:L], DH[:, 0:L - 1]))
        VF(lambda e: e.memset(DC[:, 0:1], 1.0))
        V(lambda e: e.tensor_copy(MC[:, 1:L], CM[:, 0:L - 1]))
        VF(lambda e: e.tensor_copy(MC[:, 0:1], CB))
        csnap("sm0", SM[:, 0:1024], [128, 1024])
        V(lambda e: e.tensor_copy(slot(5, 0, SG), w_.broadcast_to((128, SG))))
        V(lambda e: e.tensor_copy(slot(5, 0, SG)[:, 0:SG:L], MC))
        SCN(lambda e: e.tensor_tensor_scan(slot(7, 0, SG), ONES0[:], slot(5, 0, SG),
                                         0.0, ALU.mult, ALU.add))
        V(lambda e: e.memset(slot(5, 0, SG), 0.0))
        V(lambda e: e.tensor_copy(slot(5, 0, SG)[:, 0:SG:L], DC))
        csnap("dstage0", slot(5, 0, SG), [128, SG])
        csnap("dcsm0", SM[:, 0:1024], [128, 1024])
        SCN(lambda e: e.tensor_tensor_scan(slot(4, 0, SG), ONES0[:], slot(5, 0, SG),
                                         0.0, ALU.mult, ALU.add))
        V(lambda e: e.tensor_scalar(slot(5, 0, SG), k, u_, None, ALU.add))
        V(lambda e: e.tensor_tensor(slot(5, 0, SG), slot(7, 0, SG), slot(5, 0, SG), ALU.max))
        V(lambda e: e.tensor_scalar(slot(5, 0, SG), slot(5, 0, SG), 0.0, None, ALU.max))
        csnap("dcrep0", slot(4, 0, SG), [128, SG])
        csnap("arep0", slot(7, 0, SG), [128, SG])
        csnap("qq0", slot(5, 0, SG), [128, SG])
        V(lambda e: e.tensor_copy(slot(8, 0, SG)[:, 1:SG], Dloc[:, 0:SG - 1]))
        V(lambda e: e.memset(slot(8, 0, SG)[:, 0:SG:L], 0.0))
        V(lambda e: e.tensor_tensor(slot(7, 0, SG), slot(7, 0, SG), slot(5, 0, SG), ALU.subtract))
        V(lambda e: e.tensor_scalar(slot(9, 0, SG), slot(5, 0, SG), -1.0, None, ALU.mult))
        vsync()
        Sc(lambda e: e.activation(slot(7, 0, SG), slot(7, 0, SG), AF.Exp))
        Sc(lambda e: e.activation(slot(9, 0, SG), slot(9, 0, SG), AF.Exp))
        ssync()
        csnap("expaq0", slot(7, 0, SG), [128, SG])
        V(lambda e: e.tensor_tensor(slot(7, 0, SG), slot(7, 0, SG), slot(4, 0, SG), ALU.mult))
        V(lambda e: e.tensor_tensor(slot(8, 0, SG), slot(8, 0, SG), slot(9, 0, SG), ALU.mult))
        V(lambda e: e.tensor_tensor(slot(7, 0, SG), slot(7, 0, SG), slot(8, 0, SG), ALU.add))
        V(lambda e: e.tensor_scalar(slot(7, 0, SG), slot(7, 0, SG), 1.1754944e-38, None,
                                    ALU.max))
        V(lambda e: e.tensor_scalar(sloti(9, 0, SG), slot(7, 0, SG).bitcast(I32),
                                    23, None, ALU.logical_shift_right))
        V(lambda e: e.tensor_scalar(sloti(9, 0, SG), sloti(9, 0, SG), 127, None, ALU.subtract))
        V(lambda e: e.tensor_copy(slot(8, 0, SG), sloti(9, 0, SG)))
        V(lambda e: e.tensor_scalar(sloti(7, 0, SG), slot(7, 0, SG).bitcast(I32),
                                    0x007FFFFF, 0x3F800000, ALU.bitwise_and, ALU.bitwise_or))
        vsync()
        Sc(lambda e: e.activation(slot(7, 0, SG), slot(7, 0, SG), AF.Ln))
        ssync()
        V(lambda e: e.tensor_scalar(slot(8, 0, SG), slot(8, 0, SG), 0.6931471805599453, None,
                                    ALU.mult))
        V(lambda e: e.tensor_tensor(slot(7, 0, SG), slot(7, 0, SG), slot(8, 0, SG), ALU.add))
        V(lambda e: e.tensor_tensor(slot(6, 0, SG), slot(5, 0, SG), slot(7, 0, SG), ALU.add))
        bbm1 = slot(6, 0, SG)
        csnap("bbm10", bbm1, [128, SG])

        def coeffs():
            V(lambda e: e.tensor_scalar(slot(4, 0, SG), bbm1, w_, None, ALU.add))
            V(lambda e: e.tensor_tensor(slot(5, 0, SG), slot(4, 0, SG), k, ALU.max))
            V(lambda e: e.tensor_tensor(slot(4, 0, SG), slot(4, 0, SG), slot(5, 0, SG),
                                        ALU.subtract))
            V(lambda e: e.tensor_tensor(slot(5, 0, SG), k, slot(5, 0, SG), ALU.subtract))
            vsync()
            Sc(lambda e: e.activation(slot(7, 0, SG), slot(4, 0, SG), AF.Exp))
            Sc(lambda e: e.activation(slot(8, 0, SG), slot(5, 0, SG), AF.Exp))
            ssync()

        coeffs()
        V(lambda e: e.tensor_tensor(slot(9, 0, SG), slot(7, 0, SG), slot(8, 0, SG), ALU.add))
        vsync()
        Sc(lambda e: e.activation(slot(5, 0, SG), slot(9, 0, SG), AF.Ln))
        ssync()
        V(lambda e: e.reciprocal(slot(9, 0, SG), slot(9, 0, SG)))
        V(lambda e: e.tensor_tensor(slot(4, 0, SG), slot(4, 0, SG), slot(5, 0, SG), ALU.subtract))
        SCN(lambda e: e.tensor_tensor_scan(slot(5, 0, SG), ONES0[:], slot(4, 0, SG),
                                         0.0, ALU.mult, ALU.add))
        V(lambda e: e.tensor_copy(T1, slot(5, 0, SG)[:, L - 1:SG:L]))
        vsync()
        Sc(lambda e: e.activation(PG, T1, AF.Exp))
        Sc(lambda e: e.activation(slot(5, 0, SG), slot(5, 0, SG), AF.Exp))
        ssync()
        V(lambda e: e.tensor_tensor(slot(4, 0, SG), slot(7, 0, SG), slot(9, 0, SG), ALU.mult))
        V(lambda e: e.memset(slot(4, 0, SG)[:, 0:SG:L], 0.0))
        V(lambda e: e.tensor_scalar(slot(9, 0, SG), slot(9, 0, SG), EPS, None, ALU.mult))
        SCN(lambda e: e.tensor_tensor_scan(slot(7, 0, SG), slot(4, 0, SG), slot(9, 0, SG),
                                         0.0, ALU.mult, ALU.add))
        V(lambda e: e.tensor_copy(RE, slot(7, 0, SG)[:, L - 1:SG:L]))
        SCN(lambda e: e.tensor_tensor_scan(RC, PG, RE, 0.0, ALU.mult, ALU.add))
        V(lambda e: e.tensor_copy(T1[:, 1:L], RC[:, 0:L - 1]))
        VF(lambda e: e.memset(T1[:, 0:1], 0.0))
        V(lambda e: e.memset(slot(9, 0, SG), 0.0))
        V(lambda e: e.tensor_copy(slot(9, 0, SG)[:, 0:SG:L], T1))
        SCN(lambda e: e.tensor_tensor_scan(slot(4, 0, SG), ONES0[:], slot(9, 0, SG),
                                         0.0, ALU.mult, ALU.add))
        V(lambda e: e.tensor_tensor(slot(4, 0, SG), slot(4, 0, SG), slot(5, 0, SG), ALU.mult))
        V(lambda e: e.tensor_tensor(slot(4, 0, SG), slot(4, 0, SG), slot(7, 0, SG), ALU.add))
        V(lambda e: e.tensor_copy(slot(5, 0, SG)[:, 1:SG], slot(4, 0, SG)[:, 0:SG - 1]))
        V(lambda e: e.memset(slot(5, 0, SG)[:, 0:1], 0.0))
        V(lambda e: e.tensor_scalar(slot(5, 0, SG), slot(5, 0, SG), 1.0, None, ALU.add))
        vsync()
        Sc(lambda e: e.activation(slot(5, 0, SG), slot(5, 0, SG), AF.Ln))
        ssync()
        V(lambda e: e.tensor_tensor(slot(6, 0, SG), slot(6, 0, SG), slot(5, 0, SG), ALU.add))
        csnap("bbm1c0", slot(6, 0, SG), [128, SG])
        coeffs()
        SCN(lambda e: e.tensor_tensor_scan(slot(5, 0, SG), ONES0[:], slot(4, 0, SG),
                                         0.0, ALU.mult, ALU.add))
        vsync()
        Sc(lambda e: e.activation(slot(5, 0, SG), slot(5, 0, SG), AF.Exp))
        ssync()
        V(lambda e: e.tensor_copy(PG, slot(5, 0, SG)[:, L - 1:SG:L]))
        V(lambda e: e.tensor_copy(slot(9, 0, SG)[:, 1:SG], slot(5, 0, SG)[:, 0:SG - 1]))
        V(lambda e: e.memset(slot(9, 0, SG)[:, 0:SG:L], 1.0))
        V(lambda e: e.tensor_tensor(slot(5, 0, SG), slot(8, 0, SG), v, ALU.mult))
        V(lambda e: e.tensor_copy(slot(4, 0, SG), slot(7, 0, SG)))
        V(lambda e: e.memset(slot(4, 0, SG)[:, 0:SG:L], 0.0))
        SCN(lambda e: e.tensor_tensor_scan(slot(7, 0, SG), slot(4, 0, SG), slot(5, 0, SG),
                                         0.0, ALU.mult, ALU.add))
        V(lambda e: e.tensor_copy(RE, slot(7, 0, SG)[:, L - 1:SG:L]))
        VF(lambda e: e.tensor_tensor(T1[:, 0:1], CA, PG[:, 0:1], ALU.mult))
        VF(lambda e: e.tensor_tensor(RE[:, 0:1], RE[:, 0:1], T1[:, 0:1], ALU.add))
        SCN(lambda e: e.tensor_tensor_scan(AC, PG, RE, 0.0, ALU.mult, ALU.add))
        V(lambda e: e.tensor_copy(T1[:, 1:L], AC[:, 0:L - 1]))
        VF(lambda e: e.tensor_copy(T1[:, 0:1], CA))
        V(lambda e: e.memset(slot(4, 0, SG), 0.0))
        V(lambda e: e.tensor_copy(slot(4, 0, SG)[:, 0:SG:L], T1))
        SCN(lambda e: e.tensor_tensor_scan(slot(8, 0, SG), ONES0[:], slot(4, 0, SG),
                                         0.0, ALU.mult, ALU.add))
        V(lambda e: e.tensor_copy(slot(4, 0, SG)[:, 1:SG], slot(7, 0, SG)[:, 0:SG - 1]))
        V(lambda e: e.memset(slot(4, 0, SG)[:, 0:SG:L], 0.0))
        V(lambda e: e.tensor_tensor(slot(8, 0, SG), slot(8, 0, SG), slot(9, 0, SG), ALU.mult))
        V(lambda e: e.tensor_tensor(slot(8, 0, SG), slot(8, 0, SG), slot(4, 0, SG), ALU.add))
        csnap("aam10", slot(8, 0, SG), [128, SG])
        VF(lambda e: e.tensor_scalar(T1[:, 0:1], slot(6, 0, SG)[:, SG - 1:SG], w_, None, ALU.add))
        VF(lambda e: e.tensor_tensor(T1[:, 1:2], T1[:, 0:1], k[:, SG - 1:SG], ALU.max))
        VF(lambda e: e.tensor_tensor(T1[:, 0:1], T1[:, 0:1], T1[:, 1:2], ALU.subtract))
        VF(lambda e: e.tensor_tensor(T1[:, 2:3], k[:, SG - 1:SG], T1[:, 1:2], ALU.subtract))
        vsync()
        Sc(lambda e: e.activation(T1[:, 0:1], T1[:, 0:1], AF.Exp))
        Sc(lambda e: e.activation(T1[:, 2:3], T1[:, 2:3], AF.Exp))
        ssync()
        VF(lambda e: e.tensor_tensor(CAo, T1[:, 0:1], slot(8, 0, SG)[:, SG - 1:SG], ALU.mult))
        VF(lambda e: e.tensor_tensor(T1[:, 3:4], T1[:, 2:3], v[:, SG - 1:SG], ALU.mult))
        VF(lambda e: e.tensor_tensor(CAo, CAo, T1[:, 3:4], ALU.add))
        VF(lambda e: e.tensor_tensor(T1[:, 0:1], T1[:, 0:1], T1[:, 2:3], ALU.add))
        VF(lambda e: e.tensor_scalar(T1[:, 0:1], T1[:, 0:1], EPS, None, ALU.add))
        vsync()
        Sc(lambda e: e.activation(T1[:, 0:1], T1[:, 0:1], AF.Ln))
        ssync()
        VF(lambda e: e.tensor_tensor(CBo, T1[:, 1:2], T1[:, 0:1], ALU.add))
        V(lambda e: e.tensor_scalar(slot(4, 0, SG), k, u_, None, ALU.add))
        V(lambda e: e.tensor_tensor(slot(5, 0, SG), slot(6, 0, SG), slot(4, 0, SG), ALU.max))
        V(lambda e: e.tensor_tensor(slot(9, 0, SG), slot(6, 0, SG), slot(5, 0, SG), ALU.subtract))
        V(lambda e: e.tensor_tensor(slot(4, 0, SG), slot(4, 0, SG), slot(5, 0, SG), ALU.subtract))
        vsync()
        Sc(lambda e: e.activation(slot(9, 0, SG), slot(9, 0, SG), AF.Exp))
        Sc(lambda e: e.activation(slot(4, 0, SG), slot(4, 0, SG), AF.Exp))
        Sc(lambda e: e.activation(slot(2, 0, SG), r, AF.Sigmoid))
        ssync()
        V(lambda e: e.tensor_tensor(slot(5, 0, SG), slot(9, 0, SG), slot(4, 0, SG), ALU.add))
        V(lambda e: e.tensor_scalar(slot(5, 0, SG), slot(5, 0, SG), EPS, None, ALU.add))
        V(lambda e: e.reciprocal(slot(5, 0, SG), slot(5, 0, SG)))
        V(lambda e: e.tensor_tensor(slot(9, 0, SG), slot(9, 0, SG), slot(8, 0, SG), ALU.mult))
        V(lambda e: e.tensor_tensor(slot(4, 0, SG), slot(4, 0, SG), v, ALU.mult))
        V(lambda e: e.tensor_tensor(slot(9, 0, SG), slot(9, 0, SG), slot(4, 0, SG), ALU.add))
        V(lambda e: e.tensor_tensor(slot(9, 0, SG), slot(9, 0, SG), slot(5, 0, SG), ALU.mult))
        csnap("y0", slot(9, 0, SG), [128, SG])
        csnap("rsig0", slot(2, 0, SG), [128, SG])
        V(lambda e: e.tensor_tensor(slot(9, 0, SG), slot(9, 0, SG), slot(2, 0, SG), ALU.mult))
        V(lambda e: e.tensor_tensor(slot(9, 0, SG - 1), slot(9, 0, SG - 1), zx[:, 1:SG],
                                    ALU.add))
        return V(lambda e: e.tensor_tensor(slot(9, SG - 1, SG), slot(9, SG - 1, SG),
                                           SM[:, 790:791], ALU.add), inc=("vS", 1))

    for ci in range(HCH):
        tw = None
        for sub in range(3):
            for kc in range(NCH):
                tw = dma("sync", WTS[:, sub * 1024 + kc * 128:sub * 1024 + (kc + 1) * 128],
                         WTMC[kc * 128:(kc + 1) * 128,
                              sub * HC + ci * 128:sub * HC + (ci + 1) * 128],
                         pre=P.tok("vS"))
        P.wait("vector", tw)
        P.wait("vector", P.tok("tS"))
        for sub in range(3):
            src = WTS[:, sub * 1024:(sub + 1) * 1024]
            hi = WTB[:, sub * 1024:(sub + 1) * 1024]
            lo = WTB[:, (3 + sub) * 1024:(4 + sub) * 1024]
            P.op("vector", lambda e, o=hi, i=src: e.tensor_copy(o, i))
            P.op("vector", lambda e, o=src, i=hi: e.tensor_tensor(o, o, i, ALU.subtract))
            P.op("vector", lambda e, o=lo, i=src: e.tensor_copy(o, i))
        tws = P.op("vector", lambda e: e.engine_nop(), inc=("vS", 1))
        P.wait("tensor", tws)
        for s in range(NSEG):
            P.wait("vector", P.tok("dG"))
            tz = dma("scalar", slot(3, 0, SG),
                     ZOWN[ci * 128:(ci + 1) * 128, s * SG:s * SG + SG],
                     pre=P.tok("vS"))
            tz = dma("scalar", SM[:, 790:791],
                     ZOWN[ci * 128:(ci + 1) * 128, s * SG + SG:s * SG + SG + 1],
                     pre=P.tok("vS"), ncok=True)
            for tt8 in range(8):
                gcol = s * SG + tt8 * TT
                zbuf = tt8 % 2

                def z2b(lo, hi, b2=zbuf):
                    if b2 == 0:
                        return strb(lo, hi)
                    return slotb(4, lo, hi) if hi <= 2 * SLOT else slotb(5, lo - 2 * SLOT,
                                                                         hi - 2 * SLOT)
                tz2 = None
                for kc in range(NCH):
                    tz2 = dma("sync", z2b(kc * 1024, kc * 1024 + 512),
                              Z2H[kc * 128:(kc + 1) * 128, gcol:gcol + TT],
                              pre=[LG.get(("z2", zbuf)), P.tok("vS")])
                    tz2 = dma("sync", z2b(kc * 1024 + 512, kc * 1024 + 1024),
                              Z2L[kc * 128:(kc + 1) * 128, gcol:gcol + TT],
                              pre=tz2)
                P.wait("tensor", tz2)
                for sub in range(3):
                    pb = PS[sub]
                    bank_wait(sub)
                    nmm = 0
                    for kc in range(NCH):
                        wh = WTB[:, sub * 1024 + kc * 128:sub * 1024 + (kc + 1) * 128]
                        wl = WTB[:, (3 + sub) * 1024 + kc * 128:(3 + sub) * 1024 + (kc + 1) * 128]
                        zh = z2b(kc * 1024, kc * 1024 + 512)
                        zl = z2b(kc * 1024 + 512, kc * 1024 + 1024)
                        for (a_, b_) in ((wh, zh), (wh, zl), (wl, zh)):
                            P.op("tensor", lambda e, a=a_, b=b_, p=pb, st2=(nmm == 0),
                                 sp2=(nmm == 23): e.matmul(p[:], a, b, start=st2, stop=sp2),
                                 inc=("tS", 1) if nmm == 23 else None)
                            nmm += 1
                    tmm = P.tok("tS")
                    if sub == 2:
                        LG[("z2", zbuf)] = tmm
                    P.wait("vector", tmm)
                    tev = P.op("vector", lambda e, d=slot(sub, tt8 * TT, (tt8 + 1) * TT), p=pb:
                               e.tensor_copy(d, p[:]), inc=("vS", 1))
                    bank_done(sub, tev)
            if ci == 0 and s == 0:
                snap("kraw0", slot(0, 0, SG), [128, SG])
                snap("vraw0", slot(1, 0, SG), [128, SG])
                snap("zprev0", slot(3, 0, SG), [128, SG])
                snap("vsb0", VSB[:, 0:192], [128, 192])
            P.wait("vector", tz)
            for sub, nmv, nmo in ((0, "tmk", "om_tmk"), (1, "tmv", "om_tmv"), (2, "tmr", "om_tmr")):
                P.op("vector", lambda e, a=slot(sub, 0, SG), m=vg(nmv, ci):
                     e.tensor_scalar(a, a, m, None, ALU.mult))
                if sub == 0 and ci == 0 and s == 0:
                    snap("klerp1", slot(0, 0, SG), [128, SG])
                P.op("vector", lambda e, o=vg(nmo, ci), z=slot(3, 0, SG):
                     e.tensor_scalar(slot(4, 0, SG), z, o, None, ALU.mult))
                if sub == 0 and ci == 0 and s == 0:
                    snap("klerp2", slot(4, 0, SG), [128, SG])
                P.op("vector", lambda e, a=slot(sub, 0, SG):
                     e.tensor_tensor(a, a, slot(4, 0, SG), ALU.add))
            if ci == 0 and s == 0:
                snap("kl0", slot(0, 0, SG), [128, SG])
                snap("vl0", slot(1, 0, SG), [128, SG])
            tx1 = wkv_chain(ci, s)
            dma("gpsimd", X1IN[ci * 128:(ci + 1) * 128, s * SG:(s + 1) * SG],
                slot(9, 0, SG), pre=tx1)

    P.wait("gpsimd", P.tok("dG"))
    tc1 = P.op("gpsimd", lambda e: e.collective_compute(
        "AllGather", ALU.bypass, replica_groups=[[0, 1], [2, 3], [4, 5], [6, 7]],
        ins=[X1IN.opt()], outs=[X1OUT.opt()]), inc=("cS", 1))
    tc2 = tc1

    def split_store(src_dram, row0, cols, dsth, dstl, kc_idx, ncb):
        P.wait("vector", P.tok("dS"))
        tl = dma("sync", slot(0, 0, cols), src_dram[row0:row0 + 128, 0:cols], pre=P.tok("vS"))
        P.wait("vector", tl)
        hi = slotb(3, 0, cols)
        lo = slotb(3, SLOT, SLOT + cols)
        P.op("vector", lambda e, o=hi: e.tensor_copy(o, slot(0, 0, cols)))
        P.op("vector", lambda e, i=hi: e.tensor_tensor(slot(0, 0, cols), slot(0, 0, cols), i,
                                                       ALU.subtract))
        tv = P.op("vector", lambda e, o=lo: e.tensor_copy(o, slot(0, 0, cols)), inc=("vS", 1))
        for ob in range(ncb):
            dma("sync", dsth[ob][:, kc_idx * 128:(kc_idx + 1) * 128],
                hi[:, ob * 128:(ob + 1) * 128], pre=tv)
            dma("sync", dstl[ob][:, kc_idx * 128:(kc_idx + 1) * 128],
                lo[:, ob * 128:(ob + 1) * 128], pre=tv)

    for kc in range(NCH):
        split_store(WCM, kc * 128, 4096, WCMHB, WCMLB, kc, 32)
    for kc in range(32):
        split_store(WCP, kc * 128, 1024, WCPHB, WCPLB, kc, 8)
    for nm, src in (("wq", WQT), ("wk", WKT), ("wv", WVT)):
        for kc in range(NCH):
            split_store(src, kc * 128, 1024, PRJ[nm + "h"], PRJ[nm + "l"], kc, 8)
    tvw = None
    for kc in range(NCH):
        tl = dma("sync", slot(0, 0, 1024), L0WT[kc * 128:(kc + 1) * 128, :], pre=P.tok("vS"))
        P.wait("vector", tl)
        P.op("vector", lambda e: e.scalar_tensor_tensor(slot(1, 0, 1024), slot(0, 0, 1024),
                                                        -1.0, slot(0, 0, 1024),
                                                        ALU.mult, ALU.max))
        if kc == 0:
            tvw = P.op("vector", lambda e: e.reduce_sum(SM[:, 860:861], slot(1, 0, 1024),
                                                        AX), inc=("vS", 1))
            P.wait("vector", tvw)
        else:
            trd = P.op("vector", lambda e: e.reduce_sum(SM[:, 861:862], slot(1, 0, 1024),
                                                        AX), inc=("vS", 1))
            P.wait("vector", trd)
            tvw = P.op("vector", lambda e: e.tensor_tensor(SM[:, 860:861], SM[:, 860:861],
                                                           SM[:, 861:862], ALU.add),
                       inc=("vS", 1))
    P.wait("tensor", tvw)
    bank_wait("PROW")
    tt_ = P.op("tensor", lambda e: e.matmul(PROW[0:1, 0:1], SM[:, 860:861], vg("one"),
                                            start=True, stop=True), inc=("tS", 1))
    P.wait("vector", tt_)
    VF(lambda e: e.tensor_scalar(vc("wsc"), PROW[0:1, 0:1], 1.0 / float(C * C),
                                 None, ALU.mult))
    tv = VF(lambda e: e.tensor_scalar(SR[0:1, 1000:1001], vc("wsc"), 0.5, None,
                                      ALU.mult))
    bank_done("PROW", tv)
    t_thb = bcast_cell(SR[0:1, 1000:1001], vg("thb"), tv)
    P.wait("vector", t_thb)
    VF(lambda e: e.tensor_scalar(vg("thnb"), vg("thb"), -1.0, None, ALU.mult))
    for kc in range(NCH):
        P.wait("vector", P.tok("dS"))
        tl = dma("sync", slot(0, 0, 1024), L0WT[kc * 128:(kc + 1) * 128, :], pre=P.tok("vS"))
        P.wait("vector", tl)
        P.op("vector", lambda e: e.tensor_scalar(slot(1, 0, 1024), slot(0, 0, 1024), vg("thb"),
                                                 None, ALU.is_gt))
        P.op("vector", lambda e: e.tensor_scalar(slot(2, 0, 1024), slot(0, 0, 1024), vg("thnb"),
                                                 None, ALU.is_lt))
        P.op("vector", lambda e: e.tensor_tensor(slot(1, 0, 1024), slot(1, 0, 1024),
                                                 slot(2, 0, 1024), ALU.subtract))
        tv = P.op("vector", lambda e: e.tensor_copy(slotb(3, 0, 1024), slot(1, 0, 1024)),
                  inc=("vS", 1))
        for ob in range(8):
            dma("sync", QWTB[ob][:, kc * 128:(kc + 1) * 128],
                slotb(3, ob * 128, (ob + 1) * 128), pre=tv)
    for kc in range(NCH):
        P.wait("vector", P.tok("dS"))
        dma("sync", stru(0, 4096), K0A[kc * 128:(kc + 1) * 128, :], pre=P.tok("vS"))
        tb = dma("sync", stru(4096, 8192), K0B[kc * 128:(kc + 1) * 128, :], pre=P.tok("vS"))
        P.wait("vector", tb)
        tv = P.op("vector", lambda e: e.tensor_tensor(slotb(3, 0, 4096), stru(0, 4096),
                                                      stru(4096, 8192), ALU.subtract),
                  inc=("vS", 1))
        for ob in range(32):
            dma("sync", D0B[ob][:, kc * 128:(kc + 1) * 128],
                slotb(3, ob * 128, (ob + 1) * 128), pre=tv)
    for kc in range(32):
        P.wait("vector", P.tok("dS"))
        ta = dma("sync", stru(0, 4096), K1U[kc * 128:(kc + 1) * 128, :], pre=P.tok("vS"))
        P.wait("vector", ta)
        tv = P.op("vector", lambda e: e.tensor_scalar(slotb(3, 0, 4096), stru(0, 4096),
                                                      2.0, -1.0, ALU.mult, ALU.add),
                  inc=("vS", 1))
        for ob in range(32):
            dma("sync", W1BB[ob][:, kc * 128:(kc + 1) * 128],
                slotb(3, ob * 128, (ob + 1) * 128), pre=tv)
    for kc in range(32):
        P.wait("vector", P.tok("dS"))
        ta = dma("sync", stru(0, 1024), K2U[kc * 128:(kc + 1) * 128, :], pre=P.tok("vS"))
        P.wait("vector", ta)
        tv = P.op("vector", lambda e: e.tensor_scalar(slotb(3, 0, 1024), stru(0, 1024),
                                                      2.0, -1.0, ALU.mult, ALU.add),
                  inc=("vS", 1))
        for ob in range(8):
            dma("sync", W2BB[ob][:, kc * 128:(kc + 1) * 128],
                slotb(3, ob * 128, (ob + 1) * 128), pre=tv)

    P.barrier()
    P.wait("gpsimd", tc1)
    snap("x1out", X1OUT, [C, T])

    for tt in range(NTO):
        c0 = tt * TT
        P.wait("vector", P.tok("dG"))
        hA = (c0 - 1) % T
        hB = TO + c0 - 1
        tl = None
        for ci in range(NCH):
            tl = dma("sync", slot(0, ci * TT, (ci + 1) * TT),
                     X1OUT[ci * 128:(ci + 1) * 128, c0:c0 + TT],
                     pre=[P.tok("vS"), P.tok("dG")])
            tl = dma("sync", slot(5, ci * TT, (ci + 1) * TT),
                     X1OUT[ci * 128:(ci + 1) * 128, TO + c0:TO + c0 + TT],
                     pre=[P.tok("vS"), P.tok("tS")])
            tl = dma("sync", SM[:, 800 + ci:801 + ci],
                     X1OUT[ci * 128:(ci + 1) * 128, hA:hA + 1], pre=P.tok("vS"), ncok=True)
            tl = dma("sync", SM[:, 808 + ci:809 + ci],
                     X1OUT[ci * 128:(ci + 1) * 128, hB:hB + 1], pre=P.tok("vS"), ncok=True)
        P.wait("vector", tl)
        for ci in range(NCH):
            P.op("vector", lambda e, ci=ci: e.tensor_tensor(
                slot(5, ci * TT, (ci + 1) * TT), slot(5, ci * TT, (ci + 1) * TT),
                slot(0, ci * TT, (ci + 1) * TT), ALU.subtract))
            P.op("vector", lambda e, ci=ci: e.tensor_scalar(
                slot(5, ci * TT, (ci + 1) * TT), slot(5, ci * TT, (ci + 1) * TT), SELS[:],
                None, ALU.mult))
            P.op("vector", lambda e, ci=ci: e.tensor_tensor(
                slot(0, ci * TT, (ci + 1) * TT), slot(0, ci * TT, (ci + 1) * TT),
                slot(5, ci * TT, (ci + 1) * TT), ALU.add))
            VF(lambda e, ci=ci: e.tensor_tensor(
                SM[:, 808 + ci:809 + ci], SM[:, 808 + ci:809 + ci],
                SM[:, 800 + ci:801 + ci], ALU.subtract))
            VF(lambda e, ci=ci: e.tensor_scalar(
                SM[:, 808 + ci:809 + ci], SM[:, 808 + ci:809 + ci], SELS[:], None, ALU.mult))
            VF(lambda e, ci=ci: e.tensor_tensor(
                SM[:, 800 + ci:801 + ci], SM[:, 800 + ci:801 + ci],
                SM[:, 808 + ci:809 + ci], ALU.add))
        for ci in range(NCH):
            P.op("vector", lambda e, ci=ci: e.tensor_copy(
                slot(1, ci * TT + 1, (ci + 1) * TT), slot(0, ci * TT, (ci + 1) * TT - 1)))
            P.op("vector", lambda e, ci=ci: e.tensor_copy(
                slot(1, ci * TT, ci * TT + 1), SM[:, 800 + ci:801 + ci]))
        tv = P.op("vector", lambda e: e.engine_nop(), inc=("vS", 1))
        tok = stats_pass(lambda ci: slot(0, ci * TT, (ci + 1) * TT), NCH,
                         lambda ci: slot(4, ci * TT, (ci + 1) * TT), tv)
        tok = stats_finalize(tok)
        tbc = stats_bcast(tok)
        P.wait("vector", tbc)
        for ci in range(NCH):
            sx = slot(0, ci * TT, (ci + 1) * TT)
            sn = slot(2, ci * TT, (ci + 1) * TT)
            P.op("vector", lambda e, a=sn, b=sx: e.tensor_tensor(a, b, PB0[:], ALU.subtract))
            P.op("vector", lambda e, a=sn: e.tensor_tensor(a, a, PB1[:], ALU.mult))
            P.op("vector", lambda e, a=sn, g=vg("g2", ci), b=vg("b2", ci):
                 e.tensor_scalar(a, a, g, b, ALU.mult, ALU.add))
            bh = slotb(3, ci * TT, (ci + 1) * TT)
            bl = slotb(3, SLOT + ci * TT, SLOT + (ci + 1) * TT)
            P.op("vector", lambda e, o=bh, i=sn: e.tensor_copy(o, i))
            P.op("vector", lambda e, a=slot(4, ci * TT, (ci + 1) * TT), b=sn, i=bh:
                 e.tensor_tensor(a, b, i, ALU.subtract))
            P.op("vector", lambda e, o=bl, i=slot(4, ci * TT, (ci + 1) * TT): e.tensor_copy(o, i))
        tvn = P.op("vector", lambda e: e.engine_nop(), inc=("vS", 1))
        bank_done("PB0", tvn)
        bank_done("PB1", tvn)
        P.wait("tensor", tvn)
        for oc in range(32):
            buf = oc % 2
            tl2 = None
            tl2 = dma("sync", strb(buf * 2048, buf * 2048 + 1024),
                      WCMHB[oc], pre=LG.get(("kk", buf)))
            tl2 = dma("sync", strb(4096 + buf * 2048, 4096 + buf * 2048 + 1024),
                      WCMLB[oc], pre=tl2)
            P.wait("tensor", tl2)
            pb = PS[oc % 2]
            bank_wait(oc % 2)
            nmm = 0
            for kc in range(NCH):
                wh = strb(buf * 2048 + kc * 128, buf * 2048 + (kc + 1) * 128)
                wl = strb(4096 + buf * 2048 + kc * 128, 4096 + buf * 2048 + (kc + 1) * 128)
                xh = slotb(3, kc * TT, (kc + 1) * TT)
                xl = slotb(3, SLOT + kc * TT, SLOT + (kc + 1) * TT)
                for (a_, b_) in ((wh, xh), (wh, xl), (wl, xh)):
                    P.op("tensor", lambda e, a=a_, b=b_, p=pb, st2=(nmm == 0), sp2=(nmm == 23):
                         e.matmul(p[:], a, b, start=st2, stop=sp2),
                         inc=("tS", 1) if nmm == 23 else None)
                    nmm += 1
            tmm = P.tok("tS")
            LG[("kk", buf)] = tmm
            P.wait("vector", tmm)
            cg = oc % 8
            sq = slot(4, 0, TT)
            P.op("vector", lambda e, p=pb, m=vg("cmk", cg), o=sq:
                 e.tensor_scalar(o, p[:], m, None, ALU.mult))
            P.op("vector", lambda e, xp=slot(1, cg * TT, (cg + 1) * TT), om=vg("om_cmk", cg):
                 e.tensor_scalar(slot(4, 4 * TT, 5 * TT), xp, om, None, ALU.mult))
            P.op("vector", lambda e, o=sq:
                 e.tensor_tensor(o, o, slot(4, 4 * TT, 5 * TT), ALU.add))
            P.op("vector", lambda e, o=sq: e.tensor_scalar(o, o, 0.0, None, ALU.max))
            P.op("vector", lambda e, o=sq: e.tensor_tensor(o, o, o, ALU.mult))
            qh = WK[:, 6 * SLOT:8 * SLOT].bitcast(BF16)[:, oc * TT:(oc + 1) * TT]
            ql = WK[:, 8 * SLOT:10 * SLOT].bitcast(BF16)[:, oc * TT:(oc + 1) * TT]
            P.op("vector", lambda e, o=qh, i=sq: e.tensor_copy(o, i))
            P.op("vector", lambda e, o=sq, i=qh: e.tensor_tensor(o, o, i, ALU.subtract))
            tvq = P.op("vector", lambda e, o=ql, i=sq: e.tensor_copy(o, i), inc=("vS", 1))
            bank_done(oc % 2, tvq)
        tl3 = None
        for ci in range(NCH):
            tl3 = dma("sync", slot(3, ci * TT, (ci + 1) * TT),
                      X0O[ci * 128:(ci + 1) * 128, c0:c0 + TT],
                      pre=[P.tok("vS"), P.tok("tS")])
        P.wait("tensor", P.tok("vS"))
        last_dma = {}
        for oc2 in range(NCH):
            buf2 = oc2 % 2

            def vvb(lo, hi, b2=buf2):
                if b2 == 0:
                    return strb(lo, hi)
                return slotb(5, lo, hi)
            tl2 = None
            tl2 = dma("scalar", vvb(0, 4096), WCPHB[oc2],
                      pre=[LG.get(("vv", buf2)), P.tok("vS")])
            tl2 = dma("scalar", vvb(4096, 8192), WCPLB[oc2], pre=tl2)
            P.wait("tensor", tl2)
            pb = PS[2 + oc2 % 2]
            bank_wait(2 + oc2 % 2)
            nmm = 0
            for kc in range(32):
                wh = vvb(kc * 128, (kc + 1) * 128)
                wl = vvb(4096 + kc * 128, 4096 + (kc + 1) * 128)
                sqh = WK[:, 6 * SLOT:8 * SLOT].bitcast(BF16)[:, kc * TT:(kc + 1) * TT]
                sql = WK[:, 8 * SLOT:10 * SLOT].bitcast(BF16)[:, kc * TT:(kc + 1) * TT]
                for (a_, b_) in ((wh, sqh), (wh, sql), (wl, sqh)):
                    P.op("tensor", lambda e, a=a_, b=b_, p=pb, st2=(nmm == 0), sp2=(nmm == 95):
                         e.matmul(p[:], a, b, start=st2, stop=sp2),
                         inc=("tS", 1) if nmm == 95 else None)
                    nmm += 1
            tmm = P.tok("tS")
            LG[("vv", buf2)] = tmm
            par = oc2 % 2
            rr = slot(4, (2 * par) * TT, (2 * par + 1) * TT)
            out_ = slot(4, (2 * par + 1) * TT, (2 * par + 2) * TT)
            P.wait("vector", last_dma.get(par))
            P.op("vector", lambda e, o=rr, xn=slot(2, oc2 * TT, (oc2 + 1) * TT), m=vg("cmr", oc2):
                 e.tensor_scalar(o, xn, m, None, ALU.mult))
            P.op("vector", lambda e, xp=slot(1, oc2 * TT, (oc2 + 1) * TT), om=vg("om_cmr", oc2):
                 e.tensor_scalar(slot(4, 5 * TT, 6 * TT), xp, om, None, ALU.mult))
            tvr = P.op("vector", lambda e, o=rr:
                       e.tensor_tensor(o, o, slot(4, 5 * TT, 6 * TT), ALU.add), inc=("vS", 1))
            P.wait("scalar", tvr)
            tsr = P.op("scalar", lambda e, o=rr: e.activation(o, o, AF.Sigmoid), inc=("sS", 1))
            P.wait("vector", tsr)
            P.wait("vector", tmm)
            if oc2 == 0:
                P.wait("vector", tl3)
            P.op("vector", lambda e, o=out_, p=pb, r2=rr: e.tensor_tensor(o, r2, p[:], ALU.mult))
            P.op("vector", lambda e, o=out_, x1=slot(0, oc2 * TT, (oc2 + 1) * TT):
                 e.tensor_tensor(o, o, x1, ALU.add))
            tvo = P.op("vector", lambda e, o=out_, x0=slot(3, oc2 * TT, (oc2 + 1) * TT):
                       e.tensor_tensor(o, o, x0, ALU.add), inc=("vS", 1))
            bank_done(2 + oc2 % 2, tvo)
            tdm = dma("gpsimd", RRES[oc2 * 128:(oc2 + 1) * 128, c0:c0 + TT], out_, pre=tvo)
            last_dma[par] = tdm
        P.barrier()

    snap("rres3", RRES, [C, TO])
    P.op("vector", lambda e: e.memset(SM[:, 820:821], 0.0))

    def ln_tile(tt, dst_slot):
        c0 = tt * TT
        tl = None
        for ci in range(NCH):
            tl = dma("sync", slot(0, ci * TT, (ci + 1) * TT),
                     RRES[ci * 128:(ci + 1) * 128, c0:c0 + TT],
                     pre=[P.tok("vS"), P.tok("dG")])
        tok = stats_pass(lambda ci: slot(0, ci * TT, (ci + 1) * TT), NCH,
                         lambda ci: slot(dst_slot, ci * TT, (ci + 1) * TT), tl)
        tok = stats_finalize(tok)
        tbc = stats_bcast(tok)
        P.wait("vector", tbc)
        for ci in range(NCH):
            sx = slot(0, ci * TT, (ci + 1) * TT)
            sz = slot(dst_slot, ci * TT, (ci + 1) * TT)
            P.op("vector", lambda e, a=sz, b=sx: e.tensor_tensor(a, b, PB0[:], ALU.subtract))
            P.op("vector", lambda e, a=sz: e.tensor_tensor(a, a, PB1[:], ALU.mult))
        tv_ = P.op("vector", lambda e: e.engine_nop(), inc=("vS", 1))
        bank_done("PB0", tv_)
        bank_done("PB1", tv_)
        return tv_

    for tt in range(NTO):
        c0 = tt * TT
        ln_tile(tt, 1)
        for ci in range(NCH):
            trd = P.op("vector", lambda e, a=slot(1, ci * TT, (ci + 1) * TT):
                       e.tensor_reduce(SM[:, 821:822], a, AX, ALU.max,
                                       apply_absolute_value=True), inc=("vS", 1))
            P.wait("vector", trd)
            P.op("vector", lambda e: e.tensor_tensor(SM[:, 820:821], SM[:, 820:821],
                                                     SM[:, 821:822], ALU.max))
        tvx4 = P.op("vector", lambda e: e.engine_nop(), inc=("vS", 1))
        for ci in range(NCH):
            dma("gpsimd", X4S[ci * 128:(ci + 1) * 128, c0:c0 + TT],
                slot(1, ci * TT, (ci + 1) * TT), pre=tvx4)
    tv = P.tok("vS")
    P.wait("tensor", tv)
    bank_wait("PT")
    tt_ = P.op("tensor", lambda e: e.transpose(PT[0:1, 0:128], SM[:, 820:821], IDNS[:]),
               inc=("tS", 1))
    P.wait("vector", tt_)
    trd = P.op("vector", lambda e: e.tensor_reduce(SR[0:1, 900:901], PT[0:1, 0:128],
                                                   AX, ALU.max), inc=("vS", 1))
    P.wait("vector", trd)
    tv = VF(lambda e: e.tensor_copy(SR[0:1, 901:909],
                                    SR[0:1, 900:901].broadcast_to((1, 8))))
    bank_done("PT", tv)
    dma("gpsimd", ARI, SR[0:1, 901:909], pre=tv)
    P.wait("gpsimd", P.tok("dG"))
    tc3 = P.op("gpsimd", lambda e: e.collective_compute(
        "AllReduce", ALU.max, replica_groups=[list(range(8))],
        ins=[ARI.opt()], outs=[ARO.opt()]), inc=("cS", 1))
    P.wait("sync", tc3)
    tgm = dma("sync", SR[0:1, 910:918], ARO)
    P.wait("vector", tgm)
    VF(lambda e: e.tensor_scalar(vc("iscale"), SR[0:1, 910:911], 1.0 / 127.0,
                                 None, ALU.mult))
    VF(lambda e: e.reciprocal(SR[0:1, 920:921], vc("iscale")))
    tv = VF(lambda e: e.tensor_tensor(vc("wsis"), vc("wsc"), vc("iscale"), ALU.mult))
    t_invb = bcast_cell(SR[0:1, 920:921], vg("invb"), tv)
    t_wsis = bcast_cell(vc("wsis"), vg("wsisb"), t_invb)
    P.barrier()
    for tt in range(NTO):
        c0 = tt * TT
        tl4 = None
        for ci in range(NCH):
            tl4 = dma("sync", slot(0, ci * TT, (ci + 1) * TT),
                      RRES[ci * 128:(ci + 1) * 128, c0:c0 + TT],
                      pre=[P.tok("vS"), P.tok("dG")])
            tl4 = dma("sync", slot(1, ci * TT, (ci + 1) * TT),
                      X4S[ci * 128:(ci + 1) * 128, c0:c0 + TT], pre=P.tok("vS"))
        P.wait("vector", tl4)
        for ci in range(NCH):
            sz = slot(1, ci * TT, (ci + 1) * TT)
            P.op("vector", lambda e, a=sz: e.tensor_scalar(a, a, vg("invb"), None, ALU.mult))
            P.op("vector", lambda e, a=sz, o=sloti(2, ci * TT, (ci + 1) * TT): e.tensor_copy(o, a))
            P.op("vector", lambda e, a=sz, i=sloti(2, ci * TT, (ci + 1) * TT): e.tensor_copy(a, i))
            P.op("vector", lambda e, a=sz: e.tensor_scalar(a, a, 127.0, -128.0, ALU.min, ALU.max))
            P.op("vector", lambda e, a=sz, o=slotb(3, ci * TT, (ci + 1) * TT): e.tensor_copy(o, a))
        tvq = P.op("vector", lambda e: e.engine_nop(), inc=("vS", 1))
        P.wait("tensor", tvq)
        last_dma = {}
        for oc in range(NCH):
            buf = oc % 2
            tl2 = dma("sync", strb(buf * 2048, buf * 2048 + 1024),
                      QWTB[oc], pre=LG.get(("qw", buf)))
            P.wait("tensor", tl2)
            pb = PS[oc % 2]
            bank_wait(oc % 2)
            for kc in range(NCH):
                wq_ = strb(buf * 2048 + kc * 128, buf * 2048 + (kc + 1) * 128)
                qi = slotb(3, kc * TT, (kc + 1) * TT)
                P.op("tensor", lambda e, a=wq_, b=qi, p=pb, st2=(kc == 0), sp2=(kc == 7):
                     e.matmul(p[:], a, b, start=st2, stop=sp2),
                     inc=("tS", 1) if kc == 7 else None)
            tmm = P.tok("tS")
            LG[("qw", buf)] = tmm
            P.wait("vector", tmm)
            par = oc % 2
            out_ = slot(4, par * TT, (par + 1) * TT)
            P.wait("vector", last_dma.get(par))
            P.op("vector", lambda e, o=out_, p=pb, b=vg("l0b", oc):
                 e.tensor_scalar(o, p[:], vg("wsisb"), b, ALU.mult, ALU.add))
            tvo = P.op("vector", lambda e, o=out_, x=slot(0, oc * TT, (oc + 1) * TT):
                       e.tensor_tensor(o, o, x, ALU.add), inc=("vS", 1))
            bank_done(oc % 2, tvo)
            last_dma[par] = dma("gpsimd", RRES[oc * 128:(oc + 1) * 128, c0:c0 + TT], out_,
                                pre=tvo)
        P.barrier()

    snap("rres4", RRES, [C, TO])
    P.op("vector", lambda e: e.memset(SM[0:1, 830:832], 0.0))
    for tt in range(NTO):
        c0 = tt * TT
        ln_tile(tt, 1)
        tvlast = None
        for ci in range(NCH):
            sz = slot(1, ci * TT, (ci + 1) * TT)
            bh = slotb(3, ci * TT, (ci + 1) * TT)
            bl = slotb(3, SLOT + ci * TT, SLOT + (ci + 1) * TT)
            P.op("vector", lambda e, o=bh, i=sz: e.tensor_copy(o, i))
            P.op("vector", lambda e, a=slot(2, ci * TT, (ci + 1) * TT), b=sz, i=bh:
                 e.tensor_tensor(a, b, i, ALU.subtract))
            tvlast = P.op("vector", lambda e, o=bl, i=slot(2, ci * TT, (ci + 1) * TT):
                          e.tensor_copy(o, i), inc=("vS", 1))
            dma("gpsimd", XCH[ci * 128:(ci + 1) * 128, c0:c0 + TT], bh, pre=tvlast)
            dma("gpsimd", XCH[C + ci * 128:C + (ci + 1) * 128, c0:c0 + TT], bl, pre=tvlast)
    P.wait("gpsimd", P.tok("dG"))
    tc4 = P.op("gpsimd", lambda e: e.collective_compute(
        "AllGather", ALU.bypass, replica_groups=[[0, 1], [2, 3], [4, 5], [6, 7]],
        ins=[XCH.opt()], outs=[XCG.opt()]), inc=("cS", 1))

    def proj(dst_h, dst_l, wname, bias_nm, scale_pow, row_h, row_l, src_dram, dst_col0,
             norm_cell, do_transpose=False):
        for tt in range(NTO):
            c0 = tt * TT
            tl = None
            for kc in range(NCH):
                tl = dma("sync", slotb(5, kc * TT, (kc + 1) * TT),
                         src_dram[row_h + kc * 128:row_h + (kc + 1) * 128, c0:c0 + TT],
                         pre=[P.tok("tS"), P.tok("dG")])
                tl = dma("sync", slotb(5, SLOT + kc * TT, SLOT + (kc + 1) * TT),
                         src_dram[row_l + kc * 128:row_l + (kc + 1) * 128, c0:c0 + TT],
                         pre=P.tok("tS"))
            P.wait("tensor", tl)
            tsq_prev = None
            for oc in range(NCH):
                buf = oc % 2
                tl2 = dma("scalar", strb(buf * 2048, buf * 2048 + 1024),
                          PRJ[wname + "h"][oc], pre=LG.get((wname, buf)))
                tl2 = dma("scalar", strb(4096 + buf * 2048, 4096 + buf * 2048 + 1024),
                          PRJ[wname + "l"][oc], pre=tl2)
                P.wait("tensor", tl2)
                pb = PS[oc % 2]
                bank_wait(oc % 2)
                nmm = 0
                for kc in range(NCH):
                    wh = strb(buf * 2048 + kc * 128, buf * 2048 + kc * 128 + 128)
                    wl = strb(4096 + buf * 2048 + kc * 128, 4096 + buf * 2048 + kc * 128 + 128)
                    xh = slotb(5, kc * TT, (kc + 1) * TT)
                    xl = slotb(5, SLOT + kc * TT, SLOT + (kc + 1) * TT)
                    for (a_, b_) in ((wh, xh), (wh, xl), (wl, xh)):
                        P.op("tensor", lambda e, a=a_, b=b_, p=pb, st2=(nmm == 0), sp2=(nmm == 23):
                             e.matmul(p[:], a, b, start=st2, stop=sp2),
                             inc=("tS", 1) if nmm == 23 else None)
                        nmm += 1
                tmm = P.tok("tS")
                LG[(wname, buf)] = tmm
                P.wait("vector", tmm)
                P.wait("vector", tsq_prev)
                sq = slot(4, 0, TT)
                if scale_pow is not None:
                    P.op("vector", lambda e, p=pb, o=sq, b=vg(bias_nm, oc):
                         e.tensor_scalar(o, p[:], b, scale_pow, ALU.add, ALU.mult))
                else:
                    P.op("vector", lambda e, p=pb, o=sq, b=vg(bias_nm, oc):
                         e.tensor_scalar(o, p[:], b, None, ALU.add))
                par = oc % 2
                bh = slotb(0, par * 2 * TT, par * 2 * TT + TT)
                bl = slotb(0, par * 2 * TT + TT, par * 2 * TT + 2 * TT)
                P.wait("vector", LG.get(("pst", wname, par)))
                P.op("vector", lambda e, o=bh, i=sq: e.tensor_copy(o, i))
                P.op("vector", lambda e, o=slot(1, 0, TT), a=sq, i=bh:
                     e.tensor_tensor(o, a, i, ALU.subtract))
                tvs = P.op("vector", lambda e, o=bl, i=slot(1, 0, TT): e.tensor_copy(o, i),
                           inc=("vS", 1))
                bank_done(oc % 2, tvs)
                if not do_transpose:
                    dma("sync", dst_h[oc * 128:(oc + 1) * 128, dst_col0 + c0:dst_col0 + c0 + TT],
                        bh, pre=tvs)
                    LG[("pst", wname, par)] = dma(
                        "sync", dst_l[oc * 128:(oc + 1) * 128, dst_col0 + c0:dst_col0 + c0 + TT],
                        bl, pre=tvs)
                else:
                    P.wait("tensor", tvs)
                    toks = []
                    for tb in range(4):
                        for si, (srcb, dstd) in enumerate(((bh, dst_h), (bl, dst_l))):
                            idx = tb * 2 + si
                            tgt = idx % 3
                            bank_wait(("PTB", tgt))
                            tt2 = P.op("tensor", lambda e, s2=srcb[:, tb * 128:(tb + 1) * 128],
                                       t3=PTBS[tgt]:
                                       e.transpose(t3, s2, IDNBS[:]), inc=("tS", 1))
                            P.wait("vector", tt2)
                            tvt = P.op("vector", lambda e, idx=idx, t3=PTBS[tgt]: e.tensor_copy(
                                slotb(2, idx * 128, (idx + 1) * 128), t3), inc=("vS", 1))
                            bank_done(("PTB", tgt), tvt)
                            dma("gpsimd",
                                dstd[dst_col0 + c0 + tb * 128:dst_col0 + c0 + (tb + 1) * 128,
                                     oc * 128:(oc + 1) * 128],
                                slotb(2, idx * 128, (idx + 1) * 128), pre=tvt)
                if norm_cell is not None:
                    P.wait("scalar", tvs)
                    sqb = WK[:, SLOT + TT:SLOT + 2 * TT].bitcast(BF16)[:, 0:TT]
                    tsq_prev = P.op("scalar", lambda e, o=sqb, i=sq:
                                    e.activation(o, i, AF.Square), inc=("sS", 1))
                    P.wait("tensor", tsq_prev)
                    if oc == 0:
                        bank_wait("PROW")
                    P.op("tensor", lambda e, r=sqb, st2=(oc == 0), sp2=(oc == 7):
                         e.matmul(PROW[0:1, :], ONEB[:], r, start=st2, stop=sp2),
                         inc=("tS", 1) if oc == 7 else None)
            if norm_cell is not None:
                tns = P.tok("tS")
                P.wait("vector", tns)
                trd = P.op("vector", lambda e: e.tensor_reduce(SM[0:1, 845:846], PROW[0:1, :],
                                                               AX, ALU.max), inc=("vS", 1))
                P.wait("vector", trd)
                P.op("vector", lambda e, nc_=norm_cell: e.tensor_tensor(
                    nc_, nc_, SM[0:1, 845:846], ALU.max))
                tve = P.op("vector", lambda e: e.engine_nop(), inc=("vS", 1))
                bank_done("PROW", tve)

    proj(QSH, QSL, "wq", "bq", 1.0 / 1024.0, 0, C, XCH, 0, SM[0:1, 830:831])
    P.barrier()
    P.wait("sync", tc4)
    P.wait("scalar", tc4)
    P.wait("tensor", tc4)
    P.wait("vector", tc4)
    P.wait("gpsimd", tc4)
    for h in range(2):
        proj(KSH, KSL, "wk", "bk", None, h * 2 * C, h * 2 * C + C, XCG, h * TO,
             SM[0:1, 831:832])
    for h in range(2):
        proj(VTH, VTL, "wv", "bv", None, h * 2 * C, h * 2 * C + C, XCG, h * TO,
             None, do_transpose=True)
    tv = P.tok("vS")
    P.wait("scalar", tv)
    P.op("scalar", lambda e: e.activation(SM[0:1, 830:831], SM[0:1, 830:831], AF.Sqrt))
    ts = P.op("scalar", lambda e: e.activation(SM[0:1, 831:832], SM[0:1, 831:832], AF.Sqrt),
              inc=("sS", 1))
    P.wait("vector", ts)
    VF(lambda e: e.tensor_tensor(vc("mbc"), SM[0:1, 830:831], SM[0:1, 831:832],
                                 ALU.mult))
    tv = VF(lambda e: e.tensor_scalar(SR[0:1, 930:931], vc("mbc"), -1.01, None,
                                      ALU.mult))
    t_negm = bcast_cell(SR[0:1, 930:931], vg("negmb"), tv)
    P.barrier()

    for qb in range(NTO):
        q0 = qb * TT
        tq = None
        for kc in range(NCH):
            tq = dma("sync", slotb(2, kc * TT, (kc + 1) * TT),
                     QSH[kc * 128:(kc + 1) * 128, q0:q0 + TT],
                     pre=[P.tok("vS"), P.tok("tS"), P.tok("dG")])
            tq = dma("sync", slotb(2, SLOT + kc * TT, SLOT + (kc + 1) * TT),
                     QSL[kc * 128:(kc + 1) * 128, q0:q0 + TT], pre=P.tok("vS"))
        P.op("vector", lambda e: e.memset(slot(8, 0, SLOT), 0.0))
        tvz = P.op("vector", lambda e: e.engine_nop(), inc=("vS", 1))
        P.wait("tensor", tvz)
        P.wait("tensor", tq)
        first_den = [True]
        tvp_prev = [None]
        for kt in range(NTT):
            k0_ = kt * TT
            buf = kt % 2
            tl = None
            for kc in range(NCH):
                tl = dma("sync", slotb(3 + buf, kc * TT, (kc + 1) * TT),
                         KSH[kc * 128:(kc + 1) * 128, k0_:k0_ + TT],
                         pre=LG.get(("fKV", buf)))
                tl = dma("sync", slotb(3 + buf, SLOT + kc * TT, SLOT + (kc + 1) * TT),
                         KSL[kc * 128:(kc + 1) * 128, k0_:k0_ + TT], pre=tl)
            tv_ = None
            for tb in range(4):
                tv_ = dma("scalar", slotb(5 + buf, tb * 1024, (tb + 1) * 1024),
                          VTH[k0_ + tb * 128:k0_ + (tb + 1) * 128, :],
                          pre=LG.get(("fKV", buf)))
                tv_ = dma("scalar", slotb(5 + buf, SLOT + tb * 1024, SLOT + (tb + 1) * 1024),
                          VTL[k0_ + tb * 128:k0_ + (tb + 1) * 128, :], pre=tv_)
            P.wait("tensor", tl)
            P.wait("tensor", tv_)
            for kc4 in range(4):
                pb = PS[kc4 % 2]
                bank_wait(kc4 % 2)
                nmm = 0
                for cc in range(NCH):
                    kh = slotb(3 + buf, cc * TT + kc4 * 128, cc * TT + (kc4 + 1) * 128)
                    kl = slotb(3 + buf, SLOT + cc * TT + kc4 * 128,
                               SLOT + cc * TT + (kc4 + 1) * 128)
                    qh = slotb(2, cc * TT, (cc + 1) * TT)
                    ql = slotb(2, SLOT + cc * TT, SLOT + (cc + 1) * TT)
                    for (a_, b_) in ((kh, qh), (kh, ql), (kl, qh)):
                        P.op("tensor", lambda e, a=a_, b=b_, p=pb, st2=(nmm == 0), sp2=(nmm == 23):
                             e.matmul(p[:], a, b, start=st2, stop=sp2),
                             inc=("tS", 1) if nmm == 23 else None)
                        nmm += 1
                tmm = P.tok("tS")
                P.wait("vector", tmm)
                P.wait("vector", tvp_prev[0])
                p32 = slot(7, 0, TT)
                tvv = P.op("vector", lambda e, p=pb, o=p32:
                           e.tensor_scalar(o, p[:], vg("negmb"), None, ALU.add),
                           inc=("vS", 1))
                P.wait("scalar", tvv)
                tse = P.op("scalar", lambda e, o=p32: e.activation(o, o, AF.Exp),
                           inc=("sS", 1))
                P.wait("vector", tse)
                if kt == qb:
                    P.op("vector", lambda e, o=p32[:, kc4 * 128:(kc4 + 1) * 128]:
                         e.tensor_tensor(o, o, MSKES[:], ALU.mult))
                if kt == 8 + qb:
                    P.op("vector", lambda e, o=p32[:, kc4 * 128:(kc4 + 1) * 128]:
                         e.tensor_tensor(o, o, MSKOS[:], ALU.mult))
                ph = slotb(7, 1024 + kc4 * TT, 1024 + (kc4 + 1) * TT)
                pl = slotb(7, 3072 + kc4 * TT, 3072 + (kc4 + 1) * TT)
                P.op("vector", lambda e, o=ph, i=p32: e.tensor_copy(o, i))
                P.op("vector", lambda e, i=ph, o=slot(9, 0, TT), a=p32:
                     e.tensor_tensor(o, a, i, ALU.subtract))
                tvp = P.op("vector", lambda e, o=pl: e.tensor_copy(o, slot(9, 0, TT)),
                           inc=("vS", 1))
                tvp_prev[0] = tvp
                bank_done(kc4 % 2, tvp)
                P.wait("tensor", tvp)
                for pp in (ph, pl):
                    sp_ = (kt == NTT - 1 and kc4 == 3 and pp is pl)
                    P.op("tensor", lambda e, r=pp, st2=first_den[0], sp2=sp_:
                         e.matmul(PROW[0:1, :], ONEB[:], r, start=st2, stop=sp2),
                         inc=("tS", 1) if sp_ else None)
                    first_den[0] = False
            for qc in range(4):
                for ct in range(2):
                    pb = PB0 if (qc * 2 + ct) % 2 == 0 else PB1
                    bid = "PB0" if (qc * 2 + ct) % 2 == 0 else "PB1"
                    bank_wait(bid)
                    bank_wait(("PTB", 1 if bid == "PB0" else 2))
                    nmm = 0
                    for kc4 in range(4):
                        ph = slotb(7, 1024 + kc4 * TT + qc * 128, 1024 + kc4 * TT + (qc + 1) * 128)
                        pl = slotb(7, 3072 + kc4 * TT + qc * 128, 3072 + kc4 * TT + (qc + 1) * 128)
                        vh = slotb(5 + buf, kc4 * 1024 + ct * 512, kc4 * 1024 + (ct + 1) * 512)
                        vl = slotb(5 + buf, SLOT + kc4 * 1024 + ct * 512,
                                   SLOT + kc4 * 1024 + (ct + 1) * 512)
                        for (a_, b_) in ((ph, vh), (pl, vh), (ph, vl)):
                            P.op("tensor", lambda e, a=a_, b=b_, p=pb, st2=(nmm == 0),
                                 sp2=(nmm == 11):
                                 e.matmul(p[:], a, b, start=st2, stop=sp2),
                                 inc=("tS", 1) if nmm == 11 else None)
                            nmm += 1
                    tmm = P.tok("tS")
                    if qc == 3 and ct == 1:
                        LG[("fKV", buf)] = tmm
                    P.wait("vector", tmm)
                    oa = WK[:, 8 * SLOT + qc * 1024 + ct * 512:8 * SLOT + qc * 1024 + (ct + 1) * 512]
                    tva = P.op("vector", lambda e, o=oa, p=pb:
                               e.tensor_tensor(o, o, p[:], ALU.add), inc=("vS", 1))
                    bank_done(bid, tva)
        tden = P.tok("tS")
        P.wait("vector", tden)
        tvd = P.op("vector", lambda e: e.tensor_copy(SR[0:1, 0:512], PROW[0:1, :]), inc=("vS", 1))
        bank_done("PROW", tvd)
        P.wait("tensor", tvd)
        for qc in range(4):
            bank_wait("PT")
            tt2 = P.op("tensor", lambda e, qc=qc: e.transpose(
                PT[:, 0:1], SR[0:1, qc * 128:(qc + 1) * 128], IDNS[0:1, 0:1]), inc=("tS", 1))
            P.wait("vector", tt2)
            tvt = VF(lambda e, qc=qc: e.tensor_copy(SM[:, 850 + qc:851 + qc], PT[:, 0:1]))
            bank_done("PT", tvt)
        VF(lambda e: e.reciprocal(SM[:, 850:854], SM[:, 850:854]))
        for qc in range(4):
            P.op("vector", lambda e, qc=qc: e.tensor_scalar(
                WK[:, 8 * SLOT + qc * 1024:8 * SLOT + (qc + 1) * 1024],
                WK[:, 8 * SLOT + qc * 1024:8 * SLOT + (qc + 1) * 1024],
                SM[:, 850 + qc:851 + qc], None, ALU.mult))
        tvf = P.op("vector", lambda e: e.engine_nop(), inc=("vS", 1))
        tl = None
        for ci in range(NCH):
            tl = dma("sync", slot(0, ci * TT, (ci + 1) * TT),
                     RRES[ci * 128:(ci + 1) * 128, q0:q0 + TT],
                     pre=[P.tok("vS"), P.tok("dG")])
        P.wait("tensor", tvf)
        P.wait("vector", tl)
        for qc in range(4):
            for cc in range(NCH):
                bank_wait("PT")
                tt2 = P.op("tensor", lambda e, qc=qc, cc=cc: e.transpose(
                    PT[:],
                    WK[:, 8 * SLOT + qc * 1024 + cc * 128:8 * SLOT + qc * 1024 + (cc + 1) * 128],
                    IDNS[:]), inc=("tS", 1))
                P.wait("vector", tt2)
                tva = P.op("vector", lambda e, qc=qc, cc=cc: e.tensor_tensor(
                    slot(0, cc * TT + qc * 128, cc * TT + (qc + 1) * 128),
                    slot(0, cc * TT + qc * 128, cc * TT + (qc + 1) * 128),
                    PT[:], ALU.add), inc=("vS", 1))
                bank_done("PT", tva)
        tvo = P.tok("vS")
        for ci in range(NCH):
            dma("gpsimd", RRES[ci * 128:(ci + 1) * 128, q0:q0 + TT],
                slot(0, ci * TT, (ci + 1) * TT), pre=tvo)
        P.barrier()

    snap("rres5", RRES, [C, TO])
    for tt in range(NTO):
        c0 = tt * TT
        ln_tile(tt, 1)
        for ci in range(NCH):
            sz = slot(1, ci * TT, (ci + 1) * TT)
            bh = slotb(2, ci * TT, (ci + 1) * TT)
            bl = slotb(2, SLOT + ci * TT, SLOT + (ci + 1) * TT)
            P.op("vector", lambda e, o=bh, i=sz: e.tensor_copy(o, i))
            P.op("vector", lambda e, a=sz, i=bh: e.tensor_tensor(a, a, i, ALU.subtract))
            P.op("vector", lambda e, o=bl, i=sz: e.tensor_copy(o, i))
        tvx = P.op("vector", lambda e: e.engine_nop(), inc=("vS", 1))
        P.wait("tensor", tvx)

        def glayer(nkc, nout, wsrc, xh_fn, xl_fn, outh_fn, outl_fn):
            P.wait("tensor", P.tok("vS"))
            for oc in range(nout):
                buf = oc % 2
                tl2 = dma("scalar", strb(buf * 4096, buf * 4096 + nkc * 128),
                          wsrc[oc], pre=LG.get(("gl", buf)))
                P.wait("tensor", tl2)
                pb = PS[2 + oc % 2]
                bank_wait(2 + oc % 2)
                nmm = 0
                ntot = nkc * 2
                for kc in range(nkc):
                    wb_ = strb(buf * 4096 + kc * 128, buf * 4096 + (kc + 1) * 128)
                    for b_ in (xh_fn(kc), xl_fn(kc)):
                        P.op("tensor", lambda e, a=wb_, b=b_, p=pb, st2=(nmm == 0),
                             sp2=(nmm == ntot - 1):
                             e.matmul(p[:], a, b, start=st2, stop=sp2),
                             inc=("tS", 1) if nmm == ntot - 1 else None)
                        nmm += 1
                tmm = P.tok("tS")
                LG[("gl", buf)] = tmm
                P.wait("vector", tmm)
                if outh_fn is not None:
                    sq = slot(1, 0, TT)
                    P.op("vector", lambda e, o=sq, p=pb: e.tensor_scalar(o, p[:], 0.0, None,
                                                                         ALU.max))
                    P.op("vector", lambda e, o=outh_fn(oc), i=sq: e.tensor_copy(o, i))
                    P.op("vector", lambda e, o=sq, i=outh_fn(oc):
                         e.tensor_tensor(o, o, i, ALU.subtract))
                    tvl = P.op("vector", lambda e, o=outl_fn(oc), i=sq: e.tensor_copy(o, i),
                               inc=("vS", 1))
                    bank_done(2 + oc % 2, tvl)
                else:
                    out_ = slot(1, 0, TT)
                    P.op("vector", lambda e, o=out_, p=pb:
                         e.tensor_scalar(o, p[:], vg("csbb"), None, ALU.mult))
                    tvo2 = P.op("vector", lambda e, o=out_, x=slot(0, oc * TT, (oc + 1) * TT):
                                e.tensor_tensor(o, o, x, ALU.add), inc=("vS", 1))
                    bank_done(2 + oc % 2, tvo2)
                    P.wait("tensor", tvo2)
                    for tb in range(4):
                        bank_wait("PT")
                        tt2 = P.op("tensor", lambda e, tb=tb: e.transpose(
                            PT[:], out_[:, tb * 128:(tb + 1) * 128], IDNS[:]), inc=("tS", 1))
                        P.wait("vector", tt2)
                        P.wait("vector", P.tok("dG") if tb == 0 and oc == 0 else None)
                        tvt = P.op("vector", lambda e, tb=tb: e.tensor_copy(
                            slot(1, TT + tb * 128, TT + (tb + 1) * 128), PT[:]), inc=("vS", 1))
                        bank_done("PT", tvt)
                        P.wait("tensor", tvt)
                        dma("gpsimd", OUT[c0 + tb * 128:c0 + (tb + 1) * 128,
                                          oc * 128:(oc + 1) * 128],
                            slot(1, TT + tb * 128, TT + (tb + 1) * 128), pre=tvt)

        glayer(NCH, 32, D0B,
               lambda kc: slotb(2, kc * TT, (kc + 1) * TT),
               lambda kc: slotb(2, SLOT + kc * TT, SLOT + (kc + 1) * TT),
               lambda oc: WK[:, 3 * SLOT:5 * SLOT].bitcast(BF16)[:, oc * TT:(oc + 1) * TT],
               lambda oc: WK[:, 5 * SLOT:7 * SLOT].bitcast(BF16)[:, oc * TT:(oc + 1) * TT])
        glayer(32, 32, W1BB,
               lambda kc: WK[:, 3 * SLOT:5 * SLOT].bitcast(BF16)[:, kc * TT:(kc + 1) * TT],
               lambda kc: WK[:, 5 * SLOT:7 * SLOT].bitcast(BF16)[:, kc * TT:(kc + 1) * TT],
               lambda oc: WK[:, 7 * SLOT:9 * SLOT].bitcast(BF16)[:, oc * TT:(oc + 1) * TT],
               lambda oc: (WK[:, 9 * SLOT:10 * SLOT].bitcast(BF16)[:, oc * TT:(oc + 1) * TT]
                           if oc < 16 else
                           WK[:, 2 * SLOT:3 * SLOT].bitcast(BF16)[:, (oc - 16) * TT:(oc - 15) * TT]))
        glayer(32, 8, W2BB,
               lambda kc: WK[:, 7 * SLOT:9 * SLOT].bitcast(BF16)[:, kc * TT:(kc + 1) * TT],
               lambda kc: (WK[:, 9 * SLOT:10 * SLOT].bitcast(BF16)[:, kc * TT:(kc + 1) * TT]
                           if kc < 16 else
                           WK[:, 2 * SLOT:3 * SLOT].bitcast(BF16)[:, (kc - 16) * TT:(kc - 15) * TT]),
               None, None)
        P.barrier()

    for name, (dd, src) in DBG.items():
        dma("gpsimd", dd, src)

    P.wait("sync", P.tok("dG"))
    P.wait("sync", P.tok("dS"))
    P.wait("sync", P.tok("dA"))

    block = E(nc.Block())
    P.run(block)
    st.close()
    return nc


def kernel(**inputs):
    x = np.asarray(inputs["L0_attn_x"], np.float32)
    w_tm = np.ascontiguousarray(np.asarray(inputs["L0_attn_W_tm"], np.float32))
    td = np.asarray(inputs["L0_attn_time_decay"], np.float32)
    tf = np.asarray(inputs["L0_attn_time_first"], np.float32)
    g1 = np.asarray(inputs["L0_attn_g1"], np.float32)
    b1 = np.asarray(inputs["L0_attn_b1"], np.float32)
    tmk = np.asarray(inputs["L0_attn_tmk"], np.float32).reshape(-1)
    tmv = np.asarray(inputs["L0_attn_tmv"], np.float32).reshape(-1)
    tmr = np.asarray(inputs["L0_attn_tmr"], np.float32).reshape(-1)
    wcm = np.ascontiguousarray(np.asarray(inputs["L0_attn_W_cm"], np.float32))
    wcp = np.ascontiguousarray(np.asarray(inputs["L0_attn_W_cp"], np.float32))
    g2 = np.asarray(inputs["L0_attn_g2"], np.float32)
    b2 = np.asarray(inputs["L0_attn_b2"], np.float32)
    cmk = np.asarray(inputs["L0_attn_cmk"], np.float32).reshape(-1)
    cmr = np.asarray(inputs["L0_attn_cmr"], np.float32).reshape(-1)
    l0w = np.asarray(inputs["L0_ffn_weight"], np.float32)
    l0b = np.asarray(inputs["L0_ffn_bias"], np.float32)
    wq = np.asarray(inputs["L1_attn_Wq"], np.float32)
    bq = np.asarray(inputs["L1_attn_bq"], np.float32)
    wk = np.asarray(inputs["L1_attn_Wk"], np.float32)
    bk = np.asarray(inputs["L1_attn_bk"], np.float32)
    wv = np.asarray(inputs["L1_attn_Wv"], np.float32)
    bv = np.asarray(inputs["L1_attn_bv"], np.float32)
    k0 = np.asarray(inputs["L1_ffn_k0"]).astype(np.uint8)
    k1 = np.ascontiguousarray(np.asarray(inputs["L1_ffn_k1"]).astype(np.uint8))
    k2 = np.ascontiguousarray(np.asarray(inputs["L1_ffn_k2"]).astype(np.uint8))
    s0 = np.float32(np.asarray(inputs["L1_ffn_s0"]))
    s1 = np.float32(np.asarray(inputs["L1_ffn_s1"]))
    s2 = np.float32(np.asarray(inputs["L1_ffn_s2"]))

    iotag = np.broadcast_to(np.arange(L, dtype=np.float32), (128, L)).copy()
    idn = np.eye(128, dtype=np.float32)
    idnb = np.eye(128, dtype=ml_dtypes.bfloat16)
    eye_mask = (1.0 - np.eye(128)).astype(np.float32)
    ones_mask = np.ones((128, 128), np.float32)
    scl = np.array([[s0, s1, s2, 0.0]], np.float32)

    in_maps = []
    for c in range(8):
        b = c // 2
        half = c % 2
        perm = np.arange(C) if half == 0 else np.r_[512:1024, 0:512]
        xb = np.ascontiguousarray(x[b].T)
        d = {
            "x0f": np.ascontiguousarray(xb[perm]),
            "x0own": np.ascontiguousarray(xb[:, half * TO:(half + 1) * TO]),
            "wtmc": np.ascontiguousarray(np.concatenate(
                [w_tm[perm][:, half * HC:(half + 1) * HC],
                 w_tm[perm][:, C + half * HC:C + (half + 1) * HC],
                 w_tm[perm][:, 2 * C + half * HC:2 * C + (half + 1) * HC]], axis=1)),
            "vecw": np.ascontiguousarray(np.stack(
                [td[half * HC:(half + 1) * HC], tf[half * HC:(half + 1) * HC],
                 tmk[half * HC:(half + 1) * HC], tmv[half * HC:(half + 1) * HC],
                 tmr[half * HC:(half + 1) * HC]], axis=1)),
            "vecp": np.ascontiguousarray(np.stack([g1[perm], b1[perm]], axis=1)),
            "vecn": np.ascontiguousarray(np.stack([g2, b2, cmk, cmr, l0b, bq, bk, bv], axis=1)),
            "scl": scl,
            "wcm": wcm,
            "wcp": wcp,
            "l0wt": np.ascontiguousarray(l0w.T),
            "wqt": np.ascontiguousarray(wq.T),
            "wkt": np.ascontiguousarray(wk.T),
            "wvt": np.ascontiguousarray(wv.T),
            "k0a": np.ascontiguousarray(k0[:C]),
            "k0b": np.ascontiguousarray(k0[C:]),
            "k1u": k1,
            "k2u": k2,
            "iotag": iotag,
            "idn": idn,
            "idnb": idnb,
            "mske": eye_mask if half == 0 else ones_mask,
            "msko": ones_mask if half == 0 else eye_mask,
            "sel": np.full((128, 1), float(half), np.float32),
        }
        in_maps.append(d)

    nc = build(taps=tuple(TAPS))
    res = run_bass_kernel_spmd(nc, in_maps, core_ids=list(range(8)))
    out = np.empty((4, T, C), np.float32)
    for c in range(8):
        b = c // 2
        half = c % 2
        out[b, half * TO:(half + 1) * TO, :] = res.results[c]["out"]
    if TAPS:
        kernel.dbg = [{k: v for k, v in r.items() if k.startswith("dbg_")}
                      for r in res.results]
    return out


# version 08
# speedup vs baseline: 1.0221x; 1.0221x over previous
import os
os.environ.setdefault("TRN_ALLOW_ICI_ALL_TO_ALL", "1")
import numpy as np
import ml_dtypes
import concourse.bass as bass
import concourse.mybir as mybir
from concourse.bass_utils import run_bass_kernel_spmd
from contextlib import ExitStack

F32 = mybir.dt.float32
BF16 = mybir.dt.bfloat16
U8 = mybir.dt.uint8
I32 = mybir.dt.int32
AF = mybir.ActivationFunctionType
ALU = mybir.AluOpType

C = 1024
T = 8192
TO = 4096
HC = 512
NCH = 8
HCH = 4
L = 64
SG = 4096
NSEG = 2
TT = 512
NTT = T // TT
NTO = TO // TT
NEG = -1e38
EPS = 1e-8
LN_EPS = 1e-5
AX = None

TAPS = []


class Prog:
    def __init__(self, nc):
        self.nc = nc
        self.ops = {e: [] for e in ("sync", "gpsimd", "vector", "scalar", "tensor")}
        self.cnt = {}
        self.hw = {e: {} for e in self.ops}
        self.sems = {}

    def sem(self, stack, name):
        h = stack.enter_context(self.nc.semaphore(name))
        self.sems[name] = h
        self.cnt[name] = 0
        return h

    def bump(self, sem_name, k):
        self.cnt[sem_name] += k
        return (sem_name, self.cnt[sem_name])

    def tok(self, sem_name):
        return (sem_name, self.cnt[sem_name])

    def wait(self, eng, tokv):
        if tokv is None:
            return
        sem_name, val = tokv
        if val <= 0:
            return
        if self.hw[eng].get(sem_name, -1) >= val:
            return
        self.hw[eng][sem_name] = val
        h = self.sems[sem_name]
        self.ops[eng].append(lambda e, h=h, v=val: e.wait_ge(h, v))

    def op(self, eng, fn, inc=None):
        if inc is None:
            self.ops[eng].append(fn)
            return None
        sem_name, k = inc
        h = self.sems[sem_name]
        self.ops[eng].append(lambda e, fn=fn, h=h, k=k: fn(e).then_inc(h, k))
        return self.bump(sem_name, k)

    def retro_inc(self, eng, sem_name, k=1):
        old = self.ops[eng].pop()
        h = self.sems[sem_name]
        self.ops[eng].append(lambda e, f=old, h=h, k=k: f(e).then_inc(h, k))
        return self.bump(sem_name, k)

    def barrier(self):
        toks = [(s, self.cnt[s]) for s in self.cnt]
        for eng in self.ops:
            for t in toks:
                self.wait(eng, t)

    def run(self, block):
        for eng in ("sync", "gpsimd", "vector", "scalar", "tensor"):
            ops = self.ops[eng]
            dec = getattr(block, eng)

            def make(ops):
                def body(e):
                    for fn in ops:
                        fn(e)
                return body
            dec(make(ops))


def build(taps=()):
    global AX
    AX = mybir.AxisListType.X
    nc = bass.Bass("TRN2", target_bir_lowering=False)

    def din(name, shape, dtype=F32):
        return nc.dram_tensor(name, shape, dtype, kind="ExternalInput").ap()

    def dout(name, shape, dtype=F32):
        return nc.dram_tensor(name, shape, dtype, kind="ExternalOutput").ap()

    def dtmp(name, shape, dtype=F32):
        return nc.dram_tensor(name, shape, dtype).ap()

    X0F = din("x0f", [C, T])
    X0O = din("x0own", [C, TO])
    WTMC = din("wtmc", [C, 3 * HC])
    VECW = din("vecw", [HC, 5])
    VECP = din("vecp", [C, 2])
    VECN = din("vecn", [C, 8])
    SCL = din("scl", [1, 4])
    WCM = din("wcm", [C, 4 * C])
    WCP = din("wcp", [4 * C, C])
    L0WT = din("l0wt", [C, C])
    WQT = din("wqt", [C, C])
    WKT = din("wkt", [C, C])
    WVT = din("wvt", [C, C])
    K0A = din("k0a", [C, 4 * C], U8)
    K0B = din("k0b", [C, 4 * C], U8)
    K1U = din("k1u", [4 * C, 4 * C], U8)
    K2U = din("k2u", [4 * C, C], U8)
    IOTAG = din("iotag", [128, L])
    IDN = din("idn", [128, 128])
    IDNB = din("idnb", [128, 128], BF16)
    MSKE = din("mske", [128, 128])
    MSKO = din("msko", [128, 128])
    SELI = din("sel", [128, 1])

    OUT = dout("out", [TO, C])

    ZOWN = dtmp("zown", [HC, T + 1])
    Z2H = dtmp("z2h", [C, T], BF16)
    Z2L = dtmp("z2l", [C, T], BF16)
    X1IN = dtmp("x1in", [HC, T])
    X1OUT = dtmp("x1out", [C, T])
    RRES = dtmp("rres", [C, TO])
    X4S = dtmp("x4s", [C, TO])
    ARI = dtmp("ari", [1, 8])
    ARO = dtmp("aro", [1, 8])
    WCMHB = dtmp("wcmhb", [32, 128, NCH * 128], BF16)
    WCMLB = dtmp("wcmlb", [32, 128, NCH * 128], BF16)
    WCPHB = dtmp("wcphb", [NCH, 128, 32 * 128], BF16)
    WCPLB = dtmp("wcplb", [NCH, 128, 32 * 128], BF16)
    QWTB = dtmp("qwtb", [NCH, 128, NCH * 128], BF16)
    PRJ = {}
    for nm in ("wq", "wk", "wv"):
        PRJ[nm + "h"] = dtmp(nm + "hb", [NCH, 128, NCH * 128], BF16)
        PRJ[nm + "l"] = dtmp(nm + "lb", [NCH, 128, NCH * 128], BF16)
    D0B = dtmp("d0b", [32, 128, NCH * 128], BF16)
    W1BB = dtmp("w1bb", [32, 128, 32 * 128], BF16)
    W2BB = dtmp("w2bb", [NCH, 128, 32 * 128], BF16)
    XCH = dtmp("xch", [2 * C, TO], BF16)
    XCG = dtmp("xcg", [4 * C, TO], BF16)
    QSH = dtmp("qsh", [C, TO], BF16)
    QSL = dtmp("qsl", [C, TO], BF16)
    KSH = dtmp("ksh", [C, T], BF16)
    KSL = dtmp("ksl", [C, T], BF16)
    VTH = dtmp("vth", [T, C], BF16)
    VTL = dtmp("vtl", [T, C], BF16)

    DBG = {}

    st = ExitStack()
    E = st.enter_context

    SLOT = 4096
    NSLOT = 10
    WK = E(nc.sbuf_tensor("wk", [128, NSLOT * SLOT], F32))
    ONES0 = E(nc.sbuf_tensor("ones0", [128, SG], BF16))
    VSB = E(nc.sbuf_tensor("vsb", [128, 192], F32))
    IOT = E(nc.sbuf_tensor("iot", [128, L], F32))
    IDNS = E(nc.sbuf_tensor("idns", [128, 128], F32))
    IDNBS = E(nc.sbuf_tensor("idnbs", [128, 128], BF16))
    MSKES = E(nc.sbuf_tensor("mskes", [128, 128], F32))
    MSKOS = E(nc.sbuf_tensor("mskos", [128, 128], F32))
    SELS = E(nc.sbuf_tensor("sels", [128, 1], F32))
    ONEB = E(nc.sbuf_tensor("oneb", [128, 1], BF16))
    ONER = E(nc.sbuf_tensor("oner", [1, 128], F32))
    SCLS = E(nc.sbuf_tensor("scls", [1, 4], F32))
    SM = E(nc.sbuf_tensor("smx", [128, 1024], F32))
    SR = E(nc.sbuf_tensor("srx", [1, 1024], F32))
    WTB = E(nc.sbuf_tensor("wtb", [128, 6 * 1024], BF16))
    WTS = None
    STR = E(nc.sbuf_tensor("strm", [128, 4096], F32))
    WTS = STR
    PS = [E(nc.psum_tensor("ps%d" % i, [128, 512], F32)) for i in range(4)]
    PB0 = E(nc.psum_tensor("pb0", [128, 512], F32))
    PB1 = E(nc.psum_tensor("pb1", [128, 512], F32))
    PROW = E(nc.psum_tensor("prow", [1, 512], F32))
    PT = E(nc.psum_tensor("ptx", [128, 128], F32))
    PTB = PT[:].bitcast(BF16)[:, 0:128]
    PTBS = [PTB,
            PB0[:].bitcast(BF16)[:, 0:128],
            PB1[:].bitcast(BF16)[:, 0:128]]

    P = Prog(nc)
    for s_ in ("dS", "dG", "dA", "tS", "vS", "sS", "cS"):
        P.sem(st, s_)

    def slot(i, lo=0, hi=SLOT):
        return WK[:, i * SLOT + lo: i * SLOT + hi]

    def slotb(i, lo, hi):
        return WK[:, i * SLOT: (i + 1) * SLOT].bitcast(BF16)[:, lo:hi]

    def sloti(i, lo, hi):
        return WK[:, i * SLOT + lo: i * SLOT + hi].bitcast(I32)

    def strb(lo, hi):
        return STR.bitcast(BF16)[:, lo:hi]

    def stru(lo, hi):
        return STR.bitcast(U8)[:, lo:hi]

    def dma(q, out, in_, pre=None, ncok=False):
        sem = {"sync": "dS", "gpsimd": "dG", "scalar": "dA"}[q]
        if pre is not None:
            for t_ in (pre if isinstance(pre, list) else [pre]):
                P.wait(q, t_)
        if ncok:
            return P.op(q, lambda e, o=out, i=in_: e.dma_start(
                out=o, in_=i, allow_slow_non_contiguous=True), inc=(sem, 16))
        return P.op(q, lambda e, o=out, i=in_: e.dma_start(out=o, in_=i), inc=(sem, 16))

    def snap(name, src, shape, dtype=F32):
        if name not in taps:
            return
        dd = dout("dbg_" + name, shape, dtype)
        P.op("vector", lambda e: e.engine_nop(), inc=("vS", 1))
        P.op("scalar", lambda e: e.activation(SM[:, 815:816], SM[:, 815:816], AF.Square),
             inc=("sS", 1))
        P.barrier()
        dma("gpsimd", dd, src)
        P.barrier()

    bank_ev = {}
    LG = {}

    def bank_wait(b):
        P.wait("tensor", bank_ev.get(b))

    def bank_done(b, tok):
        bank_ev[b] = tok

    def VF(fn):
        t_ = P.op("vector", fn, inc=("vS", 1))
        P.wait("vector", t_)
        return t_

    VS = {}
    _vn = [0]

    def vreg(name, n):
        VS[name] = (_vn[0], n)
        _vn[0] += n

    for nm in ("w", "u", "tmk", "tmv", "tmr", "om_tmk", "om_tmv", "om_tmr", "ew", "w64"):
        vreg(nm, HCH)
    for nm in ("g1", "b1"):
        vreg(nm, NCH)
    for nm in ("g2", "b2", "cmk", "cmr", "om_cmk", "om_cmr", "l0b", "bq", "bk", "bv"):
        vreg(nm, NCH)
    for nm in ("one", "qmaxc", "kmaxc", "mbc", "wsc", "iscale", "wsis", "csb",
               "negmb", "invb", "wsisb", "csbb", "thb", "thnb", "epsc"):
        vreg(nm, 1)
    assert _vn[0] <= 192

    def vg(name, ch=0):
        b, n = VS[name]
        assert ch < n, name
        return VSB[:, b + ch: b + ch + 1]

    def vc(name):
        b, n = VS[name]
        return VSB[0:1, b: b + 1]

    for ap_, src in ((IOT[:], IOTAG), (IDNS[:], IDN), (IDNBS[:], IDNB),
                     (MSKES[:], MSKE), (MSKOS[:], MSKO), (SELS[:], SELI), (SCLS[:], SCL)):
        dma("sync", ap_, src)
    for ci in range(HCH):
        for vi, nm in enumerate(("w", "u", "tmk", "tmv", "tmr")):
            dma("sync", vg(nm, ci), VECW[ci * 128:(ci + 1) * 128, vi:vi + 1], ncok=True)
    for ci in range(NCH):
        for vi, nm in enumerate(("g1", "b1")):
            dma("sync", vg(nm, ci), VECP[ci * 128:(ci + 1) * 128, vi:vi + 1], ncok=True)
        for vi, nm in enumerate(("g2", "b2", "cmk", "cmr", "l0b", "bq", "bk", "bv")):
            dma("sync", vg(nm, ci), VECN[ci * 128:(ci + 1) * 128, vi:vi + 1], ncok=True)
    t_all = P.tok("dS")
    P.wait("vector", t_all)
    P.wait("scalar", t_all)
    P.op("vector", lambda e: e.memset(vg("one"), 1.0))
    P.op("vector", lambda e: e.memset(vg("epsc"), LN_EPS))
    P.op("vector", lambda e: e.memset(SM[:, 815:816], 0.0))
    P.op("vector", lambda e: e.memset(ONEB[:], 1.0))
    P.op("vector", lambda e: e.memset(ONER[:], 1.0))
    P.op("vector", lambda e: e.memset(ONES0[:], 1.0))
    P.op("vector", lambda e: e.memset(ONES0[:, 0:SG:L], 0.0))
    for nm in ("tmk", "tmv", "tmr"):
        for ci in range(HCH):
            P.op("vector", lambda e, o=vg("om_" + nm, ci), i=vg(nm, ci):
                 e.tensor_scalar(o, i, -1.0, 1.0, ALU.mult, ALU.add))
    for nm in ("cmk", "cmr"):
        for ci in range(NCH):
            P.op("vector", lambda e, o=vg("om_" + nm, ci), i=vg(nm, ci):
                 e.tensor_scalar(o, i, -1.0, 1.0, ALU.mult, ALU.add))
    for ci in range(HCH):
        P.op("scalar", lambda e, o=vg("ew", ci), i=vg("w", ci): e.activation(o, i, AF.Exp),
             inc=("sS", 1))
        P.op("vector", lambda e, o=vg("w64", ci), i=vg("w", ci):
             e.tensor_scalar(o, i, float(L), None, ALU.mult))
    VF(lambda e: e.tensor_tensor(vc("csb"), SCLS[0:1, 0:1], SCLS[0:1, 1:2], ALU.mult))
    VF(lambda e: e.tensor_tensor(vc("csb"), vc("csb"), SCLS[0:1, 2:3], ALU.mult))
    tv0 = VF(lambda e: e.tensor_scalar(vc("csb"), vc("csb"), 8.0, None, ALU.mult))

    def bcast_cell(src_cell, dst_col, pre_tok):
        P.wait("tensor", pre_tok)
        bank_wait("PT")
        tt_ = P.op("tensor", lambda e, s=src_cell: e.matmul(PT[:, 0:1], ONER[:], s,
                                                            start=True, stop=True),
                   inc=("tS", 1))
        P.wait("vector", tt_)
        tok = P.op("vector", lambda e, d=dst_col: e.tensor_copy(d, PT[:, 0:1]), inc=("vS", 1))
        bank_done("PT", tok)
        return tok

    t_csbb = bcast_cell(vc("csb"), vg("csbb"), tv0)
    P.barrier()

    def stats_pass(src_fn, n_sl, sq_fn, ready_tok):
        for ci in range(n_sl):
            P.wait("tensor", ready_tok)
            if ci == 0:
                bank_wait("PROW")
            P.op("tensor", lambda e, r=src_fn(ci), st_=(ci == 0), sp=(ci == n_sl - 1):
                 e.matmul(PROW[0:1, :], vg("one"), r, start=st_, stop=sp),
                 inc=("tS", 1) if ci == n_sl - 1 else None)
        tsum = P.tok("tS")
        P.wait("vector", tsum)
        tv = P.op("vector", lambda e: e.tensor_scalar(SR[0:1, 0:512], PROW[0:1, :],
                                                      1.0 / C, None, ALU.mult), inc=("vS", 1))
        fin = None
        for ci in range(n_sl):
            P.wait("scalar", ready_tok)
            tsq = P.op("scalar", lambda e, o=sq_fn(ci), i=src_fn(ci):
                       e.activation(o, i, AF.Square), inc=("sS", 1))
            P.wait("tensor", tsq)
            if ci == 0:
                P.wait("tensor", tv)
            fin = P.op("tensor", lambda e, r=sq_fn(ci), st_=(ci == 0), sp=(ci == n_sl - 1):
                       e.matmul(PROW[0:1, :], vg("one"), r, start=st_, stop=sp),
                       inc=("tS", 1) if ci == n_sl - 1 else None)
        return fin

    def stats_finalize(tok):
        P.wait("vector", tok)
        P.op("vector", lambda e: e.tensor_scalar(SM[0:1, 0:512], PROW[0:1, :],
                                                 1.0 / C, None, ALU.mult))
        P.op("vector", lambda e: e.tensor_tensor(SR[0:1, 512:1024], SR[0:1, 0:512],
                                                 SR[0:1, 0:512], ALU.mult))
        tv = P.op("vector", lambda e: e.tensor_tensor(SM[0:1, 0:512], SM[0:1, 0:512],
                                                      SR[0:1, 512:1024], ALU.subtract),
                  inc=("vS", 1))
        tv = P.op("vector", lambda e: e.tensor_scalar(SM[0:1, 0:512], SM[0:1, 0:512],
                                                      LN_EPS, None, ALU.add), inc=("vS", 1))
        bank_done("PROW", tv)
        P.wait("scalar", tv)
        ts_ = P.op("scalar", lambda e: e.activation(SM[0:1, 0:512], SM[0:1, 0:512], AF.Sqrt),
                   inc=("sS", 1))
        P.wait("vector", ts_)
        return P.op("vector", lambda e: e.reciprocal(SR[0:1, 512:1024], SM[0:1, 0:512]),
                    inc=("vS", 1))

    def stats_bcast(tok):
        P.wait("tensor", tok)
        bank_wait("PB0")
        bank_wait("PB1")
        bank_wait(("PTB", 1))
        bank_wait(("PTB", 2))
        P.op("tensor", lambda e: e.matmul(PB0[:], ONER[:], SR[0:1, 0:512], start=True, stop=True))
        return P.op("tensor", lambda e: e.matmul(PB1[:], ONER[:], SR[0:1, 512:1024],
                                                 start=True, stop=True), inc=("tS", 1))

    for tt in range(NTT):
        c0 = tt * TT
        P.wait("vector", P.tok("dG"))
        tl = None
        for ci in range(NCH):
            tl = dma("sync", slot(0, ci * TT, (ci + 1) * TT),
                     X0F[ci * 128:(ci + 1) * 128, c0:c0 + TT], pre=P.tok("vS"))
        tok = stats_pass(lambda ci: slot(0, ci * TT, (ci + 1) * TT), NCH,
                         lambda ci: slot(2, ci * TT, (ci + 1) * TT), tl)
        tok = stats_finalize(tok)
        tbc = stats_bcast(tok)
        P.wait("vector", tbc)
        for ci in range(NCH):
            P.op("vector", lambda e, ci=ci: e.tensor_tensor(
                slot(1, ci * TT, (ci + 1) * TT), slot(0, ci * TT, (ci + 1) * TT),
                PB0[:], ALU.subtract))
            P.op("vector", lambda e, ci=ci: e.tensor_tensor(
                slot(1, ci * TT, (ci + 1) * TT), slot(1, ci * TT, (ci + 1) * TT),
                PB1[:], ALU.mult))
        tz = P.op("vector", lambda e: e.engine_nop(), inc=("vS", 1))
        bank_done("PB0", tz)
        bank_done("PB1", tz)
        for ci in range(HCH):
            dma("gpsimd", ZOWN[ci * 128:(ci + 1) * 128, c0 + 1:c0 + TT + 1],
                slot(1, ci * TT, (ci + 1) * TT), pre=tz)
            if tt == 0:
                dma("gpsimd", ZOWN[ci * 128:(ci + 1) * 128, 0:1],
                    slot(1, ci * TT, ci * TT + 1), pre=tz, ncok=True)
        tok = stats_pass(lambda ci: slot(1, ci * TT, (ci + 1) * TT), NCH,
                         lambda ci: slot(2, ci * TT, (ci + 1) * TT), tz)
        tok = stats_finalize(tok)
        tbc = stats_bcast(tok)
        P.wait("vector", tbc)
        tvlast = None
        for ci in range(NCH):
            sz = slot(1, ci * TT, (ci + 1) * TT)
            s2_ = slot(2, ci * TT, (ci + 1) * TT)
            P.op("vector", lambda e, a=s2_, b=sz: e.tensor_tensor(a, b, PB0[:], ALU.subtract))
            P.op("vector", lambda e, a=s2_: e.tensor_tensor(a, a, PB1[:], ALU.mult))
            P.op("vector", lambda e, a=s2_, g=vg("g1", ci), b=vg("b1", ci):
                 e.tensor_scalar(a, a, g, b, ALU.mult, ALU.add))
            bh = slotb(3, ci * TT, (ci + 1) * TT)
            bl = slotb(3, SLOT + ci * TT, SLOT + (ci + 1) * TT)
            P.op("vector", lambda e, o=bh, i=s2_: e.tensor_copy(o, i))
            P.op("vector", lambda e, o=s2_, i=bh: e.tensor_tensor(o, o, i, ALU.subtract))
            tvlast = P.op("vector", lambda e, o=bl, i=s2_: e.tensor_copy(o, i), inc=("vS", 1))
            dma("gpsimd", Z2H[ci * 128:(ci + 1) * 128, c0:c0 + TT], bh, pre=tvlast)
            dma("gpsimd", Z2L[ci * 128:(ci + 1) * 128, c0:c0 + TT], bl, pre=tvlast)
        bank_done("PB0", tvlast)
        bank_done("PB1", tvlast)
    P.barrier()
    snap("z2h", Z2H, [C, T], BF16)
    snap("zown", ZOWN, [HC, T + 1])

    def wkv_chain(ci, s):
        dbg0 = (ci == 0 and s == 0)

        def csnap(name, src, shape, dtype=F32):
            if dbg0:
                snap(name, src, shape, dtype)

        c0_ = ci * 4
        CA = SM[:, c0_ + 0:c0_ + 1]
        CB = SM[:, c0_ + 1:c0_ + 2]
        CAo = SM[:, c0_ + 2:c0_ + 3]
        CBo = SM[:, c0_ + 3:c0_ + 4]
        smb = 16
        CM = SM[:, smb:smb + L]
        D0C = SM[:, smb + L:smb + 2 * L]
        D1C = SM[:, smb + 2 * L:smb + 3 * L]
        DH = SM[:, smb + 3 * L:smb + 4 * L]
        DC = SM[:, smb + 4 * L:smb + 5 * L]
        MC = SM[:, smb + 5 * L:smb + 6 * L]
        PG = SM[:, smb + 6 * L:smb + 7 * L]
        RE = SM[:, smb + 7 * L:smb + 8 * L]
        RC = SM[:, smb + 8 * L:smb + 9 * L]
        AC = SM[:, smb + 9 * L:smb + 10 * L]
        DT = SM[:, smb + 10 * L:smb + 11 * L]
        T1 = SM[:, smb + 11 * L:smb + 12 * L]
        U5 = SM[:, smb + 12 * L:smb + 12 * L + 1]
        k = slot(0, 0, SG); v = slot(1, 0, SG); r = slot(2, 0, SG)
        zx = slot(3, 0, SG)
        w_, u_, w64 = vg("w", ci), vg("u", ci), vg("w64", ci)

        def V(fn, inc=None):
            return P.op("vector", fn, inc=inc)

        def Sc(fn, inc=None):
            return P.op("scalar", fn, inc=inc)

        def vsync():
            tv_ = P.retro_inc("vector", "vS")
            P.wait("scalar", tv_)
            return tv_

        def ssync():
            ts_ = P.retro_inc("scalar", "sS")
            P.wait("vector", ts_)
            return ts_

        def SCN(fn):
            t_ = P.op("vector", fn, inc=("vS", 1))
            P.wait("vector", t_)
            return t_

        if s == 0:
            VF(lambda e: e.memset(CA, 0.0))
            VF(lambda e: e.memset(CB, NEG))
        else:
            VF(lambda e: e.tensor_copy(CA, CAo))
            VF(lambda e: e.tensor_copy(CB, CBo))
        csnap("carry0", SM[:, 0:16], [128, 16])
        vsync()
        Sc(lambda e: e.activation(slot(4, 0, SG), k, AF.Exp))
        ssync()
        V(lambda e: e.tensor_copy(slot(5, 0, SG), vg("ew", ci).broadcast_to((128, SG))))
        V(lambda e: e.memset(slot(5, 0, SG)[:, 0:SG:L], 0.0))
        SCN(lambda e: e.tensor_tensor_scan(slot(6, 0, SG), slot(5, 0, SG), slot(4, 0, SG),
                                         0.0, ALU.mult, ALU.add))
        Dloc = slot(6, 0, SG)
        V(lambda e: e.tensor_copy(DT, Dloc[:, L - 1:SG:L]))
        csnap("dloc0", Dloc, [128, SG])
        V(lambda e: e.tensor_scalar(T1, IOT[:], w64, None, ALU.mult))
        V(lambda e: e.tensor_scalar(T1, T1, 0.0, None, ALU.max))
        V(lambda e: e.tensor_scalar(CM, IOT[:], w64, w64, ALU.mult, ALU.add))
        V(lambda e: e.tensor_scalar(CM, CM, CB, None, ALU.add))
        V(lambda e: e.tensor_tensor(CM, CM, T1, ALU.max))
        V(lambda e: e.tensor_tensor(D0C[:, 1:L], CM[:, 0:L - 1], CM[:, 1:L], ALU.subtract))
        V(lambda e: e.tensor_scalar(D0C[:, 1:L], D0C[:, 1:L], w64, None, ALU.add))
        VF(lambda e: e.tensor_scalar(U5, CB, w64, None, ALU.add))
        VF(lambda e: e.tensor_tensor(U5, U5, CM[:, 0:1], ALU.subtract))
        V(lambda e: e.tensor_scalar(D1C, CM, -1.0, None, ALU.mult))
        vsync()
        Sc(lambda e: e.activation(D0C[:, 1:L], D0C[:, 1:L], AF.Exp))
        Sc(lambda e: e.activation(D1C, D1C, AF.Exp))
        Sc(lambda e: e.activation(U5, U5, AF.Exp))
        ssync()
        VF(lambda e: e.memset(D0C[:, 0:1], 0.0))
        V(lambda e: e.tensor_tensor(D1C, D1C, DT, ALU.mult))
        VF(lambda e: e.tensor_tensor(D1C[:, 0:1], D1C[:, 0:1], U5, ALU.add))
        SCN(lambda e: e.tensor_tensor_scan(DH, D0C, D1C, 0.0, ALU.mult, ALU.add))
        V(lambda e: e.tensor_copy(DC[:, 1:L], DH[:, 0:L - 1]))
        VF(lambda e: e.memset(DC[:, 0:1], 1.0))
        V(lambda e: e.tensor_copy(MC[:, 1:L], CM[:, 0:L - 1]))
        VF(lambda e: e.tensor_copy(MC[:, 0:1], CB))
        csnap("sm0", SM[:, 0:1024], [128, 1024])
        V(lambda e: e.tensor_copy(slot(5, 0, SG), w_.broadcast_to((128, SG))))
        V(lambda e: e.tensor_copy(slot(5, 0, SG)[:, 0:SG:L], MC))
        SCN(lambda e: e.tensor_tensor_scan(slot(7, 0, SG), ONES0[:], slot(5, 0, SG),
                                         0.0, ALU.mult, ALU.add))
        V(lambda e: e.memset(slot(5, 0, SG), 0.0))
        V(lambda e: e.tensor_copy(slot(5, 0, SG)[:, 0:SG:L], DC))
        csnap("dstage0", slot(5, 0, SG), [128, SG])
        csnap("dcsm0", SM[:, 0:1024], [128, 1024])
        SCN(lambda e: e.tensor_tensor_scan(slot(4, 0, SG), ONES0[:], slot(5, 0, SG),
                                         0.0, ALU.mult, ALU.add))
        V(lambda e: e.tensor_scalar(slot(5, 0, SG), k, u_, None, ALU.add))
        V(lambda e: e.tensor_tensor(slot(5, 0, SG), slot(7, 0, SG), slot(5, 0, SG), ALU.max))
        V(lambda e: e.tensor_scalar(slot(5, 0, SG), slot(5, 0, SG), 0.0, None, ALU.max))
        csnap("dcrep0", slot(4, 0, SG), [128, SG])
        csnap("arep0", slot(7, 0, SG), [128, SG])
        csnap("qq0", slot(5, 0, SG), [128, SG])
        V(lambda e: e.tensor_copy(slot(8, 0, SG)[:, 1:SG], Dloc[:, 0:SG - 1]))
        V(lambda e: e.memset(slot(8, 0, SG)[:, 0:SG:L], 0.0))
        V(lambda e: e.tensor_tensor(slot(7, 0, SG), slot(7, 0, SG), slot(5, 0, SG), ALU.subtract))
        V(lambda e: e.tensor_scalar(slot(9, 0, SG), slot(5, 0, SG), -1.0, None, ALU.mult))
        vsync()
        Sc(lambda e: e.activation(slot(7, 0, SG), slot(7, 0, SG), AF.Exp))
        Sc(lambda e: e.activation(slot(9, 0, SG), slot(9, 0, SG), AF.Exp))
        ssync()
        csnap("expaq0", slot(7, 0, SG), [128, SG])
        V(lambda e: e.tensor_tensor(slot(7, 0, SG), slot(7, 0, SG), slot(4, 0, SG), ALU.mult))
        V(lambda e: e.tensor_tensor(slot(8, 0, SG), slot(8, 0, SG), slot(9, 0, SG), ALU.mult))
        V(lambda e: e.tensor_tensor(slot(7, 0, SG), slot(7, 0, SG), slot(8, 0, SG), ALU.add))
        V(lambda e: e.tensor_scalar(slot(7, 0, SG), slot(7, 0, SG), 1.1754944e-38, None,
                                    ALU.max))
        V(lambda e: e.tensor_scalar(sloti(9, 0, SG), slot(7, 0, SG).bitcast(I32),
                                    23, None, ALU.logical_shift_right))
        V(lambda e: e.tensor_scalar(sloti(9, 0, SG), sloti(9, 0, SG), 127, None, ALU.subtract))
        V(lambda e: e.tensor_copy(slot(8, 0, SG), sloti(9, 0, SG)))
        V(lambda e: e.tensor_scalar(sloti(7, 0, SG), slot(7, 0, SG).bitcast(I32),
                                    0x007FFFFF, 0x3F800000, ALU.bitwise_and, ALU.bitwise_or))
        vsync()
        Sc(lambda e: e.activation(slot(7, 0, SG), slot(7, 0, SG), AF.Ln))
        ssync()
        V(lambda e: e.tensor_scalar(slot(8, 0, SG), slot(8, 0, SG), 0.6931471805599453, None,
                                    ALU.mult))
        V(lambda e: e.tensor_tensor(slot(7, 0, SG), slot(7, 0, SG), slot(8, 0, SG), ALU.add))
        V(lambda e: e.tensor_tensor(slot(6, 0, SG), slot(5, 0, SG), slot(7, 0, SG), ALU.add))
        bbm1 = slot(6, 0, SG)
        csnap("bbm10", bbm1, [128, SG])

        def coeffs():
            V(lambda e: e.tensor_scalar(slot(4, 0, SG), bbm1, w_, None, ALU.add))
            V(lambda e: e.tensor_tensor(slot(5, 0, SG), slot(4, 0, SG), k, ALU.max))
            V(lambda e: e.tensor_tensor(slot(4, 0, SG), slot(4, 0, SG), slot(5, 0, SG),
                                        ALU.subtract))
            V(lambda e: e.tensor_tensor(slot(5, 0, SG), k, slot(5, 0, SG), ALU.subtract))
            vsync()
            Sc(lambda e: e.activation(slot(7, 0, SG), slot(4, 0, SG), AF.Exp))
            Sc(lambda e: e.activation(slot(8, 0, SG), slot(5, 0, SG), AF.Exp))
            ssync()

        coeffs()
        V(lambda e: e.tensor_tensor(slot(9, 0, SG), slot(7, 0, SG), slot(8, 0, SG), ALU.add))
        vsync()
        Sc(lambda e: e.activation(slot(5, 0, SG), slot(9, 0, SG), AF.Ln))
        ssync()
        V(lambda e: e.reciprocal(slot(9, 0, SG), slot(9, 0, SG)))
        V(lambda e: e.tensor_tensor(slot(4, 0, SG), slot(4, 0, SG), slot(5, 0, SG), ALU.subtract))
        SCN(lambda e: e.tensor_tensor_scan(slot(5, 0, SG), ONES0[:], slot(4, 0, SG),
                                         0.0, ALU.mult, ALU.add))
        V(lambda e: e.tensor_copy(T1, slot(5, 0, SG)[:, L - 1:SG:L]))
        vsync()
        Sc(lambda e: e.activation(PG, T1, AF.Exp))
        Sc(lambda e: e.activation(slot(5, 0, SG), slot(5, 0, SG), AF.Exp))
        ssync()
        V(lambda e: e.tensor_tensor(slot(4, 0, SG), slot(7, 0, SG), slot(9, 0, SG), ALU.mult))
        V(lambda e: e.memset(slot(4, 0, SG)[:, 0:SG:L], 0.0))
        V(lambda e: e.tensor_scalar(slot(9, 0, SG), slot(9, 0, SG), EPS, None, ALU.mult))
        SCN(lambda e: e.tensor_tensor_scan(slot(7, 0, SG), slot(4, 0, SG), slot(9, 0, SG),
                                         0.0, ALU.mult, ALU.add))
        V(lambda e: e.tensor_copy(RE, slot(7, 0, SG)[:, L - 1:SG:L]))
        SCN(lambda e: e.tensor_tensor_scan(RC, PG, RE, 0.0, ALU.mult, ALU.add))
        V(lambda e: e.tensor_copy(T1[:, 1:L], RC[:, 0:L - 1]))
        VF(lambda e: e.memset(T1[:, 0:1], 0.0))
        V(lambda e: e.memset(slot(9, 0, SG), 0.0))
        V(lambda e: e.tensor_copy(slot(9, 0, SG)[:, 0:SG:L], T1))
        SCN(lambda e: e.tensor_tensor_scan(slot(4, 0, SG), ONES0[:], slot(9, 0, SG),
                                         0.0, ALU.mult, ALU.add))
        V(lambda e: e.tensor_tensor(slot(4, 0, SG), slot(4, 0, SG), slot(5, 0, SG), ALU.mult))
        V(lambda e: e.tensor_tensor(slot(4, 0, SG), slot(4, 0, SG), slot(7, 0, SG), ALU.add))
        V(lambda e: e.tensor_copy(slot(5, 0, SG)[:, 1:SG], slot(4, 0, SG)[:, 0:SG - 1]))
        V(lambda e: e.memset(slot(5, 0, SG)[:, 0:1], 0.0))
        V(lambda e: e.tensor_scalar(slot(5, 0, SG), slot(5, 0, SG), 1.0, None, ALU.add))
        vsync()
        Sc(lambda e: e.activation(slot(5, 0, SG), slot(5, 0, SG), AF.Ln))
        ssync()
        V(lambda e: e.tensor_tensor(slot(6, 0, SG), slot(6, 0, SG), slot(5, 0, SG), ALU.add))
        csnap("bbm1c0", slot(6, 0, SG), [128, SG])
        coeffs()
        SCN(lambda e: e.tensor_tensor_scan(slot(5, 0, SG), ONES0[:], slot(4, 0, SG),
                                         0.0, ALU.mult, ALU.add))
        vsync()
        Sc(lambda e: e.activation(slot(5, 0, SG), slot(5, 0, SG), AF.Exp))
        ssync()
        V(lambda e: e.tensor_copy(PG, slot(5, 0, SG)[:, L - 1:SG:L]))
        V(lambda e: e.tensor_copy(slot(9, 0, SG)[:, 1:SG], slot(5, 0, SG)[:, 0:SG - 1]))
        V(lambda e: e.memset(slot(9, 0, SG)[:, 0:SG:L], 1.0))
        V(lambda e: e.tensor_tensor(slot(5, 0, SG), slot(8, 0, SG), v, ALU.mult))
        V(lambda e: e.tensor_copy(slot(4, 0, SG), slot(7, 0, SG)))
        V(lambda e: e.memset(slot(4, 0, SG)[:, 0:SG:L], 0.0))
        SCN(lambda e: e.tensor_tensor_scan(slot(7, 0, SG), slot(4, 0, SG), slot(5, 0, SG),
                                         0.0, ALU.mult, ALU.add))
        V(lambda e: e.tensor_copy(RE, slot(7, 0, SG)[:, L - 1:SG:L]))
        VF(lambda e: e.tensor_tensor(T1[:, 0:1], CA, PG[:, 0:1], ALU.mult))
        VF(lambda e: e.tensor_tensor(RE[:, 0:1], RE[:, 0:1], T1[:, 0:1], ALU.add))
        SCN(lambda e: e.tensor_tensor_scan(AC, PG, RE, 0.0, ALU.mult, ALU.add))
        V(lambda e: e.tensor_copy(T1[:, 1:L], AC[:, 0:L - 1]))
        VF(lambda e: e.tensor_copy(T1[:, 0:1], CA))
        V(lambda e: e.memset(slot(4, 0, SG), 0.0))
        V(lambda e: e.tensor_copy(slot(4, 0, SG)[:, 0:SG:L], T1))
        SCN(lambda e: e.tensor_tensor_scan(slot(8, 0, SG), ONES0[:], slot(4, 0, SG),
                                         0.0, ALU.mult, ALU.add))
        V(lambda e: e.tensor_copy(slot(4, 0, SG)[:, 1:SG], slot(7, 0, SG)[:, 0:SG - 1]))
        V(lambda e: e.memset(slot(4, 0, SG)[:, 0:SG:L], 0.0))
        V(lambda e: e.tensor_tensor(slot(8, 0, SG), slot(8, 0, SG), slot(9, 0, SG), ALU.mult))
        V(lambda e: e.tensor_tensor(slot(8, 0, SG), slot(8, 0, SG), slot(4, 0, SG), ALU.add))
        csnap("aam10", slot(8, 0, SG), [128, SG])
        VF(lambda e: e.tensor_scalar(T1[:, 0:1], slot(6, 0, SG)[:, SG - 1:SG], w_, None, ALU.add))
        VF(lambda e: e.tensor_tensor(T1[:, 1:2], T1[:, 0:1], k[:, SG - 1:SG], ALU.max))
        VF(lambda e: e.tensor_tensor(T1[:, 0:1], T1[:, 0:1], T1[:, 1:2], ALU.subtract))
        VF(lambda e: e.tensor_tensor(T1[:, 2:3], k[:, SG - 1:SG], T1[:, 1:2], ALU.subtract))
        vsync()
        Sc(lambda e: e.activation(T1[:, 0:1], T1[:, 0:1], AF.Exp))
        Sc(lambda e: e.activation(T1[:, 2:3], T1[:, 2:3], AF.Exp))
        ssync()
        VF(lambda e: e.tensor_tensor(CAo, T1[:, 0:1], slot(8, 0, SG)[:, SG - 1:SG], ALU.mult))
        VF(lambda e: e.tensor_tensor(T1[:, 3:4], T1[:, 2:3], v[:, SG - 1:SG], ALU.mult))
        VF(lambda e: e.tensor_tensor(CAo, CAo, T1[:, 3:4], ALU.add))
        VF(lambda e: e.tensor_tensor(T1[:, 0:1], T1[:, 0:1], T1[:, 2:3], ALU.add))
        VF(lambda e: e.tensor_scalar(T1[:, 0:1], T1[:, 0:1], EPS, None, ALU.add))
        vsync()
        Sc(lambda e: e.activation(T1[:, 0:1], T1[:, 0:1], AF.Ln))
        ssync()
        VF(lambda e: e.tensor_tensor(CBo, T1[:, 1:2], T1[:, 0:1], ALU.add))
        V(lambda e: e.tensor_scalar(slot(4, 0, SG), k, u_, None, ALU.add))
        V(lambda e: e.tensor_tensor(slot(5, 0, SG), slot(6, 0, SG), slot(4, 0, SG), ALU.max))
        V(lambda e: e.tensor_tensor(slot(9, 0, SG), slot(6, 0, SG), slot(5, 0, SG), ALU.subtract))
        V(lambda e: e.tensor_tensor(slot(4, 0, SG), slot(4, 0, SG), slot(5, 0, SG), ALU.subtract))
        vsync()
        Sc(lambda e: e.activation(slot(9, 0, SG), slot(9, 0, SG), AF.Exp))
        Sc(lambda e: e.activation(slot(4, 0, SG), slot(4, 0, SG), AF.Exp))
        Sc(lambda e: e.activation(slot(2, 0, SG), r, AF.Sigmoid))
        ssync()
        V(lambda e: e.tensor_tensor(slot(5, 0, SG), slot(9, 0, SG), slot(4, 0, SG), ALU.add))
        V(lambda e: e.tensor_scalar(slot(5, 0, SG), slot(5, 0, SG), EPS, None, ALU.add))
        V(lambda e: e.reciprocal(slot(5, 0, SG), slot(5, 0, SG)))
        V(lambda e: e.tensor_tensor(slot(9, 0, SG), slot(9, 0, SG), slot(8, 0, SG), ALU.mult))
        V(lambda e: e.tensor_tensor(slot(4, 0, SG), slot(4, 0, SG), v, ALU.mult))
        V(lambda e: e.tensor_tensor(slot(9, 0, SG), slot(9, 0, SG), slot(4, 0, SG), ALU.add))
        V(lambda e: e.tensor_tensor(slot(9, 0, SG), slot(9, 0, SG), slot(5, 0, SG), ALU.mult))
        csnap("y0", slot(9, 0, SG), [128, SG])
        csnap("rsig0", slot(2, 0, SG), [128, SG])
        V(lambda e: e.tensor_tensor(slot(9, 0, SG), slot(9, 0, SG), slot(2, 0, SG), ALU.mult))
        V(lambda e: e.tensor_tensor(slot(9, 0, SG - 1), slot(9, 0, SG - 1), zx[:, 1:SG],
                                    ALU.add))
        return V(lambda e: e.tensor_tensor(slot(9, SG - 1, SG), slot(9, SG - 1, SG),
                                           SM[:, 790:791], ALU.add), inc=("vS", 1))

    for ci in range(HCH):
        tw = None
        for sub in range(3):
            for kc in range(NCH):
                tw = dma("sync", WTS[:, sub * 1024 + kc * 128:sub * 1024 + (kc + 1) * 128],
                         WTMC[kc * 128:(kc + 1) * 128,
                              sub * HC + ci * 128:sub * HC + (ci + 1) * 128],
                         pre=P.tok("vS"))
        P.wait("vector", tw)
        P.wait("vector", P.tok("tS"))
        for sub in range(3):
            src = WTS[:, sub * 1024:(sub + 1) * 1024]
            hi = WTB[:, sub * 1024:(sub + 1) * 1024]
            lo = WTB[:, (3 + sub) * 1024:(4 + sub) * 1024]
            P.op("vector", lambda e, o=hi, i=src: e.tensor_copy(o, i))
            P.op("vector", lambda e, o=src, i=hi: e.tensor_tensor(o, o, i, ALU.subtract))
            P.op("vector", lambda e, o=lo, i=src: e.tensor_copy(o, i))
        tws = P.op("vector", lambda e: e.engine_nop(), inc=("vS", 1))
        P.wait("tensor", tws)
        for s in range(NSEG):
            P.wait("vector", P.tok("dG"))
            tz = dma("scalar", slot(3, 0, SG),
                     ZOWN[ci * 128:(ci + 1) * 128, s * SG:s * SG + SG],
                     pre=P.tok("vS"))
            tz = dma("scalar", SM[:, 790:791],
                     ZOWN[ci * 128:(ci + 1) * 128, s * SG + SG:s * SG + SG + 1],
                     pre=P.tok("vS"), ncok=True)
            for tt8 in range(8):
                gcol = s * SG + tt8 * TT
                zbuf = tt8 % 2

                def z2b(lo, hi, b2=zbuf):
                    if b2 == 0:
                        return strb(lo, hi)
                    return slotb(4, lo, hi) if hi <= 2 * SLOT else slotb(5, lo - 2 * SLOT,
                                                                         hi - 2 * SLOT)
                tz2 = None
                for kc in range(NCH):
                    tz2 = dma("sync", z2b(kc * 1024, kc * 1024 + 512),
                              Z2H[kc * 128:(kc + 1) * 128, gcol:gcol + TT],
                              pre=[LG.get(("z2", zbuf)), P.tok("vS")])
                    tz2 = dma("sync", z2b(kc * 1024 + 512, kc * 1024 + 1024),
                              Z2L[kc * 128:(kc + 1) * 128, gcol:gcol + TT],
                              pre=tz2)
                P.wait("tensor", tz2)
                for sub in range(3):
                    pb = PS[sub]
                    bank_wait(sub)
                    nmm = 0
                    for kc in range(NCH):
                        wh = WTB[:, sub * 1024 + kc * 128:sub * 1024 + (kc + 1) * 128]
                        wl = WTB[:, (3 + sub) * 1024 + kc * 128:(3 + sub) * 1024 + (kc + 1) * 128]
                        zh = z2b(kc * 1024, kc * 1024 + 512)
                        zl = z2b(kc * 1024 + 512, kc * 1024 + 1024)
                        for (a_, b_) in ((wh, zh), (wh, zl), (wl, zh)):
                            P.op("tensor", lambda e, a=a_, b=b_, p=pb, st2=(nmm == 0),
                                 sp2=(nmm == 23): e.matmul(p[:], a, b, start=st2, stop=sp2),
                                 inc=("tS", 1) if nmm == 23 else None)
                            nmm += 1
                    tmm = P.tok("tS")
                    if sub == 2:
                        LG[("z2", zbuf)] = tmm
                    P.wait("vector", tmm)
                    tev = P.op("vector", lambda e, d=slot(sub, tt8 * TT, (tt8 + 1) * TT), p=pb:
                               e.tensor_copy(d, p[:]), inc=("vS", 1))
                    bank_done(sub, tev)
            if ci == 0 and s == 0:
                snap("kraw0", slot(0, 0, SG), [128, SG])
                snap("vraw0", slot(1, 0, SG), [128, SG])
                snap("zprev0", slot(3, 0, SG), [128, SG])
                snap("vsb0", VSB[:, 0:192], [128, 192])
            P.wait("vector", tz)
            for sub, nmv, nmo in ((0, "tmk", "om_tmk"), (1, "tmv", "om_tmv"), (2, "tmr", "om_tmr")):
                P.op("vector", lambda e, a=slot(sub, 0, SG), m=vg(nmv, ci):
                     e.tensor_scalar(a, a, m, None, ALU.mult))
                if sub == 0 and ci == 0 and s == 0:
                    snap("klerp1", slot(0, 0, SG), [128, SG])
                P.op("vector", lambda e, o=vg(nmo, ci), z=slot(3, 0, SG):
                     e.tensor_scalar(slot(4, 0, SG), z, o, None, ALU.mult))
                if sub == 0 and ci == 0 and s == 0:
                    snap("klerp2", slot(4, 0, SG), [128, SG])
                P.op("vector", lambda e, a=slot(sub, 0, SG):
                     e.tensor_tensor(a, a, slot(4, 0, SG), ALU.add))
            if ci == 0 and s == 0:
                snap("kl0", slot(0, 0, SG), [128, SG])
                snap("vl0", slot(1, 0, SG), [128, SG])
            tx1 = wkv_chain(ci, s)
            dma("gpsimd", X1IN[ci * 128:(ci + 1) * 128, s * SG:(s + 1) * SG],
                slot(9, 0, SG), pre=tx1)

    P.wait("gpsimd", P.tok("dG"))
    tc1 = P.op("gpsimd", lambda e: e.collective_compute(
        "AllGather", ALU.bypass, replica_groups=[[0, 1], [2, 3], [4, 5], [6, 7]],
        ins=[X1IN.opt()], outs=[X1OUT.opt()]), inc=("cS", 1))
    tc2 = tc1

    def split_store(src_dram, row0, cols, dsth, dstl, kc_idx, ncb):
        P.wait("vector", P.tok("dS"))
        tl = dma("sync", slot(0, 0, cols), src_dram[row0:row0 + 128, 0:cols], pre=P.tok("vS"))
        P.wait("vector", tl)
        hi = slotb(3, 0, cols)
        lo = slotb(3, SLOT, SLOT + cols)
        P.op("vector", lambda e, o=hi: e.tensor_copy(o, slot(0, 0, cols)))
        P.op("vector", lambda e, i=hi: e.tensor_tensor(slot(0, 0, cols), slot(0, 0, cols), i,
                                                       ALU.subtract))
        tv = P.op("vector", lambda e, o=lo: e.tensor_copy(o, slot(0, 0, cols)), inc=("vS", 1))
        for ob in range(ncb):
            dma("sync", dsth[ob][:, kc_idx * 128:(kc_idx + 1) * 128],
                hi[:, ob * 128:(ob + 1) * 128], pre=tv)
            dma("sync", dstl[ob][:, kc_idx * 128:(kc_idx + 1) * 128],
                lo[:, ob * 128:(ob + 1) * 128], pre=tv)

    for kc in range(NCH):
        split_store(WCM, kc * 128, 4096, WCMHB, WCMLB, kc, 32)
    for kc in range(32):
        split_store(WCP, kc * 128, 1024, WCPHB, WCPLB, kc, 8)
    for nm, src in (("wq", WQT), ("wk", WKT), ("wv", WVT)):
        for kc in range(NCH):
            split_store(src, kc * 128, 1024, PRJ[nm + "h"], PRJ[nm + "l"], kc, 8)
    tvw = None
    for kc in range(NCH):
        tl = dma("sync", slot(0, 0, 1024), L0WT[kc * 128:(kc + 1) * 128, :], pre=P.tok("vS"))
        P.wait("vector", tl)
        P.op("vector", lambda e: e.scalar_tensor_tensor(slot(1, 0, 1024), slot(0, 0, 1024),
                                                        -1.0, slot(0, 0, 1024),
                                                        ALU.mult, ALU.max))
        if kc == 0:
            tvw = P.op("vector", lambda e: e.reduce_sum(SM[:, 860:861], slot(1, 0, 1024),
                                                        AX), inc=("vS", 1))
            P.wait("vector", tvw)
        else:
            trd = P.op("vector", lambda e: e.reduce_sum(SM[:, 861:862], slot(1, 0, 1024),
                                                        AX), inc=("vS", 1))
            P.wait("vector", trd)
            tvw = P.op("vector", lambda e: e.tensor_tensor(SM[:, 860:861], SM[:, 860:861],
                                                           SM[:, 861:862], ALU.add),
                       inc=("vS", 1))
    P.wait("tensor", tvw)
    bank_wait("PROW")
    tt_ = P.op("tensor", lambda e: e.matmul(PROW[0:1, 0:1], SM[:, 860:861], vg("one"),
                                            start=True, stop=True), inc=("tS", 1))
    P.wait("vector", tt_)
    VF(lambda e: e.tensor_scalar(vc("wsc"), PROW[0:1, 0:1], 1.0 / float(C * C),
                                 None, ALU.mult))
    tv = VF(lambda e: e.tensor_scalar(SR[0:1, 1000:1001], vc("wsc"), 0.5, None,
                                      ALU.mult))
    bank_done("PROW", tv)
    t_thb = bcast_cell(SR[0:1, 1000:1001], vg("thb"), tv)
    P.wait("vector", t_thb)
    VF(lambda e: e.tensor_scalar(vg("thnb"), vg("thb"), -1.0, None, ALU.mult))
    for kc in range(NCH):
        P.wait("vector", P.tok("dS"))
        tl = dma("sync", slot(0, 0, 1024), L0WT[kc * 128:(kc + 1) * 128, :], pre=P.tok("vS"))
        P.wait("vector", tl)
        P.op("vector", lambda e: e.tensor_scalar(slot(1, 0, 1024), slot(0, 0, 1024), vg("thb"),
                                                 None, ALU.is_gt))
        P.op("vector", lambda e: e.tensor_scalar(slot(2, 0, 1024), slot(0, 0, 1024), vg("thnb"),
                                                 None, ALU.is_lt))
        P.op("vector", lambda e: e.tensor_tensor(slot(1, 0, 1024), slot(1, 0, 1024),
                                                 slot(2, 0, 1024), ALU.subtract))
        tv = P.op("vector", lambda e: e.tensor_copy(slotb(3, 0, 1024), slot(1, 0, 1024)),
                  inc=("vS", 1))
        for ob in range(8):
            dma("sync", QWTB[ob][:, kc * 128:(kc + 1) * 128],
                slotb(3, ob * 128, (ob + 1) * 128), pre=tv)
    for kc in range(NCH):
        P.wait("vector", P.tok("dS"))
        dma("sync", stru(0, 4096), K0A[kc * 128:(kc + 1) * 128, :], pre=P.tok("vS"))
        tb = dma("sync", stru(4096, 8192), K0B[kc * 128:(kc + 1) * 128, :], pre=P.tok("vS"))
        P.wait("vector", tb)
        tv = P.op("vector", lambda e: e.tensor_tensor(slotb(3, 0, 4096), stru(0, 4096),
                                                      stru(4096, 8192), ALU.subtract),
                  inc=("vS", 1))
        for ob in range(32):
            dma("sync", D0B[ob][:, kc * 128:(kc + 1) * 128],
                slotb(3, ob * 128, (ob + 1) * 128), pre=tv)
    for kc in range(32):
        P.wait("vector", P.tok("dS"))
        ta = dma("sync", stru(0, 4096), K1U[kc * 128:(kc + 1) * 128, :], pre=P.tok("vS"))
        P.wait("vector", ta)
        tv = P.op("vector", lambda e: e.tensor_scalar(slotb(3, 0, 4096), stru(0, 4096),
                                                      2.0, -1.0, ALU.mult, ALU.add),
                  inc=("vS", 1))
        for ob in range(32):
            dma("sync", W1BB[ob][:, kc * 128:(kc + 1) * 128],
                slotb(3, ob * 128, (ob + 1) * 128), pre=tv)
    for kc in range(32):
        P.wait("vector", P.tok("dS"))
        ta = dma("sync", stru(0, 1024), K2U[kc * 128:(kc + 1) * 128, :], pre=P.tok("vS"))
        P.wait("vector", ta)
        tv = P.op("vector", lambda e: e.tensor_scalar(slotb(3, 0, 1024), stru(0, 1024),
                                                      2.0, -1.0, ALU.mult, ALU.add),
                  inc=("vS", 1))
        for ob in range(8):
            dma("sync", W2BB[ob][:, kc * 128:(kc + 1) * 128],
                slotb(3, ob * 128, (ob + 1) * 128), pre=tv)

    P.barrier()
    P.wait("gpsimd", tc1)
    snap("x1out", X1OUT, [C, T])

    for tt in range(NTO):
        c0 = tt * TT
        P.wait("vector", P.tok("dG"))
        hA = (c0 - 1) % T
        hB = TO + c0 - 1
        tl = None
        for ci in range(NCH):
            tl = dma("sync", slot(0, ci * TT, (ci + 1) * TT),
                     X1OUT[ci * 128:(ci + 1) * 128, c0:c0 + TT],
                     pre=[P.tok("vS"), P.tok("dG")])
            tl = dma("sync", slot(5, ci * TT, (ci + 1) * TT),
                     X1OUT[ci * 128:(ci + 1) * 128, TO + c0:TO + c0 + TT],
                     pre=[P.tok("vS"), P.tok("tS")])
            tl = dma("sync", SM[:, 800 + ci:801 + ci],
                     X1OUT[ci * 128:(ci + 1) * 128, hA:hA + 1], pre=P.tok("vS"), ncok=True)
            tl = dma("sync", SM[:, 808 + ci:809 + ci],
                     X1OUT[ci * 128:(ci + 1) * 128, hB:hB + 1], pre=P.tok("vS"), ncok=True)
        P.wait("vector", tl)
        for ci in range(NCH):
            P.op("vector", lambda e, ci=ci: e.tensor_tensor(
                slot(5, ci * TT, (ci + 1) * TT), slot(5, ci * TT, (ci + 1) * TT),
                slot(0, ci * TT, (ci + 1) * TT), ALU.subtract))
            P.op("vector", lambda e, ci=ci: e.tensor_scalar(
                slot(5, ci * TT, (ci + 1) * TT), slot(5, ci * TT, (ci + 1) * TT), SELS[:],
                None, ALU.mult))
            P.op("vector", lambda e, ci=ci: e.tensor_tensor(
                slot(0, ci * TT, (ci + 1) * TT), slot(0, ci * TT, (ci + 1) * TT),
                slot(5, ci * TT, (ci + 1) * TT), ALU.add))
            VF(lambda e, ci=ci: e.tensor_tensor(
                SM[:, 808 + ci:809 + ci], SM[:, 808 + ci:809 + ci],
                SM[:, 800 + ci:801 + ci], ALU.subtract))
            VF(lambda e, ci=ci: e.tensor_scalar(
                SM[:, 808 + ci:809 + ci], SM[:, 808 + ci:809 + ci], SELS[:], None, ALU.mult))
            VF(lambda e, ci=ci: e.tensor_tensor(
                SM[:, 800 + ci:801 + ci], SM[:, 800 + ci:801 + ci],
                SM[:, 808 + ci:809 + ci], ALU.add))
        for ci in range(NCH):
            P.op("vector", lambda e, ci=ci: e.tensor_copy(
                slot(1, ci * TT + 1, (ci + 1) * TT), slot(0, ci * TT, (ci + 1) * TT - 1)))
            P.op("vector", lambda e, ci=ci: e.tensor_copy(
                slot(1, ci * TT, ci * TT + 1), SM[:, 800 + ci:801 + ci]))
        tv = P.op("vector", lambda e: e.engine_nop(), inc=("vS", 1))
        tok = stats_pass(lambda ci: slot(0, ci * TT, (ci + 1) * TT), NCH,
                         lambda ci: slot(4, ci * TT, (ci + 1) * TT), tv)
        tok = stats_finalize(tok)
        tbc = stats_bcast(tok)
        P.wait("vector", tbc)
        for ci in range(NCH):
            sx = slot(0, ci * TT, (ci + 1) * TT)
            sn = slot(2, ci * TT, (ci + 1) * TT)
            P.op("vector", lambda e, a=sn, b=sx: e.tensor_tensor(a, b, PB0[:], ALU.subtract))
            P.op("vector", lambda e, a=sn: e.tensor_tensor(a, a, PB1[:], ALU.mult))
            P.op("vector", lambda e, a=sn, g=vg("g2", ci), b=vg("b2", ci):
                 e.tensor_scalar(a, a, g, b, ALU.mult, ALU.add))
            bh = slotb(3, ci * TT, (ci + 1) * TT)
            bl = slotb(3, SLOT + ci * TT, SLOT + (ci + 1) * TT)
            P.op("vector", lambda e, o=bh, i=sn: e.tensor_copy(o, i))
            P.op("vector", lambda e, a=slot(4, ci * TT, (ci + 1) * TT), b=sn, i=bh:
                 e.tensor_tensor(a, b, i, ALU.subtract))
            P.op("vector", lambda e, o=bl, i=slot(4, ci * TT, (ci + 1) * TT): e.tensor_copy(o, i))
        tvn = P.op("vector", lambda e: e.engine_nop(), inc=("vS", 1))
        bank_done("PB0", tvn)
        bank_done("PB1", tvn)
        P.wait("tensor", tvn)
        for oc in range(32):
            buf = oc % 2
            tl2 = None
            tl2 = dma("sync", strb(buf * 2048, buf * 2048 + 1024),
                      WCMHB[oc], pre=LG.get(("kk", buf)))
            tl2 = dma("sync", strb(4096 + buf * 2048, 4096 + buf * 2048 + 1024),
                      WCMLB[oc], pre=tl2)
            P.wait("tensor", tl2)
            pb = PS[oc % 4]
            bank_wait(oc % 4)
            nmm = 0
            for kc in range(NCH):
                wh = strb(buf * 2048 + kc * 128, buf * 2048 + (kc + 1) * 128)
                wl = strb(4096 + buf * 2048 + kc * 128, 4096 + buf * 2048 + (kc + 1) * 128)
                xh = slotb(3, kc * TT, (kc + 1) * TT)
                xl = slotb(3, SLOT + kc * TT, SLOT + (kc + 1) * TT)
                for (a_, b_) in ((wh, xh), (wh, xl), (wl, xh)):
                    P.op("tensor", lambda e, a=a_, b=b_, p=pb, st2=(nmm == 0), sp2=(nmm == 23):
                         e.matmul(p[:], a, b, start=st2, stop=sp2),
                         inc=("tS", 1) if nmm == 23 else None)
                    nmm += 1
            tmm = P.tok("tS")
            LG[("kk", buf)] = tmm
            P.wait("vector", tmm)
            cg = oc % 8
            sq = slot(4, 0, TT)
            P.op("vector", lambda e, p=pb, m=vg("cmk", cg), o=sq:
                 e.tensor_scalar(o, p[:], m, None, ALU.mult))
            P.op("vector", lambda e, xp=slot(1, cg * TT, (cg + 1) * TT), om=vg("om_cmk", cg):
                 e.tensor_scalar(slot(4, 4 * TT, 5 * TT), xp, om, None, ALU.mult))
            P.op("vector", lambda e, o=sq:
                 e.tensor_tensor(o, o, slot(4, 4 * TT, 5 * TT), ALU.add))
            P.op("vector", lambda e, o=sq: e.tensor_scalar(o, o, 0.0, None, ALU.max))
            P.op("vector", lambda e, o=sq: e.tensor_tensor(o, o, o, ALU.mult))
            qh = WK[:, 6 * SLOT:8 * SLOT].bitcast(BF16)[:, oc * TT:(oc + 1) * TT]
            ql = WK[:, 8 * SLOT:10 * SLOT].bitcast(BF16)[:, oc * TT:(oc + 1) * TT]
            P.op("vector", lambda e, o=qh, i=sq: e.tensor_copy(o, i))
            P.op("vector", lambda e, o=sq, i=qh: e.tensor_tensor(o, o, i, ALU.subtract))
            tvq = P.op("vector", lambda e, o=ql, i=sq: e.tensor_copy(o, i), inc=("vS", 1))
            bank_done(oc % 4, tvq)
        tl3 = None
        for ci in range(NCH):
            tl3 = dma("sync", slot(3, ci * TT, (ci + 1) * TT),
                      X0O[ci * 128:(ci + 1) * 128, c0:c0 + TT],
                      pre=[P.tok("vS"), P.tok("tS")])
        P.wait("tensor", P.tok("vS"))
        last_dma = {}
        for oc2 in range(NCH):
            buf2 = oc2 % 2

            def vvb(lo, hi, b2=buf2):
                if b2 == 0:
                    return strb(lo, hi)
                return slotb(5, lo, hi)
            tl2 = None
            tl2 = dma("scalar", vvb(0, 4096), WCPHB[oc2],
                      pre=[LG.get(("vv", buf2)), P.tok("vS")])
            tl2 = dma("scalar", vvb(4096, 8192), WCPLB[oc2], pre=tl2)
            P.wait("tensor", tl2)
            pb = PS[oc2 % 4]
            bank_wait(oc2 % 4)
            nmm = 0
            for kc in range(32):
                wh = vvb(kc * 128, (kc + 1) * 128)
                wl = vvb(4096 + kc * 128, 4096 + (kc + 1) * 128)
                sqh = WK[:, 6 * SLOT:8 * SLOT].bitcast(BF16)[:, kc * TT:(kc + 1) * TT]
                sql = WK[:, 8 * SLOT:10 * SLOT].bitcast(BF16)[:, kc * TT:(kc + 1) * TT]
                for (a_, b_) in ((wh, sqh), (wh, sql), (wl, sqh)):
                    P.op("tensor", lambda e, a=a_, b=b_, p=pb, st2=(nmm == 0), sp2=(nmm == 95):
                         e.matmul(p[:], a, b, start=st2, stop=sp2),
                         inc=("tS", 1) if nmm == 95 else None)
                    nmm += 1
            tmm = P.tok("tS")
            LG[("vv", buf2)] = tmm
            par = oc2 % 2
            rr = slot(4, (2 * par) * TT, (2 * par + 1) * TT)
            out_ = slot(4, (2 * par + 1) * TT, (2 * par + 2) * TT)
            P.wait("vector", last_dma.get(par))
            P.op("vector", lambda e, o=rr, xn=slot(2, oc2 * TT, (oc2 + 1) * TT), m=vg("cmr", oc2):
                 e.tensor_scalar(o, xn, m, None, ALU.mult))
            P.op("vector", lambda e, xp=slot(1, oc2 * TT, (oc2 + 1) * TT), om=vg("om_cmr", oc2):
                 e.tensor_scalar(slot(4, 5 * TT, 6 * TT), xp, om, None, ALU.mult))
            tvr = P.op("vector", lambda e, o=rr:
                       e.tensor_tensor(o, o, slot(4, 5 * TT, 6 * TT), ALU.add), inc=("vS", 1))
            P.wait("scalar", tvr)
            tsr = P.op("scalar", lambda e, o=rr: e.activation(o, o, AF.Sigmoid), inc=("sS", 1))
            P.wait("vector", tsr)
            P.wait("vector", tmm)
            if oc2 == 0:
                P.wait("vector", tl3)
            P.op("vector", lambda e, o=out_, p=pb, r2=rr: e.tensor_tensor(o, r2, p[:], ALU.mult))
            P.op("vector", lambda e, o=out_, x1=slot(0, oc2 * TT, (oc2 + 1) * TT):
                 e.tensor_tensor(o, o, x1, ALU.add))
            tvo = P.op("vector", lambda e, o=out_, x0=slot(3, oc2 * TT, (oc2 + 1) * TT):
                       e.tensor_tensor(o, o, x0, ALU.add), inc=("vS", 1))
            bank_done(oc2 % 4, tvo)
            tdm = dma("gpsimd", RRES[oc2 * 128:(oc2 + 1) * 128, c0:c0 + TT], out_, pre=tvo)
            last_dma[par] = tdm
        P.barrier()

    snap("rres3", RRES, [C, TO])
    P.op("vector", lambda e: e.memset(SM[:, 820:821], 0.0))

    def ln_tile(tt, dst_slot):
        c0 = tt * TT
        tl = None
        for ci in range(NCH):
            tl = dma("sync", slot(0, ci * TT, (ci + 1) * TT),
                     RRES[ci * 128:(ci + 1) * 128, c0:c0 + TT],
                     pre=[P.tok("vS"), P.tok("dG")])
        tok = stats_pass(lambda ci: slot(0, ci * TT, (ci + 1) * TT), NCH,
                         lambda ci: slot(dst_slot, ci * TT, (ci + 1) * TT), tl)
        tok = stats_finalize(tok)
        tbc = stats_bcast(tok)
        P.wait("vector", tbc)
        for ci in range(NCH):
            sx = slot(0, ci * TT, (ci + 1) * TT)
            sz = slot(dst_slot, ci * TT, (ci + 1) * TT)
            P.op("vector", lambda e, a=sz, b=sx: e.tensor_tensor(a, b, PB0[:], ALU.subtract))
            P.op("vector", lambda e, a=sz: e.tensor_tensor(a, a, PB1[:], ALU.mult))
        tv_ = P.op("vector", lambda e: e.engine_nop(), inc=("vS", 1))
        bank_done("PB0", tv_)
        bank_done("PB1", tv_)
        return tv_

    for tt in range(NTO):
        c0 = tt * TT
        ln_tile(tt, 1)
        for ci in range(NCH):
            trd = P.op("vector", lambda e, a=slot(1, ci * TT, (ci + 1) * TT):
                       e.tensor_reduce(SM[:, 821:822], a, AX, ALU.max,
                                       apply_absolute_value=True), inc=("vS", 1))
            P.wait("vector", trd)
            P.op("vector", lambda e: e.tensor_tensor(SM[:, 820:821], SM[:, 820:821],
                                                     SM[:, 821:822], ALU.max))
        tvx4 = P.op("vector", lambda e: e.engine_nop(), inc=("vS", 1))
        for ci in range(NCH):
            dma("gpsimd", X4S[ci * 128:(ci + 1) * 128, c0:c0 + TT],
                slot(1, ci * TT, (ci + 1) * TT), pre=tvx4)
    tv = P.tok("vS")
    P.wait("tensor", tv)
    bank_wait("PT")
    tt_ = P.op("tensor", lambda e: e.transpose(PT[0:1, 0:128], SM[:, 820:821], IDNS[:]),
               inc=("tS", 1))
    P.wait("vector", tt_)
    trd = P.op("vector", lambda e: e.tensor_reduce(SR[0:1, 900:901], PT[0:1, 0:128],
                                                   AX, ALU.max), inc=("vS", 1))
    P.wait("vector", trd)
    tv = VF(lambda e: e.tensor_copy(SR[0:1, 901:909],
                                    SR[0:1, 900:901].broadcast_to((1, 8))))
    bank_done("PT", tv)
    dma("gpsimd", ARI, SR[0:1, 901:909], pre=tv)
    P.wait("gpsimd", P.tok("dG"))
    tc3 = P.op("gpsimd", lambda e: e.collective_compute(
        "AllReduce", ALU.max, replica_groups=[list(range(8))],
        ins=[ARI.opt()], outs=[ARO.opt()]), inc=("cS", 1))
    P.wait("sync", tc3)
    tgm = dma("sync", SR[0:1, 910:918], ARO)
    P.wait("vector", tgm)
    VF(lambda e: e.tensor_scalar(vc("iscale"), SR[0:1, 910:911], 1.0 / 127.0,
                                 None, ALU.mult))
    VF(lambda e: e.reciprocal(SR[0:1, 920:921], vc("iscale")))
    tv = VF(lambda e: e.tensor_tensor(vc("wsis"), vc("wsc"), vc("iscale"), ALU.mult))
    t_invb = bcast_cell(SR[0:1, 920:921], vg("invb"), tv)
    t_wsis = bcast_cell(vc("wsis"), vg("wsisb"), t_invb)
    P.barrier()
    for tt in range(NTO):
        c0 = tt * TT
        tl4 = None
        for ci in range(NCH):
            tl4 = dma("sync", slot(0, ci * TT, (ci + 1) * TT),
                      RRES[ci * 128:(ci + 1) * 128, c0:c0 + TT],
                      pre=[P.tok("vS"), P.tok("dG")])
            tl4 = dma("sync", slot(1, ci * TT, (ci + 1) * TT),
                      X4S[ci * 128:(ci + 1) * 128, c0:c0 + TT], pre=P.tok("vS"))
        P.wait("vector", tl4)
        for ci in range(NCH):
            sz = slot(1, ci * TT, (ci + 1) * TT)
            P.op("vector", lambda e, a=sz: e.tensor_scalar(a, a, vg("invb"), None, ALU.mult))
            P.op("vector", lambda e, a=sz, o=sloti(2, ci * TT, (ci + 1) * TT): e.tensor_copy(o, a))
            P.op("vector", lambda e, a=sz, i=sloti(2, ci * TT, (ci + 1) * TT): e.tensor_copy(a, i))
            P.op("vector", lambda e, a=sz: e.tensor_scalar(a, a, 127.0, -128.0, ALU.min, ALU.max))
            P.op("vector", lambda e, a=sz, o=slotb(3, ci * TT, (ci + 1) * TT): e.tensor_copy(o, a))
        tvq = P.op("vector", lambda e: e.engine_nop(), inc=("vS", 1))
        P.wait("tensor", tvq)
        last_dma = {}
        for oc in range(NCH):
            buf = oc % 2
            tl2 = dma("sync", strb(buf * 2048, buf * 2048 + 1024),
                      QWTB[oc], pre=LG.get(("qw", buf)))
            P.wait("tensor", tl2)
            pb = PS[oc % 4]
            bank_wait(oc % 4)
            for kc in range(NCH):
                wq_ = strb(buf * 2048 + kc * 128, buf * 2048 + (kc + 1) * 128)
                qi = slotb(3, kc * TT, (kc + 1) * TT)
                P.op("tensor", lambda e, a=wq_, b=qi, p=pb, st2=(kc == 0), sp2=(kc == 7):
                     e.matmul(p[:], a, b, start=st2, stop=sp2),
                     inc=("tS", 1) if kc == 7 else None)
            tmm = P.tok("tS")
            LG[("qw", buf)] = tmm
            P.wait("vector", tmm)
            par = oc % 2
            out_ = slot(4, par * TT, (par + 1) * TT)
            P.wait("vector", last_dma.get(par))
            P.op("vector", lambda e, o=out_, p=pb, b=vg("l0b", oc):
                 e.tensor_scalar(o, p[:], vg("wsisb"), b, ALU.mult, ALU.add))
            tvo = P.op("vector", lambda e, o=out_, x=slot(0, oc * TT, (oc + 1) * TT):
                       e.tensor_tensor(o, o, x, ALU.add), inc=("vS", 1))
            bank_done(oc % 4, tvo)
            last_dma[par] = dma("gpsimd", RRES[oc * 128:(oc + 1) * 128, c0:c0 + TT], out_,
                                pre=tvo)
        P.barrier()

    snap("rres4", RRES, [C, TO])
    P.op("vector", lambda e: e.memset(SM[0:1, 830:832], 0.0))
    for tt in range(NTO):
        c0 = tt * TT
        ln_tile(tt, 1)
        tvlast = None
        for ci in range(NCH):
            sz = slot(1, ci * TT, (ci + 1) * TT)
            bh = slotb(3, ci * TT, (ci + 1) * TT)
            bl = slotb(3, SLOT + ci * TT, SLOT + (ci + 1) * TT)
            P.op("vector", lambda e, o=bh, i=sz: e.tensor_copy(o, i))
            P.op("vector", lambda e, a=slot(2, ci * TT, (ci + 1) * TT), b=sz, i=bh:
                 e.tensor_tensor(a, b, i, ALU.subtract))
            tvlast = P.op("vector", lambda e, o=bl, i=slot(2, ci * TT, (ci + 1) * TT):
                          e.tensor_copy(o, i), inc=("vS", 1))
            dma("gpsimd", XCH[ci * 128:(ci + 1) * 128, c0:c0 + TT], bh, pre=tvlast)
            dma("gpsimd", XCH[C + ci * 128:C + (ci + 1) * 128, c0:c0 + TT], bl, pre=tvlast)
    P.wait("gpsimd", P.tok("dG"))
    tc4 = P.op("gpsimd", lambda e: e.collective_compute(
        "AllGather", ALU.bypass, replica_groups=[[0, 1], [2, 3], [4, 5], [6, 7]],
        ins=[XCH.opt()], outs=[XCG.opt()]), inc=("cS", 1))

    def proj(dst_h, dst_l, wname, bias_nm, scale_pow, row_h, row_l, src_dram, dst_col0,
             norm_cell, do_transpose=False):
        for tt in range(NTO):
            c0 = tt * TT
            tl = None
            for kc in range(NCH):
                tl = dma("sync", slotb(5, kc * TT, (kc + 1) * TT),
                         src_dram[row_h + kc * 128:row_h + (kc + 1) * 128, c0:c0 + TT],
                         pre=[P.tok("tS"), P.tok("dG")])
                tl = dma("sync", slotb(5, SLOT + kc * TT, SLOT + (kc + 1) * TT),
                         src_dram[row_l + kc * 128:row_l + (kc + 1) * 128, c0:c0 + TT],
                         pre=P.tok("tS"))
            P.wait("tensor", tl)
            tsq_prev = None
            for oc in range(NCH):
                buf = oc % 2
                tl2 = dma("scalar", strb(buf * 2048, buf * 2048 + 1024),
                          PRJ[wname + "h"][oc], pre=LG.get((wname, buf)))
                tl2 = dma("scalar", strb(4096 + buf * 2048, 4096 + buf * 2048 + 1024),
                          PRJ[wname + "l"][oc], pre=tl2)
                P.wait("tensor", tl2)
                pb = PS[oc % 2]
                bank_wait(oc % 2)
                nmm = 0
                for kc in range(NCH):
                    wh = strb(buf * 2048 + kc * 128, buf * 2048 + kc * 128 + 128)
                    wl = strb(4096 + buf * 2048 + kc * 128, 4096 + buf * 2048 + kc * 128 + 128)
                    xh = slotb(5, kc * TT, (kc + 1) * TT)
                    xl = slotb(5, SLOT + kc * TT, SLOT + (kc + 1) * TT)
                    for (a_, b_) in ((wh, xh), (wh, xl), (wl, xh)):
                        P.op("tensor", lambda e, a=a_, b=b_, p=pb, st2=(nmm == 0), sp2=(nmm == 23):
                             e.matmul(p[:], a, b, start=st2, stop=sp2),
                             inc=("tS", 1) if nmm == 23 else None)
                        nmm += 1
                tmm = P.tok("tS")
                LG[(wname, buf)] = tmm
                P.wait("vector", tmm)
                P.wait("vector", tsq_prev)
                sq = slot(4, 0, TT)
                if scale_pow is not None:
                    P.op("vector", lambda e, p=pb, o=sq, b=vg(bias_nm, oc):
                         e.tensor_scalar(o, p[:], b, scale_pow, ALU.add, ALU.mult))
                else:
                    P.op("vector", lambda e, p=pb, o=sq, b=vg(bias_nm, oc):
                         e.tensor_scalar(o, p[:], b, None, ALU.add))
                par = oc % 2
                bh = slotb(0, par * 2 * TT, par * 2 * TT + TT)
                bl = slotb(0, par * 2 * TT + TT, par * 2 * TT + 2 * TT)
                P.wait("vector", LG.get(("pst", wname, par)))
                P.op("vector", lambda e, o=bh, i=sq: e.tensor_copy(o, i))
                P.op("vector", lambda e, o=slot(1, 0, TT), a=sq, i=bh:
                     e.tensor_tensor(o, a, i, ALU.subtract))
                tvs = P.op("vector", lambda e, o=bl, i=slot(1, 0, TT): e.tensor_copy(o, i),
                           inc=("vS", 1))
                bank_done(oc % 4, tvs)
                if not do_transpose:
                    dma("sync", dst_h[oc * 128:(oc + 1) * 128, dst_col0 + c0:dst_col0 + c0 + TT],
                        bh, pre=tvs)
                    LG[("pst", wname, par)] = dma(
                        "sync", dst_l[oc * 128:(oc + 1) * 128, dst_col0 + c0:dst_col0 + c0 + TT],
                        bl, pre=tvs)
                else:
                    P.wait("tensor", tvs)
                    toks = []
                    for tb in range(4):
                        for si, (srcb, dstd) in enumerate(((bh, dst_h), (bl, dst_l))):
                            idx = tb * 2 + si
                            tgt = idx % 3
                            bank_wait(("PTB", tgt))
                            tt2 = P.op("tensor", lambda e, s2=srcb[:, tb * 128:(tb + 1) * 128],
                                       t3=PTBS[tgt]:
                                       e.transpose(t3, s2, IDNBS[:]), inc=("tS", 1))
                            P.wait("vector", tt2)
                            tvt = P.op("vector", lambda e, idx=idx, t3=PTBS[tgt]: e.tensor_copy(
                                slotb(2, idx * 128, (idx + 1) * 128), t3), inc=("vS", 1))
                            bank_done(("PTB", tgt), tvt)
                            dma("gpsimd",
                                dstd[dst_col0 + c0 + tb * 128:dst_col0 + c0 + (tb + 1) * 128,
                                     oc * 128:(oc + 1) * 128],
                                slotb(2, idx * 128, (idx + 1) * 128), pre=tvt)
                if norm_cell is not None:
                    P.wait("scalar", tvs)
                    sqb = WK[:, SLOT + TT:SLOT + 2 * TT].bitcast(BF16)[:, 0:TT]
                    tsq_prev = P.op("scalar", lambda e, o=sqb, i=sq:
                                    e.activation(o, i, AF.Square), inc=("sS", 1))
                    P.wait("tensor", tsq_prev)
                    if oc == 0:
                        bank_wait("PROW")
                    P.op("tensor", lambda e, r=sqb, st2=(oc == 0), sp2=(oc == 7):
                         e.matmul(PROW[0:1, :], ONEB[:], r, start=st2, stop=sp2),
                         inc=("tS", 1) if oc == 7 else None)
            if norm_cell is not None:
                tns = P.tok("tS")
                P.wait("vector", tns)
                trd = P.op("vector", lambda e: e.tensor_reduce(SM[0:1, 845:846], PROW[0:1, :],
                                                               AX, ALU.max), inc=("vS", 1))
                P.wait("vector", trd)
                P.op("vector", lambda e, nc_=norm_cell: e.tensor_tensor(
                    nc_, nc_, SM[0:1, 845:846], ALU.max))
                tve = P.op("vector", lambda e: e.engine_nop(), inc=("vS", 1))
                bank_done("PROW", tve)

    proj(QSH, QSL, "wq", "bq", 1.0 / 1024.0, 0, C, XCH, 0, SM[0:1, 830:831])
    P.barrier()
    P.wait("sync", tc4)
    P.wait("scalar", tc4)
    P.wait("tensor", tc4)
    P.wait("vector", tc4)
    P.wait("gpsimd", tc4)
    for h in range(2):
        proj(KSH, KSL, "wk", "bk", None, h * 2 * C, h * 2 * C + C, XCG, h * TO,
             SM[0:1, 831:832])
    for h in range(2):
        proj(VTH, VTL, "wv", "bv", None, h * 2 * C, h * 2 * C + C, XCG, h * TO,
             None, do_transpose=True)
    tv = P.tok("vS")
    P.wait("scalar", tv)
    P.op("scalar", lambda e: e.activation(SM[0:1, 830:831], SM[0:1, 830:831], AF.Sqrt))
    ts = P.op("scalar", lambda e: e.activation(SM[0:1, 831:832], SM[0:1, 831:832], AF.Sqrt),
              inc=("sS", 1))
    P.wait("vector", ts)
    VF(lambda e: e.tensor_tensor(vc("mbc"), SM[0:1, 830:831], SM[0:1, 831:832],
                                 ALU.mult))
    tv = VF(lambda e: e.tensor_scalar(SR[0:1, 930:931], vc("mbc"), -1.01, None,
                                      ALU.mult))
    t_negm = bcast_cell(SR[0:1, 930:931], vg("negmb"), tv)
    P.barrier()

    for qb in range(NTO):
        q0 = qb * TT
        tq = None
        for kc in range(NCH):
            tq = dma("sync", slotb(2, kc * TT, (kc + 1) * TT),
                     QSH[kc * 128:(kc + 1) * 128, q0:q0 + TT],
                     pre=[P.tok("vS"), P.tok("tS"), P.tok("dG")])
            tq = dma("sync", slotb(2, SLOT + kc * TT, SLOT + (kc + 1) * TT),
                     QSL[kc * 128:(kc + 1) * 128, q0:q0 + TT], pre=P.tok("vS"))
        P.op("vector", lambda e: e.memset(slot(8, 0, SLOT), 0.0))
        tvz = P.op("vector", lambda e: e.engine_nop(), inc=("vS", 1))
        P.wait("tensor", tvz)
        P.wait("tensor", tq)
        first_den = [True]
        tvp_prev = [None]
        for kt in range(NTT):
            k0_ = kt * TT
            buf = kt % 2
            tl = None
            for kc in range(NCH):
                tl = dma("sync", slotb(3 + buf, kc * TT, (kc + 1) * TT),
                         KSH[kc * 128:(kc + 1) * 128, k0_:k0_ + TT],
                         pre=LG.get(("fKV", buf)))
                tl = dma("sync", slotb(3 + buf, SLOT + kc * TT, SLOT + (kc + 1) * TT),
                         KSL[kc * 128:(kc + 1) * 128, k0_:k0_ + TT], pre=tl)
            tv_ = None
            for tb in range(4):
                tv_ = dma("scalar", slotb(5 + buf, tb * 1024, (tb + 1) * 1024),
                          VTH[k0_ + tb * 128:k0_ + (tb + 1) * 128, :],
                          pre=LG.get(("fKV", buf)))
                tv_ = dma("scalar", slotb(5 + buf, SLOT + tb * 1024, SLOT + (tb + 1) * 1024),
                          VTL[k0_ + tb * 128:k0_ + (tb + 1) * 128, :], pre=tv_)
            P.wait("tensor", tl)
            P.wait("tensor", tv_)
            for kc4 in range(4):
                pb = PS[kc4]
                bank_wait(kc4)
                nmm = 0
                for cc in range(NCH):
                    kh = slotb(3 + buf, cc * TT + kc4 * 128, cc * TT + (kc4 + 1) * 128)
                    kl = slotb(3 + buf, SLOT + cc * TT + kc4 * 128,
                               SLOT + cc * TT + (kc4 + 1) * 128)
                    qh = slotb(2, cc * TT, (cc + 1) * TT)
                    ql = slotb(2, SLOT + cc * TT, SLOT + (cc + 1) * TT)
                    for (a_, b_) in ((kh, qh), (kh, ql), (kl, qh)):
                        P.op("tensor", lambda e, a=a_, b=b_, p=pb, st2=(nmm == 0), sp2=(nmm == 23):
                             e.matmul(p[:], a, b, start=st2, stop=sp2),
                             inc=("tS", 1) if nmm == 23 else None)
                        nmm += 1
                tmm = P.tok("tS")
                P.wait("vector", tmm)
                P.wait("vector", tvp_prev[0])
                p32 = slot(7, 0, TT)
                tvv = P.op("vector", lambda e, p=pb, o=p32:
                           e.tensor_scalar(o, p[:], vg("negmb"), None, ALU.add),
                           inc=("vS", 1))
                P.wait("scalar", tvv)
                tse = P.op("scalar", lambda e, o=p32: e.activation(o, o, AF.Exp),
                           inc=("sS", 1))
                P.wait("vector", tse)
                if kt == qb:
                    P.op("vector", lambda e, o=p32[:, kc4 * 128:(kc4 + 1) * 128]:
                         e.tensor_tensor(o, o, MSKES[:], ALU.mult))
                if kt == 8 + qb:
                    P.op("vector", lambda e, o=p32[:, kc4 * 128:(kc4 + 1) * 128]:
                         e.tensor_tensor(o, o, MSKOS[:], ALU.mult))
                ph = slotb(7, 1024 + kc4 * TT, 1024 + (kc4 + 1) * TT)
                pl = slotb(7, 3072 + kc4 * TT, 3072 + (kc4 + 1) * TT)
                P.op("vector", lambda e, o=ph, i=p32: e.tensor_copy(o, i))
                P.op("vector", lambda e, i=ph, o=slot(9, 0, TT), a=p32:
                     e.tensor_tensor(o, a, i, ALU.subtract))
                tvp = P.op("vector", lambda e, o=pl: e.tensor_copy(o, slot(9, 0, TT)),
                           inc=("vS", 1))
                tvp_prev[0] = tvp
                bank_done(kc4, tvp)
                P.wait("tensor", tvp)
                for pp in (ph, pl):
                    sp_ = (kt == NTT - 1 and kc4 == 3 and pp is pl)
                    P.op("tensor", lambda e, r=pp, st2=first_den[0], sp2=sp_:
                         e.matmul(PROW[0:1, :], ONEB[:], r, start=st2, stop=sp2),
                         inc=("tS", 1) if sp_ else None)
                    first_den[0] = False
            for qc in range(4):
                for ct in range(2):
                    pb = PB0 if (qc * 2 + ct) % 2 == 0 else PB1
                    bid = "PB0" if (qc * 2 + ct) % 2 == 0 else "PB1"
                    bank_wait(bid)
                    bank_wait(("PTB", 1 if bid == "PB0" else 2))
                    nmm = 0
                    for kc4 in range(4):
                        ph = slotb(7, 1024 + kc4 * TT + qc * 128, 1024 + kc4 * TT + (qc + 1) * 128)
                        pl = slotb(7, 3072 + kc4 * TT + qc * 128, 3072 + kc4 * TT + (qc + 1) * 128)
                        vh = slotb(5 + buf, kc4 * 1024 + ct * 512, kc4 * 1024 + (ct + 1) * 512)
                        vl = slotb(5 + buf, SLOT + kc4 * 1024 + ct * 512,
                                   SLOT + kc4 * 1024 + (ct + 1) * 512)
                        for (a_, b_) in ((ph, vh), (pl, vh), (ph, vl)):
                            P.op("tensor", lambda e, a=a_, b=b_, p=pb, st2=(nmm == 0),
                                 sp2=(nmm == 11):
                                 e.matmul(p[:], a, b, start=st2, stop=sp2),
                                 inc=("tS", 1) if nmm == 11 else None)
                            nmm += 1
                    tmm = P.tok("tS")
                    if qc == 3 and ct == 1:
                        LG[("fKV", buf)] = tmm
                    P.wait("vector", tmm)
                    oa = WK[:, 8 * SLOT + qc * 1024 + ct * 512:8 * SLOT + qc * 1024 + (ct + 1) * 512]
                    tva = P.op("vector", lambda e, o=oa, p=pb:
                               e.tensor_tensor(o, o, p[:], ALU.add), inc=("vS", 1))
                    bank_done(bid, tva)
        tden = P.tok("tS")
        P.wait("vector", tden)
        tvd = P.op("vector", lambda e: e.tensor_copy(SR[0:1, 0:512], PROW[0:1, :]), inc=("vS", 1))
        bank_done("PROW", tvd)
        P.wait("tensor", tvd)
        for qc in range(4):
            bank_wait("PT")
            tt2 = P.op("tensor", lambda e, qc=qc: e.transpose(
                PT[:, 0:1], SR[0:1, qc * 128:(qc + 1) * 128], IDNS[0:1, 0:1]), inc=("tS", 1))
            P.wait("vector", tt2)
            tvt = VF(lambda e, qc=qc: e.tensor_copy(SM[:, 850 + qc:851 + qc], PT[:, 0:1]))
            bank_done("PT", tvt)
        VF(lambda e: e.reciprocal(SM[:, 850:854], SM[:, 850:854]))
        for qc in range(4):
            P.op("vector", lambda e, qc=qc: e.tensor_scalar(
                WK[:, 8 * SLOT + qc * 1024:8 * SLOT + (qc + 1) * 1024],
                WK[:, 8 * SLOT + qc * 1024:8 * SLOT + (qc + 1) * 1024],
                SM[:, 850 + qc:851 + qc], None, ALU.mult))
        tvf = P.op("vector", lambda e: e.engine_nop(), inc=("vS", 1))
        tl = None
        for ci in range(NCH):
            tl = dma("sync", slot(0, ci * TT, (ci + 1) * TT),
                     RRES[ci * 128:(ci + 1) * 128, q0:q0 + TT],
                     pre=[P.tok("vS"), P.tok("dG")])
        P.wait("tensor", tvf)
        P.wait("vector", tl)
        for qc in range(4):
            for cc in range(NCH):
                bank_wait("PT")
                tt2 = P.op("tensor", lambda e, qc=qc, cc=cc: e.transpose(
                    PT[:],
                    WK[:, 8 * SLOT + qc * 1024 + cc * 128:8 * SLOT + qc * 1024 + (cc + 1) * 128],
                    IDNS[:]), inc=("tS", 1))
                P.wait("vector", tt2)
                tva = P.op("vector", lambda e, qc=qc, cc=cc: e.tensor_tensor(
                    slot(0, cc * TT + qc * 128, cc * TT + (qc + 1) * 128),
                    slot(0, cc * TT + qc * 128, cc * TT + (qc + 1) * 128),
                    PT[:], ALU.add), inc=("vS", 1))
                bank_done("PT", tva)
        tvo = P.tok("vS")
        for ci in range(NCH):
            dma("gpsimd", RRES[ci * 128:(ci + 1) * 128, q0:q0 + TT],
                slot(0, ci * TT, (ci + 1) * TT), pre=tvo)
        P.barrier()

    snap("rres5", RRES, [C, TO])
    for tt in range(NTO):
        c0 = tt * TT
        ln_tile(tt, 1)
        for ci in range(NCH):
            sz = slot(1, ci * TT, (ci + 1) * TT)
            bh = slotb(2, ci * TT, (ci + 1) * TT)
            bl = slotb(2, SLOT + ci * TT, SLOT + (ci + 1) * TT)
            P.op("vector", lambda e, o=bh, i=sz: e.tensor_copy(o, i))
            P.op("vector", lambda e, a=sz, i=bh: e.tensor_tensor(a, a, i, ALU.subtract))
            P.op("vector", lambda e, o=bl, i=sz: e.tensor_copy(o, i))
        tvx = P.op("vector", lambda e: e.engine_nop(), inc=("vS", 1))
        P.wait("tensor", tvx)

        def glayer(nkc, nout, wsrc, xh_fn, xl_fn, outh_fn, outl_fn):
            P.wait("tensor", P.tok("vS"))
            for oc in range(nout):
                buf = oc % 2
                tl2 = dma("scalar", strb(buf * 4096, buf * 4096 + nkc * 128),
                          wsrc[oc], pre=LG.get(("gl", buf)))
                P.wait("tensor", tl2)
                pb = PS[oc % 4]
                bank_wait(oc % 4)
                nmm = 0
                ntot = nkc * 2
                for kc in range(nkc):
                    wb_ = strb(buf * 4096 + kc * 128, buf * 4096 + (kc + 1) * 128)
                    for b_ in (xh_fn(kc), xl_fn(kc)):
                        P.op("tensor", lambda e, a=wb_, b=b_, p=pb, st2=(nmm == 0),
                             sp2=(nmm == ntot - 1):
                             e.matmul(p[:], a, b, start=st2, stop=sp2),
                             inc=("tS", 1) if nmm == ntot - 1 else None)
                        nmm += 1
                tmm = P.tok("tS")
                LG[("gl", buf)] = tmm
                P.wait("vector", tmm)
                if outh_fn is not None:
                    sq = slot(1, 0, TT)
                    P.op("vector", lambda e, o=sq, p=pb: e.tensor_scalar(o, p[:], 0.0, None,
                                                                         ALU.max))
                    P.op("vector", lambda e, o=outh_fn(oc), i=sq: e.tensor_copy(o, i))
                    P.op("vector", lambda e, o=sq, i=outh_fn(oc):
                         e.tensor_tensor(o, o, i, ALU.subtract))
                    tvl = P.op("vector", lambda e, o=outl_fn(oc), i=sq: e.tensor_copy(o, i),
                               inc=("vS", 1))
                    bank_done(oc % 4, tvl)
                else:
                    out_ = slot(1, 0, TT)
                    P.op("vector", lambda e, o=out_, p=pb:
                         e.tensor_scalar(o, p[:], vg("csbb"), None, ALU.mult))
                    tvo2 = P.op("vector", lambda e, o=out_, x=slot(0, oc * TT, (oc + 1) * TT):
                                e.tensor_tensor(o, o, x, ALU.add), inc=("vS", 1))
                    bank_done(oc % 4, tvo2)
                    P.wait("tensor", tvo2)
                    for tb in range(4):
                        bank_wait("PT")
                        tt2 = P.op("tensor", lambda e, tb=tb: e.transpose(
                            PT[:], out_[:, tb * 128:(tb + 1) * 128], IDNS[:]), inc=("tS", 1))
                        P.wait("vector", tt2)
                        P.wait("vector", P.tok("dG") if tb == 0 and oc == 0 else None)
                        tvt = P.op("vector", lambda e, tb=tb: e.tensor_copy(
                            slot(1, TT + tb * 128, TT + (tb + 1) * 128), PT[:]), inc=("vS", 1))
                        bank_done("PT", tvt)
                        P.wait("tensor", tvt)
                        dma("gpsimd", OUT[c0 + tb * 128:c0 + (tb + 1) * 128,
                                          oc * 128:(oc + 1) * 128],
                            slot(1, TT + tb * 128, TT + (tb + 1) * 128), pre=tvt)

        glayer(NCH, 32, D0B,
               lambda kc: slotb(2, kc * TT, (kc + 1) * TT),
               lambda kc: slotb(2, SLOT + kc * TT, SLOT + (kc + 1) * TT),
               lambda oc: WK[:, 3 * SLOT:5 * SLOT].bitcast(BF16)[:, oc * TT:(oc + 1) * TT],
               lambda oc: WK[:, 5 * SLOT:7 * SLOT].bitcast(BF16)[:, oc * TT:(oc + 1) * TT])
        glayer(32, 32, W1BB,
               lambda kc: WK[:, 3 * SLOT:5 * SLOT].bitcast(BF16)[:, kc * TT:(kc + 1) * TT],
               lambda kc: WK[:, 5 * SLOT:7 * SLOT].bitcast(BF16)[:, kc * TT:(kc + 1) * TT],
               lambda oc: WK[:, 7 * SLOT:9 * SLOT].bitcast(BF16)[:, oc * TT:(oc + 1) * TT],
               lambda oc: (WK[:, 9 * SLOT:10 * SLOT].bitcast(BF16)[:, oc * TT:(oc + 1) * TT]
                           if oc < 16 else
                           WK[:, 2 * SLOT:3 * SLOT].bitcast(BF16)[:, (oc - 16) * TT:(oc - 15) * TT]))
        glayer(32, 8, W2BB,
               lambda kc: WK[:, 7 * SLOT:9 * SLOT].bitcast(BF16)[:, kc * TT:(kc + 1) * TT],
               lambda kc: (WK[:, 9 * SLOT:10 * SLOT].bitcast(BF16)[:, kc * TT:(kc + 1) * TT]
                           if kc < 16 else
                           WK[:, 2 * SLOT:3 * SLOT].bitcast(BF16)[:, (kc - 16) * TT:(kc - 15) * TT]),
               None, None)
        P.barrier()

    for name, (dd, src) in DBG.items():
        dma("gpsimd", dd, src)

    P.wait("sync", P.tok("dG"))
    P.wait("sync", P.tok("dS"))
    P.wait("sync", P.tok("dA"))

    block = E(nc.Block())
    P.run(block)
    st.close()
    return nc


def kernel(**inputs):
    x = np.asarray(inputs["L0_attn_x"], np.float32)
    w_tm = np.ascontiguousarray(np.asarray(inputs["L0_attn_W_tm"], np.float32))
    td = np.asarray(inputs["L0_attn_time_decay"], np.float32)
    tf = np.asarray(inputs["L0_attn_time_first"], np.float32)
    g1 = np.asarray(inputs["L0_attn_g1"], np.float32)
    b1 = np.asarray(inputs["L0_attn_b1"], np.float32)
    tmk = np.asarray(inputs["L0_attn_tmk"], np.float32).reshape(-1)
    tmv = np.asarray(inputs["L0_attn_tmv"], np.float32).reshape(-1)
    tmr = np.asarray(inputs["L0_attn_tmr"], np.float32).reshape(-1)
    wcm = np.ascontiguousarray(np.asarray(inputs["L0_attn_W_cm"], np.float32))
    wcp = np.ascontiguousarray(np.asarray(inputs["L0_attn_W_cp"], np.float32))
    g2 = np.asarray(inputs["L0_attn_g2"], np.float32)
    b2 = np.asarray(inputs["L0_attn_b2"], np.float32)
    cmk = np.asarray(inputs["L0_attn_cmk"], np.float32).reshape(-1)
    cmr = np.asarray(inputs["L0_attn_cmr"], np.float32).reshape(-1)
    l0w = np.asarray(inputs["L0_ffn_weight"], np.float32)
    l0b = np.asarray(inputs["L0_ffn_bias"], np.float32)
    wq = np.asarray(inputs["L1_attn_Wq"], np.float32)
    bq = np.asarray(inputs["L1_attn_bq"], np.float32)
    wk = np.asarray(inputs["L1_attn_Wk"], np.float32)
    bk = np.asarray(inputs["L1_attn_bk"], np.float32)
    wv = np.asarray(inputs["L1_attn_Wv"], np.float32)
    bv = np.asarray(inputs["L1_attn_bv"], np.float32)
    k0 = np.asarray(inputs["L1_ffn_k0"]).astype(np.uint8)
    k1 = np.ascontiguousarray(np.asarray(inputs["L1_ffn_k1"]).astype(np.uint8))
    k2 = np.ascontiguousarray(np.asarray(inputs["L1_ffn_k2"]).astype(np.uint8))
    s0 = np.float32(np.asarray(inputs["L1_ffn_s0"]))
    s1 = np.float32(np.asarray(inputs["L1_ffn_s1"]))
    s2 = np.float32(np.asarray(inputs["L1_ffn_s2"]))

    iotag = np.broadcast_to(np.arange(L, dtype=np.float32), (128, L)).copy()
    idn = np.eye(128, dtype=np.float32)
    idnb = np.eye(128, dtype=ml_dtypes.bfloat16)
    eye_mask = (1.0 - np.eye(128)).astype(np.float32)
    ones_mask = np.ones((128, 128), np.float32)
    scl = np.array([[s0, s1, s2, 0.0]], np.float32)

    in_maps = []
    for c in range(8):
        b = c // 2
        half = c % 2
        perm = np.arange(C) if half == 0 else np.r_[512:1024, 0:512]
        xb = np.ascontiguousarray(x[b].T)
        d = {
            "x0f": np.ascontiguousarray(xb[perm]),
            "x0own": np.ascontiguousarray(xb[:, half * TO:(half + 1) * TO]),
            "wtmc": np.ascontiguousarray(np.concatenate(
                [w_tm[perm][:, half * HC:(half + 1) * HC],
                 w_tm[perm][:, C + half * HC:C + (half + 1) * HC],
                 w_tm[perm][:, 2 * C + half * HC:2 * C + (half + 1) * HC]], axis=1)),
            "vecw": np.ascontiguousarray(np.stack(
                [td[half * HC:(half + 1) * HC], tf[half * HC:(half + 1) * HC],
                 tmk[half * HC:(half + 1) * HC], tmv[half * HC:(half + 1) * HC],
                 tmr[half * HC:(half + 1) * HC]], axis=1)),
            "vecp": np.ascontiguousarray(np.stack([g1[perm], b1[perm]], axis=1)),
            "vecn": np.ascontiguousarray(np.stack([g2, b2, cmk, cmr, l0b, bq, bk, bv], axis=1)),
            "scl": scl,
            "wcm": wcm,
            "wcp": wcp,
            "l0wt": np.ascontiguousarray(l0w.T),
            "wqt": np.ascontiguousarray(wq.T),
            "wkt": np.ascontiguousarray(wk.T),
            "wvt": np.ascontiguousarray(wv.T),
            "k0a": np.ascontiguousarray(k0[:C]),
            "k0b": np.ascontiguousarray(k0[C:]),
            "k1u": k1,
            "k2u": k2,
            "iotag": iotag,
            "idn": idn,
            "idnb": idnb,
            "mske": eye_mask if half == 0 else ones_mask,
            "msko": ones_mask if half == 0 else eye_mask,
            "sel": np.full((128, 1), float(half), np.float32),
        }
        in_maps.append(d)

    nc = build(taps=tuple(TAPS))
    res = run_bass_kernel_spmd(nc, in_maps, core_ids=list(range(8)))
    out = np.empty((4, T, C), np.float32)
    for c in range(8):
        b = c // 2
        half = c % 2
        out[b, half * TO:(half + 1) * TO, :] = res.results[c]["out"]
    if TAPS:
        kernel.dbg = [{k: v for k, v in r.items() if k.startswith("dbg_")}
                      for r in res.results]
    return out


# version 09
# speedup vs baseline: 1.0556x; 1.0328x over previous
import os
os.environ.setdefault("TRN_ALLOW_ICI_ALL_TO_ALL", "1")
import numpy as np
import ml_dtypes
import concourse.bass as bass
import concourse.mybir as mybir
from concourse.bass_utils import run_bass_kernel_spmd
from contextlib import ExitStack

F32 = mybir.dt.float32
BF16 = mybir.dt.bfloat16
U8 = mybir.dt.uint8
I32 = mybir.dt.int32
AF = mybir.ActivationFunctionType
ALU = mybir.AluOpType

C = 1024
T = 8192
TO = 4096
HC = 512
NCH = 8
HCH = 4
L = 64
SG = 4096
NSEG = 2
TT = 512
NTT = T // TT
NTO = TO // TT
NEG = -1e38
EPS = 1e-8
LN_EPS = 1e-5
AX = None

TAPS = []


class Prog:
    def __init__(self, nc):
        self.nc = nc
        self.ops = {e: [] for e in ("sync", "gpsimd", "vector", "scalar", "tensor")}
        self.cnt = {}
        self.hw = {e: {} for e in self.ops}
        self.sems = {}

    def sem(self, stack, name):
        h = stack.enter_context(self.nc.semaphore(name))
        self.sems[name] = h
        self.cnt[name] = 0
        return h

    def bump(self, sem_name, k):
        self.cnt[sem_name] += k
        return (sem_name, self.cnt[sem_name])

    def tok(self, sem_name):
        return (sem_name, self.cnt[sem_name])

    def wait(self, eng, tokv):
        if tokv is None:
            return
        sem_name, val = tokv
        if val <= 0:
            return
        if self.hw[eng].get(sem_name, -1) >= val:
            return
        self.hw[eng][sem_name] = val
        h = self.sems[sem_name]
        self.ops[eng].append(lambda e, h=h, v=val: e.wait_ge(h, v))

    def op(self, eng, fn, inc=None):
        if inc is None:
            self.ops[eng].append(fn)
            return None
        sem_name, k = inc
        h = self.sems[sem_name]
        self.ops[eng].append(lambda e, fn=fn, h=h, k=k: fn(e).then_inc(h, k))
        return self.bump(sem_name, k)

    def retro_inc(self, eng, sem_name, k=1):
        old = self.ops[eng].pop()
        h = self.sems[sem_name]
        self.ops[eng].append(lambda e, f=old, h=h, k=k: f(e).then_inc(h, k))
        return self.bump(sem_name, k)

    def barrier(self):
        toks = [(s, self.cnt[s]) for s in self.cnt]
        for eng in self.ops:
            for t in toks:
                self.wait(eng, t)

    def run(self, block):
        for eng in ("sync", "gpsimd", "vector", "scalar", "tensor"):
            ops = self.ops[eng]
            dec = getattr(block, eng)

            def make(ops):
                def body(e):
                    for fn in ops:
                        fn(e)
                return body
            dec(make(ops))


def build(taps=()):
    global AX
    AX = mybir.AxisListType.X
    nc = bass.Bass("TRN2", target_bir_lowering=False)

    def din(name, shape, dtype=F32):
        return nc.dram_tensor(name, shape, dtype, kind="ExternalInput").ap()

    def dout(name, shape, dtype=F32):
        return nc.dram_tensor(name, shape, dtype, kind="ExternalOutput").ap()

    def dtmp(name, shape, dtype=F32):
        return nc.dram_tensor(name, shape, dtype).ap()

    X0F = din("x0f", [C, T])
    X0O = din("x0own", [C, TO])
    WTMC = din("wtmc", [C, 3 * HC])
    VECW = din("vecw", [HC, 5])
    VECP = din("vecp", [C, 2])
    VECN = din("vecn", [C, 8])
    SCL = din("scl", [1, 4])
    WCM = din("wcm", [C, 4 * C])
    WCP = din("wcp", [4 * C, C])
    L0WT = din("l0wt", [C, C])
    WQT = din("wqt", [C, C])
    WKT = din("wkt", [C, C])
    WVT = din("wvt", [C, C])
    K0A = din("k0a", [C, 4 * C], U8)
    K0B = din("k0b", [C, 4 * C], U8)
    K1U = din("k1u", [4 * C, 4 * C], U8)
    K2U = din("k2u", [4 * C, C], U8)
    IOTAG = din("iotag", [128, L])
    IDN = din("idn", [128, 128])
    IDNB = din("idnb", [128, 128], BF16)
    MSKE = din("mske", [128, 128])
    MSKO = din("msko", [128, 128])
    SELI = din("sel", [128, 1])

    OUT = dout("out", [TO, C])

    ZOWN = dtmp("zown", [HC, T + 1])
    Z2H = dtmp("z2h", [C, T], BF16)
    Z2L = dtmp("z2l", [C, T], BF16)
    X1IN = dtmp("x1in", [HC, T])
    X1OUT = dtmp("x1out", [C, T])
    RRES = dtmp("rres", [C, TO])
    ARI = dtmp("ari", [1, 8])
    ARO = dtmp("aro", [1, 8])
    WCMHB = dtmp("wcmhb", [32, 128, NCH * 128], BF16)
    WCMLB = dtmp("wcmlb", [32, 128, NCH * 128], BF16)
    WCPHB = dtmp("wcphb", [NCH, 128, 32 * 128], BF16)
    WCPLB = dtmp("wcplb", [NCH, 128, 32 * 128], BF16)
    QWTB = dtmp("qwtb", [NCH, 128, NCH * 128], BF16)
    PRJ = {}
    for nm in ("wq", "wk", "wv"):
        PRJ[nm + "h"] = dtmp(nm + "hb", [NCH, 128, NCH * 128], BF16)
        PRJ[nm + "l"] = dtmp(nm + "lb", [NCH, 128, NCH * 128], BF16)
    D0B = dtmp("d0b", [32, 128, NCH * 128], BF16)
    W1BB = dtmp("w1bb", [32, 128, 32 * 128], BF16)
    W2BB = dtmp("w2bb", [NCH, 128, 32 * 128], BF16)
    XCH = dtmp("xch", [2 * C, TO], BF16)
    XCG = dtmp("xcg", [4 * C, TO], BF16)
    QSH = dtmp("qsh", [C, TO], BF16)
    QSL = dtmp("qsl", [C, TO], BF16)
    KSH = dtmp("ksh", [C, T], BF16)
    KSL = dtmp("ksl", [C, T], BF16)
    VTH = dtmp("vth", [T, C], BF16)
    VTL = dtmp("vtl", [T, C], BF16)

    DBG = {}

    st = ExitStack()
    E = st.enter_context

    SLOT = 4096
    NSLOT = 10
    WK = E(nc.sbuf_tensor("wk", [128, NSLOT * SLOT], F32))
    ONES0 = E(nc.sbuf_tensor("ones0", [128, SG], BF16))
    VSB = E(nc.sbuf_tensor("vsb", [128, 192], F32))
    IOT = E(nc.sbuf_tensor("iot", [128, L], F32))
    IDNS = E(nc.sbuf_tensor("idns", [128, 128], F32))
    IDNBS = E(nc.sbuf_tensor("idnbs", [128, 128], BF16))
    MSKES = E(nc.sbuf_tensor("mskes", [128, 128], F32))
    MSKOS = E(nc.sbuf_tensor("mskos", [128, 128], F32))
    SELS = E(nc.sbuf_tensor("sels", [128, 1], F32))
    ONEB = E(nc.sbuf_tensor("oneb", [128, 1], BF16))
    ONER = E(nc.sbuf_tensor("oner", [1, 128], F32))
    SCLS = E(nc.sbuf_tensor("scls", [1, 4], F32))
    SM = E(nc.sbuf_tensor("smx", [128, 1024], F32))
    SR = E(nc.sbuf_tensor("srx", [1, 1024], F32))
    WTB = E(nc.sbuf_tensor("wtb", [128, 6 * 1024], BF16))
    WTS = None
    STR = E(nc.sbuf_tensor("strm", [128, 4096], F32))
    WTS = STR
    PS = [E(nc.psum_tensor("ps%d" % i, [128, 512], F32)) for i in range(4)]
    PB0 = E(nc.psum_tensor("pb0", [128, 512], F32))
    PB1 = E(nc.psum_tensor("pb1", [128, 512], F32))
    PROW = E(nc.psum_tensor("prow", [1, 512], F32))
    PT = E(nc.psum_tensor("ptx", [128, 128], F32))
    PTB = PT[:].bitcast(BF16)[:, 0:128]

    P = Prog(nc)
    for s_ in ("dS", "dG", "dA", "tS", "vS", "sS", "cS"):
        P.sem(st, s_)

    def slot(i, lo=0, hi=SLOT):
        return WK[:, i * SLOT + lo: i * SLOT + hi]

    def slotb(i, lo, hi):
        return WK[:, i * SLOT: (i + 1) * SLOT].bitcast(BF16)[:, lo:hi]

    def sloti(i, lo, hi):
        return WK[:, i * SLOT + lo: i * SLOT + hi].bitcast(I32)

    def strb(lo, hi):
        return STR.bitcast(BF16)[:, lo:hi]

    def stru(lo, hi):
        return STR.bitcast(U8)[:, lo:hi]

    def dma(q, out, in_, pre=None, ncok=False):
        sem = {"sync": "dS", "gpsimd": "dG", "scalar": "dA"}[q]
        if pre is not None:
            for t_ in (pre if isinstance(pre, list) else [pre]):
                P.wait(q, t_)
        if ncok:
            return P.op(q, lambda e, o=out, i=in_: e.dma_start(
                out=o, in_=i, allow_slow_non_contiguous=True), inc=(sem, 16))
        return P.op(q, lambda e, o=out, i=in_: e.dma_start(out=o, in_=i), inc=(sem, 16))

    def snap(name, src, shape, dtype=F32):
        if name not in taps:
            return
        dd = dout("dbg_" + name, shape, dtype)
        P.op("vector", lambda e: e.engine_nop(), inc=("vS", 1))
        P.op("scalar", lambda e: e.activation(SM[:, 815:816], SM[:, 815:816], AF.Square),
             inc=("sS", 1))
        P.barrier()
        dma("gpsimd", dd, src)
        P.barrier()

    bank_ev = {}
    LG = {}

    def bank_wait(b):
        P.wait("tensor", bank_ev.get(b))

    def bank_done(b, tok):
        bank_ev[b] = tok

    def VF(fn):
        t_ = P.op("vector", fn, inc=("vS", 1))
        P.wait("vector", t_)
        return t_

    VS = {}
    _vn = [0]

    def vreg(name, n):
        VS[name] = (_vn[0], n)
        _vn[0] += n

    for nm in ("w", "u", "tmk", "tmv", "tmr", "om_tmk", "om_tmv", "om_tmr", "ew", "w64"):
        vreg(nm, HCH)
    for nm in ("g1", "b1"):
        vreg(nm, NCH)
    for nm in ("g2", "b2", "cmk", "cmr", "om_cmk", "om_cmr", "l0b", "bq", "bk", "bv"):
        vreg(nm, NCH)
    for nm in ("one", "qmaxc", "kmaxc", "mbc", "wsc", "iscale", "wsis", "csb",
               "negmb", "invb", "wsisb", "csbb", "thb", "thnb", "epsc"):
        vreg(nm, 1)
    assert _vn[0] <= 192

    def vg(name, ch=0):
        b, n = VS[name]
        assert ch < n, name
        return VSB[:, b + ch: b + ch + 1]

    def vc(name):
        b, n = VS[name]
        return VSB[0:1, b: b + 1]

    for ap_, src in ((IOT[:], IOTAG), (IDNS[:], IDN), (IDNBS[:], IDNB),
                     (MSKES[:], MSKE), (MSKOS[:], MSKO), (SELS[:], SELI), (SCLS[:], SCL)):
        dma("sync", ap_, src)
    for ci in range(HCH):
        for vi, nm in enumerate(("w", "u", "tmk", "tmv", "tmr")):
            dma("sync", vg(nm, ci), VECW[ci * 128:(ci + 1) * 128, vi:vi + 1], ncok=True)
    for ci in range(NCH):
        for vi, nm in enumerate(("g1", "b1")):
            dma("sync", vg(nm, ci), VECP[ci * 128:(ci + 1) * 128, vi:vi + 1], ncok=True)
        for vi, nm in enumerate(("g2", "b2", "cmk", "cmr", "l0b", "bq", "bk", "bv")):
            dma("sync", vg(nm, ci), VECN[ci * 128:(ci + 1) * 128, vi:vi + 1], ncok=True)
    t_all = P.tok("dS")
    P.wait("vector", t_all)
    P.wait("scalar", t_all)
    P.op("vector", lambda e: e.memset(vg("one"), 1.0))
    P.op("vector", lambda e: e.memset(vg("epsc"), LN_EPS))
    P.op("vector", lambda e: e.memset(SM[:, 815:816], 0.0))
    P.op("vector", lambda e: e.memset(ONEB[:], 1.0))
    P.op("vector", lambda e: e.memset(ONER[:], 1.0))
    P.op("vector", lambda e: e.memset(ONES0[:], 1.0))
    P.op("vector", lambda e: e.memset(ONES0[:, 0:SG:L], 0.0))
    for nm in ("tmk", "tmv", "tmr"):
        for ci in range(HCH):
            P.op("vector", lambda e, o=vg("om_" + nm, ci), i=vg(nm, ci):
                 e.tensor_scalar(o, i, -1.0, 1.0, ALU.mult, ALU.add))
    for nm in ("cmk", "cmr"):
        for ci in range(NCH):
            P.op("vector", lambda e, o=vg("om_" + nm, ci), i=vg(nm, ci):
                 e.tensor_scalar(o, i, -1.0, 1.0, ALU.mult, ALU.add))
    for ci in range(HCH):
        P.op("scalar", lambda e, o=vg("ew", ci), i=vg("w", ci): e.activation(o, i, AF.Exp),
             inc=("sS", 1))
        P.op("vector", lambda e, o=vg("w64", ci), i=vg("w", ci):
             e.tensor_scalar(o, i, float(L), None, ALU.mult))
    VF(lambda e: e.tensor_tensor(vc("csb"), SCLS[0:1, 0:1], SCLS[0:1, 1:2], ALU.mult))
    VF(lambda e: e.tensor_tensor(vc("csb"), vc("csb"), SCLS[0:1, 2:3], ALU.mult))
    tv0 = VF(lambda e: e.tensor_scalar(vc("csb"), vc("csb"), 8.0, None, ALU.mult))

    def bcast_cell(src_cell, dst_col, pre_tok):
        P.wait("tensor", pre_tok)
        bank_wait("PT")
        tt_ = P.op("tensor", lambda e, s=src_cell: e.matmul(PT[:, 0:1], ONER[:], s,
                                                            start=True, stop=True),
                   inc=("tS", 1))
        P.wait("vector", tt_)
        tok = P.op("vector", lambda e, d=dst_col: e.tensor_copy(d, PT[:, 0:1]), inc=("vS", 1))
        bank_done("PT", tok)
        return tok

    t_csbb = bcast_cell(vc("csb"), vg("csbb"), tv0)
    P.barrier()

    def stats_pass(src_fn, n_sl, sq_fn, ready_tok):
        for ci in range(n_sl):
            P.wait("tensor", ready_tok)
            if ci == 0:
                bank_wait("PROW")
            P.op("tensor", lambda e, r=src_fn(ci), st_=(ci == 0), sp=(ci == n_sl - 1):
                 e.matmul(PROW[0:1, :], vg("one"), r, start=st_, stop=sp),
                 inc=("tS", 1) if ci == n_sl - 1 else None)
        tsum = P.tok("tS")
        P.wait("vector", tsum)
        tv = P.op("vector", lambda e: e.tensor_scalar(SR[0:1, 0:512], PROW[0:1, :],
                                                      1.0 / C, None, ALU.mult), inc=("vS", 1))
        fin = None
        for ci in range(n_sl):
            P.wait("scalar", ready_tok)
            tsq = P.op("scalar", lambda e, o=sq_fn(ci), i=src_fn(ci):
                       e.activation(o, i, AF.Square), inc=("sS", 1))
            P.wait("tensor", tsq)
            if ci == 0:
                P.wait("tensor", tv)
            fin = P.op("tensor", lambda e, r=sq_fn(ci), st_=(ci == 0), sp=(ci == n_sl - 1):
                       e.matmul(PROW[0:1, :], vg("one"), r, start=st_, stop=sp),
                       inc=("tS", 1) if ci == n_sl - 1 else None)
        return fin

    def stats_finalize(tok):
        P.wait("vector", tok)
        P.op("vector", lambda e: e.tensor_scalar(SM[0:1, 0:512], PROW[0:1, :],
                                                 1.0 / C, None, ALU.mult))
        P.op("vector", lambda e: e.tensor_tensor(SR[0:1, 512:1024], SR[0:1, 0:512],
                                                 SR[0:1, 0:512], ALU.mult))
        tv = P.op("vector", lambda e: e.tensor_tensor(SM[0:1, 0:512], SM[0:1, 0:512],
                                                      SR[0:1, 512:1024], ALU.subtract),
                  inc=("vS", 1))
        tv = P.op("vector", lambda e: e.tensor_scalar(SM[0:1, 0:512], SM[0:1, 0:512],
                                                      LN_EPS, None, ALU.add), inc=("vS", 1))
        bank_done("PROW", tv)
        P.wait("scalar", tv)
        ts_ = P.op("scalar", lambda e: e.activation(SM[0:1, 0:512], SM[0:1, 0:512], AF.Sqrt),
                   inc=("sS", 1))
        P.wait("vector", ts_)
        return P.op("vector", lambda e: e.reciprocal(SR[0:1, 512:1024], SM[0:1, 0:512]),
                    inc=("vS", 1))

    def stats_bcast(tok):
        P.wait("tensor", tok)
        bank_wait("PB0")
        bank_wait("PB1")
        P.op("tensor", lambda e: e.matmul(PB0[:], ONER[:], SR[0:1, 0:512], start=True, stop=True))
        return P.op("tensor", lambda e: e.matmul(PB1[:], ONER[:], SR[0:1, 512:1024],
                                                 start=True, stop=True), inc=("tS", 1))

    for tt in range(NTT):
        c0 = tt * TT
        P.wait("vector", P.tok("dG"))
        tl = None
        for ci in range(NCH):
            tl = dma("sync", slot(0, ci * TT, (ci + 1) * TT),
                     X0F[ci * 128:(ci + 1) * 128, c0:c0 + TT], pre=P.tok("vS"))
        tok = stats_pass(lambda ci: slot(0, ci * TT, (ci + 1) * TT), NCH,
                         lambda ci: slot(2, ci * TT, (ci + 1) * TT), tl)
        tok = stats_finalize(tok)
        tbc = stats_bcast(tok)
        P.wait("vector", tbc)
        for ci in range(NCH):
            P.op("vector", lambda e, ci=ci: e.tensor_tensor(
                slot(1, ci * TT, (ci + 1) * TT), slot(0, ci * TT, (ci + 1) * TT),
                PB0[:], ALU.subtract))
            P.op("vector", lambda e, ci=ci: e.tensor_tensor(
                slot(1, ci * TT, (ci + 1) * TT), slot(1, ci * TT, (ci + 1) * TT),
                PB1[:], ALU.mult))
        tz = P.op("vector", lambda e: e.engine_nop(), inc=("vS", 1))
        bank_done("PB0", tz)
        bank_done("PB1", tz)
        for ci in range(HCH):
            dma("gpsimd", ZOWN[ci * 128:(ci + 1) * 128, c0 + 1:c0 + TT + 1],
                slot(1, ci * TT, (ci + 1) * TT), pre=tz)
            if tt == 0:
                dma("gpsimd", ZOWN[ci * 128:(ci + 1) * 128, 0:1],
                    slot(1, ci * TT, ci * TT + 1), pre=tz, ncok=True)
        tok = stats_pass(lambda ci: slot(1, ci * TT, (ci + 1) * TT), NCH,
                         lambda ci: slot(2, ci * TT, (ci + 1) * TT), tz)
        tok = stats_finalize(tok)
        tbc = stats_bcast(tok)
        P.wait("vector", tbc)
        tvlast = None
        for ci in range(NCH):
            sz = slot(1, ci * TT, (ci + 1) * TT)
            s2_ = slot(2, ci * TT, (ci + 1) * TT)
            P.op("vector", lambda e, a=s2_, b=sz: e.tensor_tensor(a, b, PB0[:], ALU.subtract))
            P.op("vector", lambda e, a=s2_: e.tensor_tensor(a, a, PB1[:], ALU.mult))
            P.op("vector", lambda e, a=s2_, g=vg("g1", ci), b=vg("b1", ci):
                 e.tensor_scalar(a, a, g, b, ALU.mult, ALU.add))
            bh = slotb(3, ci * TT, (ci + 1) * TT)
            bl = slotb(3, SLOT + ci * TT, SLOT + (ci + 1) * TT)
            P.op("vector", lambda e, o=bh, i=s2_: e.tensor_copy(o, i))
            P.op("vector", lambda e, o=s2_, i=bh: e.tensor_tensor(o, o, i, ALU.subtract))
            tvlast = P.op("vector", lambda e, o=bl, i=s2_: e.tensor_copy(o, i), inc=("vS", 1))
            dma("gpsimd", Z2H[ci * 128:(ci + 1) * 128, c0:c0 + TT], bh, pre=tvlast)
            dma("gpsimd", Z2L[ci * 128:(ci + 1) * 128, c0:c0 + TT], bl, pre=tvlast)
        bank_done("PB0", tvlast)
        bank_done("PB1", tvlast)
    P.barrier()
    snap("z2h", Z2H, [C, T], BF16)
    snap("zown", ZOWN, [HC, T + 1])

    def wkv_chain(ci, s):
        dbg0 = (ci == 0 and s == 0)

        def csnap(name, src, shape, dtype=F32):
            if dbg0:
                snap(name, src, shape, dtype)

        c0_ = ci * 4
        CA = SM[:, c0_ + 0:c0_ + 1]
        CB = SM[:, c0_ + 1:c0_ + 2]
        CAo = SM[:, c0_ + 2:c0_ + 3]
        CBo = SM[:, c0_ + 3:c0_ + 4]
        smb = 16
        CM = SM[:, smb:smb + L]
        D0C = SM[:, smb + L:smb + 2 * L]
        D1C = SM[:, smb + 2 * L:smb + 3 * L]
        DH = SM[:, smb + 3 * L:smb + 4 * L]
        DC = SM[:, smb + 4 * L:smb + 5 * L]
        MC = SM[:, smb + 5 * L:smb + 6 * L]
        PG = SM[:, smb + 6 * L:smb + 7 * L]
        RE = SM[:, smb + 7 * L:smb + 8 * L]
        RC = SM[:, smb + 8 * L:smb + 9 * L]
        AC = SM[:, smb + 9 * L:smb + 10 * L]
        DT = SM[:, smb + 10 * L:smb + 11 * L]
        T1 = SM[:, smb + 11 * L:smb + 12 * L]
        U5 = SM[:, smb + 12 * L:smb + 12 * L + 1]
        k = slot(0, 0, SG); v = slot(1, 0, SG); r = slot(2, 0, SG)
        zx = slot(3, 0, SG)
        w_, u_, w64 = vg("w", ci), vg("u", ci), vg("w64", ci)

        def V(fn, inc=None):
            return P.op("vector", fn, inc=inc)

        def Sc(fn, inc=None):
            return P.op("scalar", fn, inc=inc)

        def vsync():
            tv_ = P.retro_inc("vector", "vS")
            P.wait("scalar", tv_)
            return tv_

        def ssync():
            ts_ = P.retro_inc("scalar", "sS")
            P.wait("vector", ts_)
            return ts_

        def SCN(fn):
            t_ = P.op("vector", fn, inc=("vS", 1))
            P.wait("vector", t_)
            return t_

        if s == 0:
            VF(lambda e: e.memset(CA, 0.0))
            VF(lambda e: e.memset(CB, NEG))
        else:
            VF(lambda e: e.tensor_copy(CA, CAo))
            VF(lambda e: e.tensor_copy(CB, CBo))
        csnap("carry0", SM[:, 0:16], [128, 16])
        vsync()
        Sc(lambda e: e.activation(slot(4, 0, SG), k, AF.Exp))
        ssync()
        V(lambda e: e.tensor_copy(slot(5, 0, SG), vg("ew", ci).broadcast_to((128, SG))))
        V(lambda e: e.memset(slot(5, 0, SG)[:, 0:SG:L], 0.0))
        SCN(lambda e: e.tensor_tensor_scan(slot(6, 0, SG), slot(5, 0, SG), slot(4, 0, SG),
                                         0.0, ALU.mult, ALU.add))
        Dloc = slot(6, 0, SG)
        V(lambda e: e.tensor_copy(DT, Dloc[:, L - 1:SG:L]))
        csnap("dloc0", Dloc, [128, SG])
        V(lambda e: e.tensor_scalar(T1, IOT[:], w64, None, ALU.mult))
        V(lambda e: e.tensor_scalar(T1, T1, 0.0, None, ALU.max))
        V(lambda e: e.tensor_scalar(CM, IOT[:], w64, w64, ALU.mult, ALU.add))
        V(lambda e: e.tensor_scalar(CM, CM, CB, None, ALU.add))
        V(lambda e: e.tensor_tensor(CM, CM, T1, ALU.max))
        V(lambda e: e.tensor_tensor(D0C[:, 1:L], CM[:, 0:L - 1], CM[:, 1:L], ALU.subtract))
        V(lambda e: e.tensor_scalar(D0C[:, 1:L], D0C[:, 1:L], w64, None, ALU.add))
        VF(lambda e: e.tensor_scalar(U5, CB, w64, None, ALU.add))
        VF(lambda e: e.tensor_tensor(U5, U5, CM[:, 0:1], ALU.subtract))
        V(lambda e: e.tensor_scalar(D1C, CM, -1.0, None, ALU.mult))
        vsync()
        Sc(lambda e: e.activation(D0C[:, 1:L], D0C[:, 1:L], AF.Exp))
        Sc(lambda e: e.activation(D1C, D1C, AF.Exp))
        Sc(lambda e: e.activation(U5, U5, AF.Exp))
        ssync()
        VF(lambda e: e.memset(D0C[:, 0:1], 0.0))
        V(lambda e: e.tensor_tensor(D1C, D1C, DT, ALU.mult))
        VF(lambda e: e.tensor_tensor(D1C[:, 0:1], D1C[:, 0:1], U5, ALU.add))
        SCN(lambda e: e.tensor_tensor_scan(DH, D0C, D1C, 0.0, ALU.mult, ALU.add))
        V(lambda e: e.tensor_copy(DC[:, 1:L], DH[:, 0:L - 1]))
        VF(lambda e: e.memset(DC[:, 0:1], 1.0))
        V(lambda e: e.tensor_copy(MC[:, 1:L], CM[:, 0:L - 1]))
        VF(lambda e: e.tensor_copy(MC[:, 0:1], CB))
        csnap("sm0", SM[:, 0:1024], [128, 1024])
        V(lambda e: e.tensor_copy(slot(5, 0, SG), w_.broadcast_to((128, SG))))
        V(lambda e: e.tensor_copy(slot(5, 0, SG)[:, 0:SG:L], MC))
        SCN(lambda e: e.tensor_tensor_scan(slot(7, 0, SG), ONES0[:], slot(5, 0, SG),
                                         0.0, ALU.mult, ALU.add))
        V(lambda e: e.memset(slot(5, 0, SG), 0.0))
        V(lambda e: e.tensor_copy(slot(5, 0, SG)[:, 0:SG:L], DC))
        csnap("dstage0", slot(5, 0, SG), [128, SG])
        csnap("dcsm0", SM[:, 0:1024], [128, 1024])
        SCN(lambda e: e.tensor_tensor_scan(slot(4, 0, SG), ONES0[:], slot(5, 0, SG),
                                         0.0, ALU.mult, ALU.add))
        V(lambda e: e.tensor_scalar(slot(5, 0, SG), k, u_, None, ALU.add))
        V(lambda e: e.tensor_tensor(slot(5, 0, SG), slot(7, 0, SG), slot(5, 0, SG), ALU.max))
        V(lambda e: e.tensor_scalar(slot(5, 0, SG), slot(5, 0, SG), 0.0, None, ALU.max))
        csnap("dcrep0", slot(4, 0, SG), [128, SG])
        csnap("arep0", slot(7, 0, SG), [128, SG])
        csnap("qq0", slot(5, 0, SG), [128, SG])
        V(lambda e: e.tensor_copy(slot(8, 0, SG)[:, 1:SG], Dloc[:, 0:SG - 1]))
        V(lambda e: e.memset(slot(8, 0, SG)[:, 0:SG:L], 0.0))
        V(lambda e: e.tensor_tensor(slot(7, 0, SG), slot(7, 0, SG), slot(5, 0, SG), ALU.subtract))
        V(lambda e: e.tensor_scalar(slot(9, 0, SG), slot(5, 0, SG), -1.0, None, ALU.mult))
        vsync()
        Sc(lambda e: e.activation(slot(7, 0, SG), slot(7, 0, SG), AF.Exp))
        Sc(lambda e: e.activation(slot(9, 0, SG), slot(9, 0, SG), AF.Exp))
        ssync()
        csnap("expaq0", slot(7, 0, SG), [128, SG])
        V(lambda e: e.tensor_tensor(slot(7, 0, SG), slot(7, 0, SG), slot(4, 0, SG), ALU.mult))
        V(lambda e: e.tensor_tensor(slot(8, 0, SG), slot(8, 0, SG), slot(9, 0, SG), ALU.mult))
        V(lambda e: e.tensor_tensor(slot(7, 0, SG), slot(7, 0, SG), slot(8, 0, SG), ALU.add))
        V(lambda e: e.tensor_scalar(slot(7, 0, SG), slot(7, 0, SG), 1.1754944e-38, None,
                                    ALU.max))
        V(lambda e: e.tensor_scalar(sloti(9, 0, SG), slot(7, 0, SG).bitcast(I32),
                                    23, None, ALU.logical_shift_right))
        V(lambda e: e.tensor_scalar(sloti(9, 0, SG), sloti(9, 0, SG), 127, None, ALU.subtract))
        V(lambda e: e.tensor_copy(slot(8, 0, SG), sloti(9, 0, SG)))
        V(lambda e: e.tensor_scalar(sloti(7, 0, SG), slot(7, 0, SG).bitcast(I32),
                                    0x007FFFFF, 0x3F800000, ALU.bitwise_and, ALU.bitwise_or))
        vsync()
        Sc(lambda e: e.activation(slot(7, 0, SG), slot(7, 0, SG), AF.Ln))
        ssync()
        V(lambda e: e.tensor_scalar(slot(8, 0, SG), slot(8, 0, SG), 0.6931471805599453, None,
                                    ALU.mult))
        V(lambda e: e.tensor_tensor(slot(7, 0, SG), slot(7, 0, SG), slot(8, 0, SG), ALU.add))
        V(lambda e: e.tensor_tensor(slot(6, 0, SG), slot(5, 0, SG), slot(7, 0, SG), ALU.add))
        bbm1 = slot(6, 0, SG)
        csnap("bbm10", bbm1, [128, SG])

        def coeffs():
            V(lambda e: e.tensor_scalar(slot(4, 0, SG), bbm1, w_, None, ALU.add))
            V(lambda e: e.tensor_tensor(slot(5, 0, SG), slot(4, 0, SG), k, ALU.max))
            V(lambda e: e.tensor_tensor(slot(4, 0, SG), slot(4, 0, SG), slot(5, 0, SG),
                                        ALU.subtract))
            V(lambda e: e.tensor_tensor(slot(5, 0, SG), k, slot(5, 0, SG), ALU.subtract))
            vsync()
            Sc(lambda e: e.activation(slot(7, 0, SG), slot(4, 0, SG), AF.Exp))
            Sc(lambda e: e.activation(slot(8, 0, SG), slot(5, 0, SG), AF.Exp))
            ssync()

        coeffs()
        V(lambda e: e.tensor_tensor(slot(9, 0, SG), slot(7, 0, SG), slot(8, 0, SG), ALU.add))
        vsync()
        Sc(lambda e: e.activation(slot(5, 0, SG), slot(9, 0, SG), AF.Ln))
        ssync()
        V(lambda e: e.reciprocal(slot(9, 0, SG), slot(9, 0, SG)))
        V(lambda e: e.tensor_tensor(slot(4, 0, SG), slot(4, 0, SG), slot(5, 0, SG), ALU.subtract))
        SCN(lambda e: e.tensor_tensor_scan(slot(5, 0, SG), ONES0[:], slot(4, 0, SG),
                                         0.0, ALU.mult, ALU.add))
        V(lambda e: e.tensor_copy(T1, slot(5, 0, SG)[:, L - 1:SG:L]))
        vsync()
        Sc(lambda e: e.activation(PG, T1, AF.Exp))
        Sc(lambda e: e.activation(slot(5, 0, SG), slot(5, 0, SG), AF.Exp))
        ssync()
        V(lambda e: e.tensor_tensor(slot(4, 0, SG), slot(7, 0, SG), slot(9, 0, SG), ALU.mult))
        V(lambda e: e.memset(slot(4, 0, SG)[:, 0:SG:L], 0.0))
        V(lambda e: e.tensor_scalar(slot(9, 0, SG), slot(9, 0, SG), EPS, None, ALU.mult))
        SCN(lambda e: e.tensor_tensor_scan(slot(7, 0, SG), slot(4, 0, SG), slot(9, 0, SG),
                                         0.0, ALU.mult, ALU.add))
        V(lambda e: e.tensor_copy(RE, slot(7, 0, SG)[:, L - 1:SG:L]))
        SCN(lambda e: e.tensor_tensor_scan(RC, PG, RE, 0.0, ALU.mult, ALU.add))
        V(lambda e: e.tensor_copy(T1[:, 1:L], RC[:, 0:L - 1]))
        VF(lambda e: e.memset(T1[:, 0:1], 0.0))
        V(lambda e: e.memset(slot(9, 0, SG), 0.0))
        V(lambda e: e.tensor_copy(slot(9, 0, SG)[:, 0:SG:L], T1))
        SCN(lambda e: e.tensor_tensor_scan(slot(4, 0, SG), ONES0[:], slot(9, 0, SG),
                                         0.0, ALU.mult, ALU.add))
        V(lambda e: e.tensor_tensor(slot(4, 0, SG), slot(4, 0, SG), slot(5, 0, SG), ALU.mult))
        V(lambda e: e.tensor_tensor(slot(4, 0, SG), slot(4, 0, SG), slot(7, 0, SG), ALU.add))
        V(lambda e: e.tensor_copy(slot(5, 0, SG)[:, 1:SG], slot(4, 0, SG)[:, 0:SG - 1]))
        V(lambda e: e.memset(slot(5, 0, SG)[:, 0:1], 0.0))
        V(lambda e: e.tensor_scalar(slot(5, 0, SG), slot(5, 0, SG), 1.0, None, ALU.add))
        vsync()
        Sc(lambda e: e.activation(slot(5, 0, SG), slot(5, 0, SG), AF.Ln))
        ssync()
        V(lambda e: e.tensor_tensor(slot(6, 0, SG), slot(6, 0, SG), slot(5, 0, SG), ALU.add))
        csnap("bbm1c0", slot(6, 0, SG), [128, SG])
        coeffs()
        SCN(lambda e: e.tensor_tensor_scan(slot(5, 0, SG), ONES0[:], slot(4, 0, SG),
                                         0.0, ALU.mult, ALU.add))
        vsync()
        Sc(lambda e: e.activation(slot(5, 0, SG), slot(5, 0, SG), AF.Exp))
        ssync()
        V(lambda e: e.tensor_copy(PG, slot(5, 0, SG)[:, L - 1:SG:L]))
        V(lambda e: e.tensor_copy(slot(9, 0, SG)[:, 1:SG], slot(5, 0, SG)[:, 0:SG - 1]))
        V(lambda e: e.memset(slot(9, 0, SG)[:, 0:SG:L], 1.0))
        V(lambda e: e.tensor_tensor(slot(5, 0, SG), slot(8, 0, SG), v, ALU.mult))
        V(lambda e: e.tensor_copy(slot(4, 0, SG), slot(7, 0, SG)))
        V(lambda e: e.memset(slot(4, 0, SG)[:, 0:SG:L], 0.0))
        SCN(lambda e: e.tensor_tensor_scan(slot(7, 0, SG), slot(4, 0, SG), slot(5, 0, SG),
                                         0.0, ALU.mult, ALU.add))
        V(lambda e: e.tensor_copy(RE, slot(7, 0, SG)[:, L - 1:SG:L]))
        VF(lambda e: e.tensor_tensor(T1[:, 0:1], CA, PG[:, 0:1], ALU.mult))
        VF(lambda e: e.tensor_tensor(RE[:, 0:1], RE[:, 0:1], T1[:, 0:1], ALU.add))
        SCN(lambda e: e.tensor_tensor_scan(AC, PG, RE, 0.0, ALU.mult, ALU.add))
        V(lambda e: e.tensor_copy(T1[:, 1:L], AC[:, 0:L - 1]))
        VF(lambda e: e.tensor_copy(T1[:, 0:1], CA))
        V(lambda e: e.memset(slot(4, 0, SG), 0.0))
        V(lambda e: e.tensor_copy(slot(4, 0, SG)[:, 0:SG:L], T1))
        SCN(lambda e: e.tensor_tensor_scan(slot(8, 0, SG), ONES0[:], slot(4, 0, SG),
                                         0.0, ALU.mult, ALU.add))
        V(lambda e: e.tensor_copy(slot(4, 0, SG)[:, 1:SG], slot(7, 0, SG)[:, 0:SG - 1]))
        V(lambda e: e.memset(slot(4, 0, SG)[:, 0:SG:L], 0.0))
        V(lambda e: e.tensor_tensor(slot(8, 0, SG), slot(8, 0, SG), slot(9, 0, SG), ALU.mult))
        V(lambda e: e.tensor_tensor(slot(8, 0, SG), slot(8, 0, SG), slot(4, 0, SG), ALU.add))
        csnap("aam10", slot(8, 0, SG), [128, SG])
        VF(lambda e: e.tensor_scalar(T1[:, 0:1], slot(6, 0, SG)[:, SG - 1:SG], w_, None, ALU.add))
        VF(lambda e: e.tensor_tensor(T1[:, 1:2], T1[:, 0:1], k[:, SG - 1:SG], ALU.max))
        VF(lambda e: e.tensor_tensor(T1[:, 0:1], T1[:, 0:1], T1[:, 1:2], ALU.subtract))
        VF(lambda e: e.tensor_tensor(T1[:, 2:3], k[:, SG - 1:SG], T1[:, 1:2], ALU.subtract))
        vsync()
        Sc(lambda e: e.activation(T1[:, 0:1], T1[:, 0:1], AF.Exp))
        Sc(lambda e: e.activation(T1[:, 2:3], T1[:, 2:3], AF.Exp))
        ssync()
        VF(lambda e: e.tensor_tensor(CAo, T1[:, 0:1], slot(8, 0, SG)[:, SG - 1:SG], ALU.mult))
        VF(lambda e: e.tensor_tensor(T1[:, 3:4], T1[:, 2:3], v[:, SG - 1:SG], ALU.mult))
        VF(lambda e: e.tensor_tensor(CAo, CAo, T1[:, 3:4], ALU.add))
        VF(lambda e: e.tensor_tensor(T1[:, 0:1], T1[:, 0:1], T1[:, 2:3], ALU.add))
        VF(lambda e: e.tensor_scalar(T1[:, 0:1], T1[:, 0:1], EPS, None, ALU.add))
        vsync()
        Sc(lambda e: e.activation(T1[:, 0:1], T1[:, 0:1], AF.Ln))
        ssync()
        VF(lambda e: e.tensor_tensor(CBo, T1[:, 1:2], T1[:, 0:1], ALU.add))
        V(lambda e: e.tensor_scalar(slot(4, 0, SG), k, u_, None, ALU.add))
        V(lambda e: e.tensor_tensor(slot(5, 0, SG), slot(6, 0, SG), slot(4, 0, SG), ALU.max))
        V(lambda e: e.tensor_tensor(slot(9, 0, SG), slot(6, 0, SG), slot(5, 0, SG), ALU.subtract))
        V(lambda e: e.tensor_tensor(slot(4, 0, SG), slot(4, 0, SG), slot(5, 0, SG), ALU.subtract))
        vsync()
        Sc(lambda e: e.activation(slot(9, 0, SG), slot(9, 0, SG), AF.Exp))
        Sc(lambda e: e.activation(slot(4, 0, SG), slot(4, 0, SG), AF.Exp))
        Sc(lambda e: e.activation(slot(2, 0, SG), r, AF.Sigmoid))
        ssync()
        V(lambda e: e.tensor_tensor(slot(5, 0, SG), slot(9, 0, SG), slot(4, 0, SG), ALU.add))
        V(lambda e: e.tensor_scalar(slot(5, 0, SG), slot(5, 0, SG), EPS, None, ALU.add))
        V(lambda e: e.reciprocal(slot(5, 0, SG), slot(5, 0, SG)))
        V(lambda e: e.tensor_tensor(slot(9, 0, SG), slot(9, 0, SG), slot(8, 0, SG), ALU.mult))
        V(lambda e: e.tensor_tensor(slot(4, 0, SG), slot(4, 0, SG), v, ALU.mult))
        V(lambda e: e.tensor_tensor(slot(9, 0, SG), slot(9, 0, SG), slot(4, 0, SG), ALU.add))
        V(lambda e: e.tensor_tensor(slot(9, 0, SG), slot(9, 0, SG), slot(5, 0, SG), ALU.mult))
        csnap("y0", slot(9, 0, SG), [128, SG])
        csnap("rsig0", slot(2, 0, SG), [128, SG])
        V(lambda e: e.tensor_tensor(slot(9, 0, SG), slot(9, 0, SG), slot(2, 0, SG), ALU.mult))
        V(lambda e: e.tensor_tensor(slot(9, 0, SG - 1), slot(9, 0, SG - 1), zx[:, 1:SG],
                                    ALU.add))
        return V(lambda e: e.tensor_tensor(slot(9, SG - 1, SG), slot(9, SG - 1, SG),
                                           SM[:, 790:791], ALU.add), inc=("vS", 1))

    for ci in range(HCH):
        tw = None
        for sub in range(3):
            for kc in range(NCH):
                tw = dma("sync", WTS[:, sub * 1024 + kc * 128:sub * 1024 + (kc + 1) * 128],
                         WTMC[kc * 128:(kc + 1) * 128,
                              sub * HC + ci * 128:sub * HC + (ci + 1) * 128],
                         pre=P.tok("vS"))
        P.wait("vector", tw)
        P.wait("vector", P.tok("tS"))
        for sub in range(3):
            src = WTS[:, sub * 1024:(sub + 1) * 1024]
            hi = WTB[:, sub * 1024:(sub + 1) * 1024]
            lo = WTB[:, (3 + sub) * 1024:(4 + sub) * 1024]
            P.op("vector", lambda e, o=hi, i=src: e.tensor_copy(o, i))
            P.op("vector", lambda e, o=src, i=hi: e.tensor_tensor(o, o, i, ALU.subtract))
            P.op("vector", lambda e, o=lo, i=src: e.tensor_copy(o, i))
        tws = P.op("vector", lambda e: e.engine_nop(), inc=("vS", 1))
        P.wait("tensor", tws)
        for s in range(NSEG):
            P.wait("vector", P.tok("dG"))
            tz = dma("scalar", slot(3, 0, SG),
                     ZOWN[ci * 128:(ci + 1) * 128, s * SG:s * SG + SG],
                     pre=P.tok("vS"))
            tz = dma("scalar", SM[:, 790:791],
                     ZOWN[ci * 128:(ci + 1) * 128, s * SG + SG:s * SG + SG + 1],
                     pre=P.tok("vS"), ncok=True)
            for tt8 in range(8):
                gcol = s * SG + tt8 * TT
                zbuf = tt8 % 2

                def z2b(lo, hi, b2=zbuf):
                    if b2 == 0:
                        return strb(lo, hi)
                    return slotb(4, lo, hi) if hi <= 2 * SLOT else slotb(5, lo - 2 * SLOT,
                                                                         hi - 2 * SLOT)
                tz2 = None
                for kc in range(NCH):
                    tz2 = dma("sync", z2b(kc * 1024, kc * 1024 + 512),
                              Z2H[kc * 128:(kc + 1) * 128, gcol:gcol + TT],
                              pre=[LG.get(("z2", zbuf)), P.tok("vS")])
                    tz2 = dma("sync", z2b(kc * 1024 + 512, kc * 1024 + 1024),
                              Z2L[kc * 128:(kc + 1) * 128, gcol:gcol + TT],
                              pre=tz2)
                P.wait("tensor", tz2)
                for sub in range(3):
                    pb = PS[sub]
                    bank_wait(sub)
                    nmm = 0
                    for kc in range(NCH):
                        wh = WTB[:, sub * 1024 + kc * 128:sub * 1024 + (kc + 1) * 128]
                        wl = WTB[:, (3 + sub) * 1024 + kc * 128:(3 + sub) * 1024 + (kc + 1) * 128]
                        zh = z2b(kc * 1024, kc * 1024 + 512)
                        zl = z2b(kc * 1024 + 512, kc * 1024 + 1024)
                        for (a_, b_) in ((wh, zh), (wh, zl), (wl, zh)):
                            P.op("tensor", lambda e, a=a_, b=b_, p=pb, st2=(nmm == 0),
                                 sp2=(nmm == 23): e.matmul(p[:], a, b, start=st2, stop=sp2),
                                 inc=("tS", 1) if nmm == 23 else None)
                            nmm += 1
                    tmm = P.tok("tS")
                    if sub == 2:
                        LG[("z2", zbuf)] = tmm
                    P.wait("vector", tmm)
                    tev = P.op("vector", lambda e, d=slot(sub, tt8 * TT, (tt8 + 1) * TT), p=pb:
                               e.tensor_copy(d, p[:]), inc=("vS", 1))
                    bank_done(sub, tev)
            if ci == 0 and s == 0:
                snap("kraw0", slot(0, 0, SG), [128, SG])
                snap("vraw0", slot(1, 0, SG), [128, SG])
                snap("zprev0", slot(3, 0, SG), [128, SG])
                snap("vsb0", VSB[:, 0:192], [128, 192])
            P.wait("vector", tz)
            for sub, nmv, nmo in ((0, "tmk", "om_tmk"), (1, "tmv", "om_tmv"), (2, "tmr", "om_tmr")):
                P.op("vector", lambda e, a=slot(sub, 0, SG), m=vg(nmv, ci):
                     e.tensor_scalar(a, a, m, None, ALU.mult))
                if sub == 0 and ci == 0 and s == 0:
                    snap("klerp1", slot(0, 0, SG), [128, SG])
                P.op("vector", lambda e, o=vg(nmo, ci), z=slot(3, 0, SG):
                     e.tensor_scalar(slot(4, 0, SG), z, o, None, ALU.mult))
                if sub == 0 and ci == 0 and s == 0:
                    snap("klerp2", slot(4, 0, SG), [128, SG])
                P.op("vector", lambda e, a=slot(sub, 0, SG):
                     e.tensor_tensor(a, a, slot(4, 0, SG), ALU.add))
            if ci == 0 and s == 0:
                snap("kl0", slot(0, 0, SG), [128, SG])
                snap("vl0", slot(1, 0, SG), [128, SG])
            tx1 = wkv_chain(ci, s)
            dma("gpsimd", X1IN[ci * 128:(ci + 1) * 128, s * SG:(s + 1) * SG],
                slot(9, 0, SG), pre=tx1)

    P.wait("gpsimd", P.tok("dG"))
    tc1 = P.op("gpsimd", lambda e: e.collective_compute(
        "AllGather", ALU.bypass, replica_groups=[[0, 1], [2, 3], [4, 5], [6, 7]],
        ins=[X1IN.opt()], outs=[X1OUT.opt()]), inc=("cS", 1))
    tc2 = tc1

    def split_store(src_dram, row0, cols, dsth, dstl, kc_idx, ncb):
        P.wait("vector", P.tok("dG"))
        tl = dma("sync", slot(0, 0, cols), src_dram[row0:row0 + 128, 0:cols], pre=P.tok("vS"))
        P.wait("vector", tl)
        hi = slotb(3, 0, cols)
        lo = slotb(3, SLOT, SLOT + cols)
        P.op("vector", lambda e, o=hi: e.tensor_copy(o, slot(0, 0, cols)))
        P.op("vector", lambda e, i=hi: e.tensor_tensor(slot(0, 0, cols), slot(0, 0, cols), i,
                                                       ALU.subtract))
        tv = P.op("vector", lambda e, o=lo: e.tensor_copy(o, slot(0, 0, cols)), inc=("vS", 1))
        for ob in range(ncb):
            dma("gpsimd", dsth[ob][:, kc_idx * 128:(kc_idx + 1) * 128],
                hi[:, ob * 128:(ob + 1) * 128], pre=tv)
            dma("gpsimd", dstl[ob][:, kc_idx * 128:(kc_idx + 1) * 128],
                lo[:, ob * 128:(ob + 1) * 128], pre=tv)

    for kc in range(NCH):
        split_store(WCM, kc * 128, 4096, WCMHB, WCMLB, kc, 32)
    for kc in range(32):
        split_store(WCP, kc * 128, 1024, WCPHB, WCPLB, kc, 8)
    for nm, src in (("wq", WQT), ("wk", WKT), ("wv", WVT)):
        for kc in range(NCH):
            split_store(src, kc * 128, 1024, PRJ[nm + "h"], PRJ[nm + "l"], kc, 8)
    tvw = None
    for kc in range(NCH):
        tl = dma("sync", slot(0, 0, 1024), L0WT[kc * 128:(kc + 1) * 128, :], pre=P.tok("vS"))
        P.wait("vector", tl)
        P.op("vector", lambda e: e.scalar_tensor_tensor(slot(1, 0, 1024), slot(0, 0, 1024),
                                                        -1.0, slot(0, 0, 1024),
                                                        ALU.mult, ALU.max))
        if kc == 0:
            tvw = P.op("vector", lambda e: e.reduce_sum(SM[:, 860:861], slot(1, 0, 1024),
                                                        AX), inc=("vS", 1))
            P.wait("vector", tvw)
        else:
            trd = P.op("vector", lambda e: e.reduce_sum(SM[:, 861:862], slot(1, 0, 1024),
                                                        AX), inc=("vS", 1))
            P.wait("vector", trd)
            tvw = P.op("vector", lambda e: e.tensor_tensor(SM[:, 860:861], SM[:, 860:861],
                                                           SM[:, 861:862], ALU.add),
                       inc=("vS", 1))
    P.wait("tensor", tvw)
    bank_wait("PROW")
    tt_ = P.op("tensor", lambda e: e.matmul(PROW[0:1, 0:1], SM[:, 860:861], vg("one"),
                                            start=True, stop=True), inc=("tS", 1))
    P.wait("vector", tt_)
    VF(lambda e: e.tensor_scalar(vc("wsc"), PROW[0:1, 0:1], 1.0 / float(C * C),
                                 None, ALU.mult))
    tv = VF(lambda e: e.tensor_scalar(SR[0:1, 1000:1001], vc("wsc"), 0.5, None,
                                      ALU.mult))
    bank_done("PROW", tv)
    t_thb = bcast_cell(SR[0:1, 1000:1001], vg("thb"), tv)
    P.wait("vector", t_thb)
    VF(lambda e: e.tensor_scalar(vg("thnb"), vg("thb"), -1.0, None, ALU.mult))
    for kc in range(NCH):
        P.wait("vector", P.tok("dG"))
        tl = dma("sync", slot(0, 0, 1024), L0WT[kc * 128:(kc + 1) * 128, :], pre=P.tok("vS"))
        P.wait("vector", tl)
        P.op("vector", lambda e: e.tensor_scalar(slot(1, 0, 1024), slot(0, 0, 1024), vg("thb"),
                                                 None, ALU.is_gt))
        P.op("vector", lambda e: e.tensor_scalar(slot(2, 0, 1024), slot(0, 0, 1024), vg("thnb"),
                                                 None, ALU.is_lt))
        P.op("vector", lambda e: e.tensor_tensor(slot(1, 0, 1024), slot(1, 0, 1024),
                                                 slot(2, 0, 1024), ALU.subtract))
        tv = P.op("vector", lambda e: e.tensor_copy(slotb(3, 0, 1024), slot(1, 0, 1024)),
                  inc=("vS", 1))
        for ob in range(8):
            dma("gpsimd", QWTB[ob][:, kc * 128:(kc + 1) * 128],
                slotb(3, ob * 128, (ob + 1) * 128), pre=tv)
    for kc in range(NCH):
        P.wait("vector", P.tok("dG"))
        dma("sync", stru(0, 4096), K0A[kc * 128:(kc + 1) * 128, :], pre=P.tok("vS"))
        tb = dma("sync", stru(4096, 8192), K0B[kc * 128:(kc + 1) * 128, :], pre=P.tok("vS"))
        P.wait("vector", tb)
        tv = P.op("vector", lambda e: e.tensor_tensor(slotb(3, 0, 4096), stru(0, 4096),
                                                      stru(4096, 8192), ALU.subtract),
                  inc=("vS", 1))
        for ob in range(32):
            dma("gpsimd", D0B[ob][:, kc * 128:(kc + 1) * 128],
                slotb(3, ob * 128, (ob + 1) * 128), pre=tv)
    for kc in range(32):
        P.wait("vector", P.tok("dG"))
        ta = dma("sync", stru(0, 4096), K1U[kc * 128:(kc + 1) * 128, :], pre=P.tok("vS"))
        P.wait("vector", ta)
        tv = P.op("vector", lambda e: e.tensor_scalar(slotb(3, 0, 4096), stru(0, 4096),
                                                      2.0, -1.0, ALU.mult, ALU.add),
                  inc=("vS", 1))
        for ob in range(32):
            dma("gpsimd", W1BB[ob][:, kc * 128:(kc + 1) * 128],
                slotb(3, ob * 128, (ob + 1) * 128), pre=tv)
    for kc in range(32):
        P.wait("vector", P.tok("dG"))
        ta = dma("sync", stru(0, 1024), K2U[kc * 128:(kc + 1) * 128, :], pre=P.tok("vS"))
        P.wait("vector", ta)
        tv = P.op("vector", lambda e: e.tensor_scalar(slotb(3, 0, 1024), stru(0, 1024),
                                                      2.0, -1.0, ALU.mult, ALU.add),
                  inc=("vS", 1))
        for ob in range(8):
            dma("gpsimd", W2BB[ob][:, kc * 128:(kc + 1) * 128],
                slotb(3, ob * 128, (ob + 1) * 128), pre=tv)

    P.barrier()
    P.wait("gpsimd", tc1)
    snap("x1out", X1OUT, [C, T])

    for tt in range(NTO):
        c0 = tt * TT
        P.wait("vector", P.tok("dG"))
        hA = (c0 - 1) % T
        hB = TO + c0 - 1
        tl = None
        for ci in range(NCH):
            tl = dma("sync", slot(0, ci * TT, (ci + 1) * TT),
                     X1OUT[ci * 128:(ci + 1) * 128, c0:c0 + TT],
                     pre=[P.tok("vS"), P.tok("dG")])
            tl = dma("sync", slot(5, ci * TT, (ci + 1) * TT),
                     X1OUT[ci * 128:(ci + 1) * 128, TO + c0:TO + c0 + TT],
                     pre=[P.tok("vS"), P.tok("tS")])
            tl = dma("sync", SM[:, 800 + ci:801 + ci],
                     X1OUT[ci * 128:(ci + 1) * 128, hA:hA + 1], pre=P.tok("vS"), ncok=True)
            tl = dma("sync", SM[:, 808 + ci:809 + ci],
                     X1OUT[ci * 128:(ci + 1) * 128, hB:hB + 1], pre=P.tok("vS"), ncok=True)
        P.wait("vector", tl)
        for ci in range(NCH):
            P.op("vector", lambda e, ci=ci: e.tensor_tensor(
                slot(5, ci * TT, (ci + 1) * TT), slot(5, ci * TT, (ci + 1) * TT),
                slot(0, ci * TT, (ci + 1) * TT), ALU.subtract))
            P.op("vector", lambda e, ci=ci: e.tensor_scalar(
                slot(5, ci * TT, (ci + 1) * TT), slot(5, ci * TT, (ci + 1) * TT), SELS[:],
                None, ALU.mult))
            P.op("vector", lambda e, ci=ci: e.tensor_tensor(
                slot(0, ci * TT, (ci + 1) * TT), slot(0, ci * TT, (ci + 1) * TT),
                slot(5, ci * TT, (ci + 1) * TT), ALU.add))
            VF(lambda e, ci=ci: e.tensor_tensor(
                SM[:, 808 + ci:809 + ci], SM[:, 808 + ci:809 + ci],
                SM[:, 800 + ci:801 + ci], ALU.subtract))
            VF(lambda e, ci=ci: e.tensor_scalar(
                SM[:, 808 + ci:809 + ci], SM[:, 808 + ci:809 + ci], SELS[:], None, ALU.mult))
            VF(lambda e, ci=ci: e.tensor_tensor(
                SM[:, 800 + ci:801 + ci], SM[:, 800 + ci:801 + ci],
                SM[:, 808 + ci:809 + ci], ALU.add))
        for ci in range(NCH):
            P.op("vector", lambda e, ci=ci: e.tensor_copy(
                slot(1, ci * TT + 1, (ci + 1) * TT), slot(0, ci * TT, (ci + 1) * TT - 1)))
            P.op("vector", lambda e, ci=ci: e.tensor_copy(
                slot(1, ci * TT, ci * TT + 1), SM[:, 800 + ci:801 + ci]))
        tv = P.op("vector", lambda e: e.engine_nop(), inc=("vS", 1))
        tok = stats_pass(lambda ci: slot(0, ci * TT, (ci + 1) * TT), NCH,
                         lambda ci: slot(4, ci * TT, (ci + 1) * TT), tv)
        tok = stats_finalize(tok)
        tbc = stats_bcast(tok)
        P.wait("vector", tbc)
        for ci in range(NCH):
            sx = slot(0, ci * TT, (ci + 1) * TT)
            sn = slot(2, ci * TT, (ci + 1) * TT)
            P.op("vector", lambda e, a=sn, b=sx: e.tensor_tensor(a, b, PB0[:], ALU.subtract))
            P.op("vector", lambda e, a=sn: e.tensor_tensor(a, a, PB1[:], ALU.mult))
            P.op("vector", lambda e, a=sn, g=vg("g2", ci), b=vg("b2", ci):
                 e.tensor_scalar(a, a, g, b, ALU.mult, ALU.add))
            bh = slotb(3, ci * TT, (ci + 1) * TT)
            bl = slotb(3, SLOT + ci * TT, SLOT + (ci + 1) * TT)
            P.op("vector", lambda e, o=bh, i=sn: e.tensor_copy(o, i))
            P.op("vector", lambda e, a=slot(4, ci * TT, (ci + 1) * TT), b=sn, i=bh:
                 e.tensor_tensor(a, b, i, ALU.subtract))
            P.op("vector", lambda e, o=bl, i=slot(4, ci * TT, (ci + 1) * TT): e.tensor_copy(o, i))
        tvn = P.op("vector", lambda e: e.engine_nop(), inc=("vS", 1))
        bank_done("PB0", tvn)
        bank_done("PB1", tvn)
        P.wait("tensor", tvn)
        for oc in range(32):
            buf = oc % 2
            tl2 = None
            tl2 = dma("sync", strb(buf * 2048, buf * 2048 + 1024),
                      WCMHB[oc], pre=LG.get(("kk", buf)))
            tl2 = dma("sync", strb(4096 + buf * 2048, 4096 + buf * 2048 + 1024),
                      WCMLB[oc], pre=tl2)
            P.wait("tensor", tl2)
            pb = PS[oc % 2]
            bank_wait(oc % 2)
            nmm = 0
            for kc in range(NCH):
                wh = strb(buf * 2048 + kc * 128, buf * 2048 + (kc + 1) * 128)
                wl = strb(4096 + buf * 2048 + kc * 128, 4096 + buf * 2048 + (kc + 1) * 128)
                xh = slotb(3, kc * TT, (kc + 1) * TT)
                xl = slotb(3, SLOT + kc * TT, SLOT + (kc + 1) * TT)
                for (a_, b_) in ((wh, xh), (wh, xl), (wl, xh)):
                    P.op("tensor", lambda e, a=a_, b=b_, p=pb, st2=(nmm == 0), sp2=(nmm == 23):
                         e.matmul(p[:], a, b, start=st2, stop=sp2),
                         inc=("tS", 1) if nmm == 23 else None)
                    nmm += 1
            tmm = P.tok("tS")
            LG[("kk", buf)] = tmm
            P.wait("vector", tmm)
            cg = oc % 8
            sq = slot(4, 0, TT)
            P.op("vector", lambda e, p=pb, m=vg("cmk", cg), o=sq:
                 e.tensor_scalar(o, p[:], m, None, ALU.mult))
            P.op("vector", lambda e, xp=slot(1, cg * TT, (cg + 1) * TT), om=vg("om_cmk", cg):
                 e.tensor_scalar(slot(4, 4 * TT, 5 * TT), xp, om, None, ALU.mult))
            P.op("vector", lambda e, o=sq:
                 e.tensor_tensor(o, o, slot(4, 4 * TT, 5 * TT), ALU.add))
            P.op("vector", lambda e, o=sq: e.tensor_scalar(o, o, 0.0, None, ALU.max))
            P.op("vector", lambda e, o=sq: e.tensor_tensor(o, o, o, ALU.mult))
            qh = WK[:, 6 * SLOT:8 * SLOT].bitcast(BF16)[:, oc * TT:(oc + 1) * TT]
            ql = WK[:, 8 * SLOT:10 * SLOT].bitcast(BF16)[:, oc * TT:(oc + 1) * TT]
            P.op("vector", lambda e, o=qh, i=sq: e.tensor_copy(o, i))
            P.op("vector", lambda e, o=sq, i=qh: e.tensor_tensor(o, o, i, ALU.subtract))
            tvq = P.op("vector", lambda e, o=ql, i=sq: e.tensor_copy(o, i), inc=("vS", 1))
            bank_done(oc % 2, tvq)
        tl3 = None
        for ci in range(NCH):
            tl3 = dma("sync", slot(3, ci * TT, (ci + 1) * TT),
                      X0O[ci * 128:(ci + 1) * 128, c0:c0 + TT],
                      pre=[P.tok("vS"), P.tok("tS")])
        P.wait("tensor", P.tok("vS"))
        last_dma = {}
        for oc2 in range(NCH):
            buf2 = oc2 % 2

            def vvb(lo, hi, b2=buf2):
                if b2 == 0:
                    return strb(lo, hi)
                return slotb(5, lo, hi)
            tl2 = None
            tl2 = dma("scalar", vvb(0, 4096), WCPHB[oc2],
                      pre=[LG.get(("vv", buf2)), P.tok("vS")])
            tl2 = dma("scalar", vvb(4096, 8192), WCPLB[oc2], pre=tl2)
            P.wait("tensor", tl2)
            pb = PS[2 + oc2 % 2]
            bank_wait(2 + oc2 % 2)
            nmm = 0
            for kc in range(32):
                wh = vvb(kc * 128, (kc + 1) * 128)
                wl = vvb(4096 + kc * 128, 4096 + (kc + 1) * 128)
                sqh = WK[:, 6 * SLOT:8 * SLOT].bitcast(BF16)[:, kc * TT:(kc + 1) * TT]
                sql = WK[:, 8 * SLOT:10 * SLOT].bitcast(BF16)[:, kc * TT:(kc + 1) * TT]
                for (a_, b_) in ((wh, sqh), (wh, sql), (wl, sqh)):
                    P.op("tensor", lambda e, a=a_, b=b_, p=pb, st2=(nmm == 0), sp2=(nmm == 95):
                         e.matmul(p[:], a, b, start=st2, stop=sp2),
                         inc=("tS", 1) if nmm == 95 else None)
                    nmm += 1
            tmm = P.tok("tS")
            LG[("vv", buf2)] = tmm
            par = oc2 % 2
            rr = slot(4, (2 * par) * TT, (2 * par + 1) * TT)
            out_ = slot(4, (2 * par + 1) * TT, (2 * par + 2) * TT)
            P.wait("vector", last_dma.get(par))
            P.op("vector", lambda e, o=rr, xn=slot(2, oc2 * TT, (oc2 + 1) * TT), m=vg("cmr", oc2):
                 e.tensor_scalar(o, xn, m, None, ALU.mult))
            P.op("vector", lambda e, xp=slot(1, oc2 * TT, (oc2 + 1) * TT), om=vg("om_cmr", oc2):
                 e.tensor_scalar(slot(4, 5 * TT, 6 * TT), xp, om, None, ALU.mult))
            tvr = P.op("vector", lambda e, o=rr:
                       e.tensor_tensor(o, o, slot(4, 5 * TT, 6 * TT), ALU.add), inc=("vS", 1))
            P.wait("scalar", tvr)
            tsr = P.op("scalar", lambda e, o=rr: e.activation(o, o, AF.Sigmoid), inc=("sS", 1))
            P.wait("vector", tsr)
            P.wait("vector", tmm)
            if oc2 == 0:
                P.wait("vector", tl3)
            P.op("vector", lambda e, o=out_, p=pb, r2=rr: e.tensor_tensor(o, r2, p[:], ALU.mult))
            P.op("vector", lambda e, o=out_, x1=slot(0, oc2 * TT, (oc2 + 1) * TT):
                 e.tensor_tensor(o, o, x1, ALU.add))
            tvo = P.op("vector", lambda e, o=out_, x0=slot(3, oc2 * TT, (oc2 + 1) * TT):
                       e.tensor_tensor(o, o, x0, ALU.add), inc=("vS", 1))
            bank_done(2 + oc2 % 2, tvo)
            tdm = dma("gpsimd", RRES[oc2 * 128:(oc2 + 1) * 128, c0:c0 + TT], out_, pre=tvo)
            last_dma[par] = tdm
        P.barrier()

    snap("rres3", RRES, [C, TO])
    P.op("vector", lambda e: e.memset(SM[:, 820:821], 0.0))

    def ln_tile(tt, dst_slot):
        c0 = tt * TT
        tl = None
        for ci in range(NCH):
            tl = dma("sync", slot(0, ci * TT, (ci + 1) * TT),
                     RRES[ci * 128:(ci + 1) * 128, c0:c0 + TT],
                     pre=[P.tok("vS"), P.tok("dG")])
        tok = stats_pass(lambda ci: slot(0, ci * TT, (ci + 1) * TT), NCH,
                         lambda ci: slot(dst_slot, ci * TT, (ci + 1) * TT), tl)
        tok = stats_finalize(tok)
        tbc = stats_bcast(tok)
        P.wait("vector", tbc)
        for ci in range(NCH):
            sx = slot(0, ci * TT, (ci + 1) * TT)
            sz = slot(dst_slot, ci * TT, (ci + 1) * TT)
            P.op("vector", lambda e, a=sz, b=sx: e.tensor_tensor(a, b, PB0[:], ALU.subtract))
            P.op("vector", lambda e, a=sz: e.tensor_tensor(a, a, PB1[:], ALU.mult))
        tv_ = P.op("vector", lambda e: e.engine_nop(), inc=("vS", 1))
        bank_done("PB0", tv_)
        bank_done("PB1", tv_)
        return tv_

    for tt in range(NTO):
        ln_tile(tt, 1)
        for ci in range(NCH):
            trd = P.op("vector", lambda e, a=slot(1, ci * TT, (ci + 1) * TT):
                       e.tensor_reduce(SM[:, 821:822], a, AX, ALU.max,
                                       apply_absolute_value=True), inc=("vS", 1))
            P.wait("vector", trd)
            P.op("vector", lambda e: e.tensor_tensor(SM[:, 820:821], SM[:, 820:821],
                                                     SM[:, 821:822], ALU.max))
        P.op("vector", lambda e: e.engine_nop(), inc=("vS", 1))
    tv = P.tok("vS")
    P.wait("tensor", tv)
    bank_wait("PT")
    tt_ = P.op("tensor", lambda e: e.transpose(PT[0:1, 0:128], SM[:, 820:821], IDNS[:]),
               inc=("tS", 1))
    P.wait("vector", tt_)
    trd = P.op("vector", lambda e: e.tensor_reduce(SR[0:1, 900:901], PT[0:1, 0:128],
                                                   AX, ALU.max), inc=("vS", 1))
    P.wait("vector", trd)
    tv = VF(lambda e: e.tensor_copy(SR[0:1, 901:909],
                                    SR[0:1, 900:901].broadcast_to((1, 8))))
    bank_done("PT", tv)
    dma("gpsimd", ARI, SR[0:1, 901:909], pre=tv)
    P.wait("gpsimd", P.tok("dG"))
    tc3 = P.op("gpsimd", lambda e: e.collective_compute(
        "AllReduce", ALU.max, replica_groups=[list(range(8))],
        ins=[ARI.opt()], outs=[ARO.opt()]), inc=("cS", 1))
    P.wait("sync", tc3)
    tgm = dma("sync", SR[0:1, 910:918], ARO)
    P.wait("vector", tgm)
    VF(lambda e: e.tensor_scalar(vc("iscale"), SR[0:1, 910:911], 1.0 / 127.0,
                                 None, ALU.mult))
    VF(lambda e: e.reciprocal(SR[0:1, 920:921], vc("iscale")))
    tv = VF(lambda e: e.tensor_tensor(vc("wsis"), vc("wsc"), vc("iscale"), ALU.mult))
    t_invb = bcast_cell(SR[0:1, 920:921], vg("invb"), tv)
    t_wsis = bcast_cell(vc("wsis"), vg("wsisb"), t_invb)
    P.barrier()
    for tt in range(NTO):
        c0 = tt * TT
        ln_tile(tt, 1)
        for ci in range(NCH):
            sz = slot(1, ci * TT, (ci + 1) * TT)
            P.op("vector", lambda e, a=sz: e.tensor_scalar(a, a, vg("invb"), None, ALU.mult))
            P.op("vector", lambda e, a=sz, o=sloti(2, ci * TT, (ci + 1) * TT): e.tensor_copy(o, a))
            P.op("vector", lambda e, a=sz, i=sloti(2, ci * TT, (ci + 1) * TT): e.tensor_copy(a, i))
            P.op("vector", lambda e, a=sz: e.tensor_scalar(a, a, 127.0, -128.0, ALU.min, ALU.max))
            P.op("vector", lambda e, a=sz, o=slotb(3, ci * TT, (ci + 1) * TT): e.tensor_copy(o, a))
        tvq = P.op("vector", lambda e: e.engine_nop(), inc=("vS", 1))
        P.wait("tensor", tvq)
        last_dma = {}
        for oc in range(NCH):
            buf = oc % 2
            tl2 = dma("sync", strb(buf * 2048, buf * 2048 + 1024),
                      QWTB[oc], pre=LG.get(("qw", buf)))
            P.wait("tensor", tl2)
            pb = PS[oc % 2]
            bank_wait(oc % 2)
            for kc in range(NCH):
                wq_ = strb(buf * 2048 + kc * 128, buf * 2048 + (kc + 1) * 128)
                qi = slotb(3, kc * TT, (kc + 1) * TT)
                P.op("tensor", lambda e, a=wq_, b=qi, p=pb, st2=(kc == 0), sp2=(kc == 7):
                     e.matmul(p[:], a, b, start=st2, stop=sp2),
                     inc=("tS", 1) if kc == 7 else None)
            tmm = P.tok("tS")
            LG[("qw", buf)] = tmm
            P.wait("vector", tmm)
            par = oc % 2
            out_ = slot(4, par * TT, (par + 1) * TT)
            P.wait("vector", last_dma.get(par))
            P.op("vector", lambda e, o=out_, p=pb, b=vg("l0b", oc):
                 e.tensor_scalar(o, p[:], vg("wsisb"), b, ALU.mult, ALU.add))
            tvo = P.op("vector", lambda e, o=out_, x=slot(0, oc * TT, (oc + 1) * TT):
                       e.tensor_tensor(o, o, x, ALU.add), inc=("vS", 1))
            bank_done(oc % 2, tvo)
            last_dma[par] = dma("gpsimd", RRES[oc * 128:(oc + 1) * 128, c0:c0 + TT], out_,
                                pre=tvo)
        P.barrier()

    snap("rres4", RRES, [C, TO])
    P.op("vector", lambda e: e.memset(SM[0:1, 830:832], 0.0))
    for tt in range(NTO):
        c0 = tt * TT
        ln_tile(tt, 1)
        tvlast = None
        for ci in range(NCH):
            sz = slot(1, ci * TT, (ci + 1) * TT)
            bh = slotb(3, ci * TT, (ci + 1) * TT)
            bl = slotb(3, SLOT + ci * TT, SLOT + (ci + 1) * TT)
            P.op("vector", lambda e, o=bh, i=sz: e.tensor_copy(o, i))
            P.op("vector", lambda e, a=slot(2, ci * TT, (ci + 1) * TT), b=sz, i=bh:
                 e.tensor_tensor(a, b, i, ALU.subtract))
            tvlast = P.op("vector", lambda e, o=bl, i=slot(2, ci * TT, (ci + 1) * TT):
                          e.tensor_copy(o, i), inc=("vS", 1))
            dma("gpsimd", XCH[ci * 128:(ci + 1) * 128, c0:c0 + TT], bh, pre=tvlast)
            dma("gpsimd", XCH[C + ci * 128:C + (ci + 1) * 128, c0:c0 + TT], bl, pre=tvlast)
    P.wait("gpsimd", P.tok("dG"))
    tc4 = P.op("gpsimd", lambda e: e.collective_compute(
        "AllGather", ALU.bypass, replica_groups=[[0, 1], [2, 3], [4, 5], [6, 7]],
        ins=[XCH.opt()], outs=[XCG.opt()]), inc=("cS", 1))

    def proj(dst_h, dst_l, wname, bias_nm, scale_pow, row_h, row_l, src_dram, dst_col0,
             norm_cell, do_transpose=False):
        for tt in range(NTO):
            c0 = tt * TT
            tl = None
            for kc in range(NCH):
                tl = dma("sync", slotb(5, kc * TT, (kc + 1) * TT),
                         src_dram[row_h + kc * 128:row_h + (kc + 1) * 128, c0:c0 + TT],
                         pre=[P.tok("tS"), P.tok("dG")])
                tl = dma("sync", slotb(5, SLOT + kc * TT, SLOT + (kc + 1) * TT),
                         src_dram[row_l + kc * 128:row_l + (kc + 1) * 128, c0:c0 + TT],
                         pre=P.tok("tS"))
            P.wait("tensor", tl)
            tsq_prev = None
            for oc in range(NCH):
                buf = oc % 2
                tl2 = dma("scalar", strb(buf * 2048, buf * 2048 + 1024),
                          PRJ[wname + "h"][oc], pre=LG.get((wname, buf)))
                tl2 = dma("scalar", strb(4096 + buf * 2048, 4096 + buf * 2048 + 1024),
                          PRJ[wname + "l"][oc], pre=tl2)
                P.wait("tensor", tl2)
                pb = PS[oc % 2]
                bank_wait(oc % 2)
                nmm = 0
                for kc in range(NCH):
                    wh = strb(buf * 2048 + kc * 128, buf * 2048 + kc * 128 + 128)
                    wl = strb(4096 + buf * 2048 + kc * 128, 4096 + buf * 2048 + kc * 128 + 128)
                    xh = slotb(5, kc * TT, (kc + 1) * TT)
                    xl = slotb(5, SLOT + kc * TT, SLOT + (kc + 1) * TT)
                    for (a_, b_) in ((wh, xh), (wh, xl), (wl, xh)):
                        P.op("tensor", lambda e, a=a_, b=b_, p=pb, st2=(nmm == 0), sp2=(nmm == 23):
                             e.matmul(p[:], a, b, start=st2, stop=sp2),
                             inc=("tS", 1) if nmm == 23 else None)
                        nmm += 1
                tmm = P.tok("tS")
                LG[(wname, buf)] = tmm
                P.wait("vector", tmm)
                P.wait("vector", tsq_prev)
                sq = slot(4, 0, TT)
                if scale_pow is not None:
                    P.op("vector", lambda e, p=pb, o=sq, b=vg(bias_nm, oc):
                         e.tensor_scalar(o, p[:], b, scale_pow, ALU.add, ALU.mult))
                else:
                    P.op("vector", lambda e, p=pb, o=sq, b=vg(bias_nm, oc):
                         e.tensor_scalar(o, p[:], b, None, ALU.add))
                bh = slotb(0, 0, TT)
                bl = slotb(0, TT, 2 * TT)
                P.op("vector", lambda e, o=bh, i=sq: e.tensor_copy(o, i))
                P.op("vector", lambda e, o=slot(1, 0, TT), a=sq, i=bh:
                     e.tensor_tensor(o, a, i, ALU.subtract))
                tvs = P.op("vector", lambda e, o=bl, i=slot(1, 0, TT): e.tensor_copy(o, i),
                           inc=("vS", 1))
                bank_done(oc % 2, tvs)
                if not do_transpose:
                    dma("gpsimd", dst_h[oc * 128:(oc + 1) * 128, dst_col0 + c0:dst_col0 + c0 + TT],
                        bh, pre=tvs)
                    dma("gpsimd", dst_l[oc * 128:(oc + 1) * 128, dst_col0 + c0:dst_col0 + c0 + TT],
                        bl, pre=tvs)
                else:
                    P.wait("tensor", tvs)
                    for tb in range(4):
                        for si, (srcb, dstd) in enumerate(((bh, dst_h), (bl, dst_l))):
                            idx = tb * 2 + si
                            bank_wait("PTB")
                            tt2 = P.op("tensor", lambda e, s2=srcb[:, tb * 128:(tb + 1) * 128]:
                                       e.transpose(PTB, s2, IDNBS[:]), inc=("tS", 1))
                            P.wait("vector", tt2)
                            tvt = P.op("vector", lambda e, idx=idx: e.tensor_copy(
                                slotb(2, idx * 128, (idx + 1) * 128), PTB), inc=("vS", 1))
                            bank_done("PTB", tvt)
                            dma("gpsimd",
                                dstd[dst_col0 + c0 + tb * 128:dst_col0 + c0 + (tb + 1) * 128,
                                     oc * 128:(oc + 1) * 128],
                                slotb(2, idx * 128, (idx + 1) * 128), pre=tvt)
                if norm_cell is not None:
                    P.wait("scalar", tvs)
                    tsq_prev = P.op("scalar", lambda e, o=slot(1, TT, 2 * TT), i=sq:
                                    e.activation(o, i, AF.Square), inc=("sS", 1))
                    P.wait("tensor", tsq_prev)
                    if oc == 0:
                        bank_wait("PROW")
                    P.op("tensor", lambda e, r=slot(1, TT, 2 * TT), st2=(oc == 0), sp2=(oc == 7):
                         e.matmul(PROW[0:1, :], vg("one"), r, start=st2, stop=sp2),
                         inc=("tS", 1) if oc == 7 else None)
            if norm_cell is not None:
                tns = P.tok("tS")
                P.wait("vector", tns)
                trd = P.op("vector", lambda e: e.tensor_reduce(SM[0:1, 845:846], PROW[0:1, :],
                                                               AX, ALU.max), inc=("vS", 1))
                P.wait("vector", trd)
                P.op("vector", lambda e, nc_=norm_cell: e.tensor_tensor(
                    nc_, nc_, SM[0:1, 845:846], ALU.max))
                tve = P.op("vector", lambda e: e.engine_nop(), inc=("vS", 1))
                bank_done("PROW", tve)

    proj(QSH, QSL, "wq", "bq", 1.0 / 1024.0, 0, C, XCH, 0, SM[0:1, 830:831])
    P.barrier()
    P.wait("sync", tc4)
    P.wait("scalar", tc4)
    P.wait("tensor", tc4)
    P.wait("vector", tc4)
    P.wait("gpsimd", tc4)
    for h in range(2):
        proj(KSH, KSL, "wk", "bk", None, h * 2 * C, h * 2 * C + C, XCG, h * TO,
             SM[0:1, 831:832])
    for h in range(2):
        proj(VTH, VTL, "wv", "bv", None, h * 2 * C, h * 2 * C + C, XCG, h * TO,
             None, do_transpose=True)
    tv = P.tok("vS")
    P.wait("scalar", tv)
    P.op("scalar", lambda e: e.activation(SM[0:1, 830:831], SM[0:1, 830:831], AF.Sqrt))
    ts = P.op("scalar", lambda e: e.activation(SM[0:1, 831:832], SM[0:1, 831:832], AF.Sqrt),
              inc=("sS", 1))
    P.wait("vector", ts)
    VF(lambda e: e.tensor_tensor(vc("mbc"), SM[0:1, 830:831], SM[0:1, 831:832],
                                 ALU.mult))
    tv = VF(lambda e: e.tensor_scalar(SR[0:1, 930:931], vc("mbc"), -1.0, None,
                                      ALU.mult))
    t_negm = bcast_cell(SR[0:1, 930:931], vg("negmb"), tv)
    P.barrier()

    for qb in range(NTO):
        q0 = qb * TT
        tq = None
        for kc in range(NCH):
            tq = dma("sync", slotb(2, kc * TT, (kc + 1) * TT),
                     QSH[kc * 128:(kc + 1) * 128, q0:q0 + TT],
                     pre=[P.tok("vS"), P.tok("tS"), P.tok("dG")])
            tq = dma("sync", slotb(2, SLOT + kc * TT, SLOT + (kc + 1) * TT),
                     QSL[kc * 128:(kc + 1) * 128, q0:q0 + TT], pre=P.tok("vS"))
        P.op("vector", lambda e: e.memset(slot(8, 0, SLOT), 0.0))
        tvz = P.op("vector", lambda e: e.engine_nop(), inc=("vS", 1))
        P.wait("tensor", tvz)
        P.wait("tensor", tq)
        first_den = [True]
        tvp_prev = [None]
        for kt in range(NTT):
            k0_ = kt * TT
            buf = kt % 2
            tl = None
            for kc in range(NCH):
                tl = dma("sync", slotb(3 + buf, kc * TT, (kc + 1) * TT),
                         KSH[kc * 128:(kc + 1) * 128, k0_:k0_ + TT],
                         pre=LG.get(("fKV", buf)))
                tl = dma("sync", slotb(3 + buf, SLOT + kc * TT, SLOT + (kc + 1) * TT),
                         KSL[kc * 128:(kc + 1) * 128, k0_:k0_ + TT], pre=tl)
            tv_ = None
            for tb in range(4):
                tv_ = dma("scalar", slotb(5 + buf, tb * 1024, (tb + 1) * 1024),
                          VTH[k0_ + tb * 128:k0_ + (tb + 1) * 128, :],
                          pre=LG.get(("fKV", buf)))
                tv_ = dma("scalar", slotb(5 + buf, SLOT + tb * 1024, SLOT + (tb + 1) * 1024),
                          VTL[k0_ + tb * 128:k0_ + (tb + 1) * 128, :], pre=tv_)
            P.wait("tensor", tl)
            P.wait("tensor", tv_)
            for kc4 in range(4):
                pb = PS[kc4 % 2]
                bank_wait(kc4 % 2)
                nmm = 0
                for cc in range(NCH):
                    kh = slotb(3 + buf, cc * TT + kc4 * 128, cc * TT + (kc4 + 1) * 128)
                    kl = slotb(3 + buf, SLOT + cc * TT + kc4 * 128,
                               SLOT + cc * TT + (kc4 + 1) * 128)
                    qh = slotb(2, cc * TT, (cc + 1) * TT)
                    ql = slotb(2, SLOT + cc * TT, SLOT + (cc + 1) * TT)
                    for (a_, b_) in ((kh, qh), (kh, ql), (kl, qh)):
                        P.op("tensor", lambda e, a=a_, b=b_, p=pb, st2=(nmm == 0), sp2=(nmm == 23):
                             e.matmul(p[:], a, b, start=st2, stop=sp2),
                             inc=("tS", 1) if nmm == 23 else None)
                        nmm += 1
                tmm = P.tok("tS")
                P.wait("vector", tmm)
                P.wait("vector", tvp_prev[0])
                p32 = slot(7, 0, TT)
                tvv = P.op("vector", lambda e, p=pb, o=p32:
                           e.tensor_scalar(o, p[:], vg("negmb"), None, ALU.add),
                           inc=("vS", 1))
                P.wait("scalar", tvv)
                tse = P.op("scalar", lambda e, o=p32: e.activation(o, o, AF.Exp),
                           inc=("sS", 1))
                P.wait("vector", tse)
                if kt == qb:
                    P.op("vector", lambda e, o=p32[:, kc4 * 128:(kc4 + 1) * 128]:
                         e.tensor_tensor(o, o, MSKES[:], ALU.mult))
                if kt == 8 + qb:
                    P.op("vector", lambda e, o=p32[:, kc4 * 128:(kc4 + 1) * 128]:
                         e.tensor_tensor(o, o, MSKOS[:], ALU.mult))
                ph = slotb(7, 1024 + kc4 * TT, 1024 + (kc4 + 1) * TT)
                pl = slotb(7, 3072 + kc4 * TT, 3072 + (kc4 + 1) * TT)
                P.op("vector", lambda e, o=ph, i=p32: e.tensor_copy(o, i))
                P.op("vector", lambda e, i=ph, o=slot(9, 0, TT), a=p32:
                     e.tensor_tensor(o, a, i, ALU.subtract))
                tvp = P.op("vector", lambda e, o=pl: e.tensor_copy(o, slot(9, 0, TT)),
                           inc=("vS", 1))
                tvp_prev[0] = tvp
                bank_done(kc4 % 2, tvp)
            P.wait("tensor", tvp_prev[0])
            for kc4 in range(4):
                ph = slotb(7, 1024 + kc4 * TT, 1024 + (kc4 + 1) * TT)
                pl = slotb(7, 3072 + kc4 * TT, 3072 + (kc4 + 1) * TT)
                for pp in (ph, pl):
                    sp_ = (kt == NTT - 1 and kc4 == 3 and pp is pl)
                    P.op("tensor", lambda e, r=pp, st2=first_den[0], sp2=sp_:
                         e.matmul(PROW[0:1, :], ONEB[:], r, start=st2, stop=sp2),
                         inc=("tS", 1) if sp_ else None)
                    first_den[0] = False
            for qc in range(4):
                for ct in range(2):
                    pb = PB0 if (qc * 2 + ct) % 2 == 0 else PB1
                    bid = "PB0" if (qc * 2 + ct) % 2 == 0 else "PB1"
                    bank_wait(bid)
                    nmm = 0
                    for kc4 in range(4):
                        ph = slotb(7, 1024 + kc4 * TT + qc * 128, 1024 + kc4 * TT + (qc + 1) * 128)
                        pl = slotb(7, 3072 + kc4 * TT + qc * 128, 3072 + kc4 * TT + (qc + 1) * 128)
                        vh = slotb(5 + buf, kc4 * 1024 + ct * 512, kc4 * 1024 + (ct + 1) * 512)
                        vl = slotb(5 + buf, SLOT + kc4 * 1024 + ct * 512,
                                   SLOT + kc4 * 1024 + (ct + 1) * 512)
                        for (a_, b_) in ((ph, vh), (pl, vh), (ph, vl)):
                            P.op("tensor", lambda e, a=a_, b=b_, p=pb, st2=(nmm == 0),
                                 sp2=(nmm == 11):
                                 e.matmul(p[:], a, b, start=st2, stop=sp2),
                                 inc=("tS", 1) if nmm == 11 else None)
                            nmm += 1
                    tmm = P.tok("tS")
                    if qc == 3 and ct == 1:
                        LG[("fKV", buf)] = tmm
                    P.wait("vector", tmm)
                    oa = WK[:, 8 * SLOT + qc * 1024 + ct * 512:8 * SLOT + qc * 1024 + (ct + 1) * 512]
                    tva = P.op("vector", lambda e, o=oa, p=pb:
                               e.tensor_tensor(o, o, p[:], ALU.add), inc=("vS", 1))
                    bank_done(bid, tva)
        tden = P.tok("tS")
        P.wait("vector", tden)
        tvd = P.op("vector", lambda e: e.tensor_copy(SR[0:1, 0:512], PROW[0:1, :]), inc=("vS", 1))
        bank_done("PROW", tvd)
        P.wait("tensor", tvd)
        for qc in range(4):
            bank_wait("PT")
            tt2 = P.op("tensor", lambda e, qc=qc: e.transpose(
                PT[:, 0:1], SR[0:1, qc * 128:(qc + 1) * 128], IDNS[0:1, 0:1]), inc=("tS", 1))
            P.wait("vector", tt2)
            tvt = VF(lambda e, qc=qc: e.tensor_copy(SM[:, 850 + qc:851 + qc], PT[:, 0:1]))
            bank_done("PT", tvt)
        VF(lambda e: e.reciprocal(SM[:, 850:854], SM[:, 850:854]))
        for qc in range(4):
            P.op("vector", lambda e, qc=qc: e.tensor_scalar(
                WK[:, 8 * SLOT + qc * 1024:8 * SLOT + (qc + 1) * 1024],
                WK[:, 8 * SLOT + qc * 1024:8 * SLOT + (qc + 1) * 1024],
                SM[:, 850 + qc:851 + qc], None, ALU.mult))
        tvf = P.op("vector", lambda e: e.engine_nop(), inc=("vS", 1))
        tl = None
        for ci in range(NCH):
            tl = dma("sync", slot(0, ci * TT, (ci + 1) * TT),
                     RRES[ci * 128:(ci + 1) * 128, q0:q0 + TT],
                     pre=[P.tok("vS"), P.tok("dG")])
        P.wait("tensor", tvf)
        P.wait("vector", tl)
        for qc in range(4):
            for cc in range(NCH):
                bank_wait("PT")
                tt2 = P.op("tensor", lambda e, qc=qc, cc=cc: e.transpose(
                    PT[:],
                    WK[:, 8 * SLOT + qc * 1024 + cc * 128:8 * SLOT + qc * 1024 + (cc + 1) * 128],
                    IDNS[:]), inc=("tS", 1))
                P.wait("vector", tt2)
                tva = P.op("vector", lambda e, qc=qc, cc=cc: e.tensor_tensor(
                    slot(0, cc * TT + qc * 128, cc * TT + (qc + 1) * 128),
                    slot(0, cc * TT + qc * 128, cc * TT + (qc + 1) * 128),
                    PT[:], ALU.add), inc=("vS", 1))
                bank_done("PT", tva)
        tvo = P.tok("vS")
        for ci in range(NCH):
            dma("gpsimd", RRES[ci * 128:(ci + 1) * 128, q0:q0 + TT],
                slot(0, ci * TT, (ci + 1) * TT), pre=tvo)
        P.barrier()

    snap("rres5", RRES, [C, TO])
    for tt in range(NTO):
        c0 = tt * TT
        ln_tile(tt, 1)
        for ci in range(NCH):
            sz = slot(1, ci * TT, (ci + 1) * TT)
            bh = slotb(2, ci * TT, (ci + 1) * TT)
            bl = slotb(2, SLOT + ci * TT, SLOT + (ci + 1) * TT)
            P.op("vector", lambda e, o=bh, i=sz: e.tensor_copy(o, i))
            P.op("vector", lambda e, a=sz, i=bh: e.tensor_tensor(a, a, i, ALU.subtract))
            P.op("vector", lambda e, o=bl, i=sz: e.tensor_copy(o, i))
        tvx = P.op("vector", lambda e: e.engine_nop(), inc=("vS", 1))
        P.wait("tensor", tvx)

        def glayer(nkc, nout, wsrc, xh_fn, xl_fn, outh_fn, outl_fn):
            P.wait("tensor", P.tok("vS"))
            for oc in range(nout):
                buf = oc % 2
                tl2 = dma("scalar", strb(buf * 4096, buf * 4096 + nkc * 128),
                          wsrc[oc], pre=LG.get(("gl", buf)))
                P.wait("tensor", tl2)
                pb = PS[2 + oc % 2]
                bank_wait(2 + oc % 2)
                nmm = 0
                ntot = nkc * 2
                for kc in range(nkc):
                    wb_ = strb(buf * 4096 + kc * 128, buf * 4096 + (kc + 1) * 128)
                    for b_ in (xh_fn(kc), xl_fn(kc)):
                        P.op("tensor", lambda e, a=wb_, b=b_, p=pb, st2=(nmm == 0),
                             sp2=(nmm == ntot - 1):
                             e.matmul(p[:], a, b, start=st2, stop=sp2),
                             inc=("tS", 1) if nmm == ntot - 1 else None)
                        nmm += 1
                tmm = P.tok("tS")
                LG[("gl", buf)] = tmm
                P.wait("vector", tmm)
                if outh_fn is not None:
                    sq = slot(1, 0, TT)
                    P.op("vector", lambda e, o=sq, p=pb: e.tensor_scalar(o, p[:], 0.0, None,
                                                                         ALU.max))
                    P.op("vector", lambda e, o=outh_fn(oc), i=sq: e.tensor_copy(o, i))
                    P.op("vector", lambda e, o=sq, i=outh_fn(oc):
                         e.tensor_tensor(o, o, i, ALU.subtract))
                    tvl = P.op("vector", lambda e, o=outl_fn(oc), i=sq: e.tensor_copy(o, i),
                               inc=("vS", 1))
                    bank_done(2 + oc % 2, tvl)
                else:
                    out_ = slot(1, 0, TT)
                    P.op("vector", lambda e, o=out_, p=pb:
                         e.tensor_scalar(o, p[:], vg("csbb"), None, ALU.mult))
                    tvo2 = P.op("vector", lambda e, o=out_, x=slot(0, oc * TT, (oc + 1) * TT):
                                e.tensor_tensor(o, o, x, ALU.add), inc=("vS", 1))
                    bank_done(2 + oc % 2, tvo2)
                    P.wait("tensor", tvo2)
                    for tb in range(4):
                        bank_wait("PT")
                        tt2 = P.op("tensor", lambda e, tb=tb: e.transpose(
                            PT[:], out_[:, tb * 128:(tb + 1) * 128], IDNS[:]), inc=("tS", 1))
                        P.wait("vector", tt2)
                        P.wait("vector", P.tok("dG") if tb == 0 and oc == 0 else None)
                        tvt = P.op("vector", lambda e, tb=tb: e.tensor_copy(
                            slot(1, TT + tb * 128, TT + (tb + 1) * 128), PT[:]), inc=("vS", 1))
                        bank_done("PT", tvt)
                        P.wait("tensor", tvt)
                        dma("gpsimd", OUT[c0 + tb * 128:c0 + (tb + 1) * 128,
                                          oc * 128:(oc + 1) * 128],
                            slot(1, TT + tb * 128, TT + (tb + 1) * 128), pre=tvt)

        glayer(NCH, 32, D0B,
               lambda kc: slotb(2, kc * TT, (kc + 1) * TT),
               lambda kc: slotb(2, SLOT + kc * TT, SLOT + (kc + 1) * TT),
               lambda oc: WK[:, 3 * SLOT:5 * SLOT].bitcast(BF16)[:, oc * TT:(oc + 1) * TT],
               lambda oc: WK[:, 5 * SLOT:7 * SLOT].bitcast(BF16)[:, oc * TT:(oc + 1) * TT])
        glayer(32, 32, W1BB,
               lambda kc: WK[:, 3 * SLOT:5 * SLOT].bitcast(BF16)[:, kc * TT:(kc + 1) * TT],
               lambda kc: WK[:, 5 * SLOT:7 * SLOT].bitcast(BF16)[:, kc * TT:(kc + 1) * TT],
               lambda oc: WK[:, 7 * SLOT:9 * SLOT].bitcast(BF16)[:, oc * TT:(oc + 1) * TT],
               lambda oc: (WK[:, 9 * SLOT:10 * SLOT].bitcast(BF16)[:, oc * TT:(oc + 1) * TT]
                           if oc < 16 else
                           WK[:, 2 * SLOT:3 * SLOT].bitcast(BF16)[:, (oc - 16) * TT:(oc - 15) * TT]))
        glayer(32, 8, W2BB,
               lambda kc: WK[:, 7 * SLOT:9 * SLOT].bitcast(BF16)[:, kc * TT:(kc + 1) * TT],
               lambda kc: (WK[:, 9 * SLOT:10 * SLOT].bitcast(BF16)[:, kc * TT:(kc + 1) * TT]
                           if kc < 16 else
                           WK[:, 2 * SLOT:3 * SLOT].bitcast(BF16)[:, (kc - 16) * TT:(kc - 15) * TT]),
               None, None)
        P.barrier()

    for name, (dd, src) in DBG.items():
        dma("gpsimd", dd, src)

    P.wait("sync", P.tok("dG"))
    P.wait("sync", P.tok("dS"))
    P.wait("sync", P.tok("dA"))

    block = E(nc.Block())
    P.run(block)
    st.close()
    return nc


def kernel(**inputs):
    x = np.asarray(inputs["L0_attn_x"], np.float32)
    w_tm = np.ascontiguousarray(np.asarray(inputs["L0_attn_W_tm"], np.float32))
    td = np.asarray(inputs["L0_attn_time_decay"], np.float32)
    tf = np.asarray(inputs["L0_attn_time_first"], np.float32)
    g1 = np.asarray(inputs["L0_attn_g1"], np.float32)
    b1 = np.asarray(inputs["L0_attn_b1"], np.float32)
    tmk = np.asarray(inputs["L0_attn_tmk"], np.float32).reshape(-1)
    tmv = np.asarray(inputs["L0_attn_tmv"], np.float32).reshape(-1)
    tmr = np.asarray(inputs["L0_attn_tmr"], np.float32).reshape(-1)
    wcm = np.ascontiguousarray(np.asarray(inputs["L0_attn_W_cm"], np.float32))
    wcp = np.ascontiguousarray(np.asarray(inputs["L0_attn_W_cp"], np.float32))
    g2 = np.asarray(inputs["L0_attn_g2"], np.float32)
    b2 = np.asarray(inputs["L0_attn_b2"], np.float32)
    cmk = np.asarray(inputs["L0_attn_cmk"], np.float32).reshape(-1)
    cmr = np.asarray(inputs["L0_attn_cmr"], np.float32).reshape(-1)
    l0w = np.asarray(inputs["L0_ffn_weight"], np.float32)
    l0b = np.asarray(inputs["L0_ffn_bias"], np.float32)
    wq = np.asarray(inputs["L1_attn_Wq"], np.float32)
    bq = np.asarray(inputs["L1_attn_bq"], np.float32)
    wk = np.asarray(inputs["L1_attn_Wk"], np.float32)
    bk = np.asarray(inputs["L1_attn_bk"], np.float32)
    wv = np.asarray(inputs["L1_attn_Wv"], np.float32)
    bv = np.asarray(inputs["L1_attn_bv"], np.float32)
    k0 = np.asarray(inputs["L1_ffn_k0"]).astype(np.uint8)
    k1 = np.ascontiguousarray(np.asarray(inputs["L1_ffn_k1"]).astype(np.uint8))
    k2 = np.ascontiguousarray(np.asarray(inputs["L1_ffn_k2"]).astype(np.uint8))
    s0 = np.float32(np.asarray(inputs["L1_ffn_s0"]))
    s1 = np.float32(np.asarray(inputs["L1_ffn_s1"]))
    s2 = np.float32(np.asarray(inputs["L1_ffn_s2"]))

    iotag = np.broadcast_to(np.arange(L, dtype=np.float32), (128, L)).copy()
    idn = np.eye(128, dtype=np.float32)
    idnb = np.eye(128, dtype=ml_dtypes.bfloat16)
    eye_mask = (1.0 - np.eye(128)).astype(np.float32)
    ones_mask = np.ones((128, 128), np.float32)
    scl = np.array([[s0, s1, s2, 0.0]], np.float32)

    in_maps = []
    for c in range(8):
        b = c // 2
        half = c % 2
        perm = np.arange(C) if half == 0 else np.r_[512:1024, 0:512]
        xb = np.ascontiguousarray(x[b].T)
        d = {
            "x0f": np.ascontiguousarray(xb[perm]),
            "x0own": np.ascontiguousarray(xb[:, half * TO:(half + 1) * TO]),
            "wtmc": np.ascontiguousarray(np.concatenate(
                [w_tm[perm][:, half * HC:(half + 1) * HC],
                 w_tm[perm][:, C + half * HC:C + (half + 1) * HC],
                 w_tm[perm][:, 2 * C + half * HC:2 * C + (half + 1) * HC]], axis=1)),
            "vecw": np.ascontiguousarray(np.stack(
                [td[half * HC:(half + 1) * HC], tf[half * HC:(half + 1) * HC],
                 tmk[half * HC:(half + 1) * HC], tmv[half * HC:(half + 1) * HC],
                 tmr[half * HC:(half + 1) * HC]], axis=1)),
            "vecp": np.ascontiguousarray(np.stack([g1[perm], b1[perm]], axis=1)),
            "vecn": np.ascontiguousarray(np.stack([g2, b2, cmk, cmr, l0b, bq, bk, bv], axis=1)),
            "scl": scl,
            "wcm": wcm,
            "wcp": wcp,
            "l0wt": np.ascontiguousarray(l0w.T),
            "wqt": np.ascontiguousarray(wq.T),
            "wkt": np.ascontiguousarray(wk.T),
            "wvt": np.ascontiguousarray(wv.T),
            "k0a": np.ascontiguousarray(k0[:C]),
            "k0b": np.ascontiguousarray(k0[C:]),
            "k1u": k1,
            "k2u": k2,
            "iotag": iotag,
            "idn": idn,
            "idnb": idnb,
            "mske": eye_mask if half == 0 else ones_mask,
            "msko": ones_mask if half == 0 else eye_mask,
            "sel": np.full((128, 1), float(half), np.float32),
        }
        in_maps.append(d)

    nc = build(taps=tuple(TAPS))
    res = run_bass_kernel_spmd(nc, in_maps, core_ids=list(range(8)))
    out = np.empty((4, T, C), np.float32)
    for c in range(8):
        b = c // 2
        half = c % 2
        out[b, half * TO:(half + 1) * TO, :] = res.results[c]["out"]
    if TAPS:
        kernel.dbg = [{k: v for k, v in r.items() if k.startswith("dbg_")}
                      for r in res.results]
    return out
